# version 03
# baseline (speedup 1.0000x reference)
import numpy as np
import concourse.bass as bass
import concourse.mybir as mybir
from concourse.bass_utils import run_bass_kernel_spmd

F32 = mybir.dt.float32
BF16 = mybir.dt.bfloat16
AF = mybir.ActivationFunctionType
ALU = mybir.AluOpType
AX = mybir.AxisListType

N = 2048
D = 1024
DC = 8
NH = 16
DH = 64
HGn = 8
HID = 4096
HCn = 32
NB = 4
NEG = -1.0e30

ENGINES = ("sync", "tensor", "vector", "scalar", "gpsimd")


class Sem:
    def __init__(self, nc, name):
        self.h = nc.alloc_semaphore(name)
        self.c = 0

    def now(self):
        return self.c


class P:

    def __init__(self, nc):
        self.nc = nc
        self.q = {e: [] for e in ENGINES}
        self.sems = {}

    def sem(self, name):
        if name not in self.sems:
            self.sems[name] = Sem(self.nc, name)
        return self.sems[name]

    def alloc(self, name, shape, dtype):
        if not hasattr(self, "bind"):
            self.bind = {}
        if name in self.bind:
            return self.bind[name]
        h = self.nc.alloc_sbuf_tensor(name, shape, dtype)
        return h[tuple(slice(None) for _ in shape)]

    def barrier(self, name=""):
        bar = self.sem("bar")
        if not hasattr(self, "bar_t"):
            self.bar_t = {e: self.nc.alloc_sbuf_tensor(f"bar_{e}", [128, 1], F32)
                          for e in ("vector", "scalar", "gpsimd")}
        for e in ENGINES:
            if e in ("vector", "gpsimd"):
                self.emit(e, lambda eng, o=self.bar_t[e]: eng.memset(o[:], 0.0),
                          incs=[(bar, 1)])
            elif e == "scalar":
                self.emit(e, lambda eng, o=self.bar_t[e]: eng.memzero(o[:]),
                          incs=[(bar, 1)])
            else:
                self.emit(e, lambda eng: eng.nop(), incs=[(bar, 1)])
        tgt = bar.now()
        dws = [self.sems[k] for k in ("dw", "dx", "dq", "do", "dacc", "dcp",
                                      "dbg", "cst")
               if k in self.sems]
        for e in ENGINES:
            self.emit(e, lambda eng: eng.nop(),
                      waits=[(bar, tgt)] + [(s, s.now()) for s in dws])

    def emit(self, eng, fn, waits=(), incs=()):
        wl = [(s.h, int(v)) for (s, v) in waits]
        il = []
        for (s, n) in incs:
            s.c += n
            il.append((s.h, n))
        self.q[eng].append((fn, wl, il))

    def replay(self):
        def runner(lst):
            def run(eng):
                for (fn, wl, il) in lst:
                    for (h, v) in wl:
                        eng.wait_ge(h, v)
                    ins = fn(eng)
                    for (h, n) in il:
                        ins.then_inc(h, n)
            return run

        with self.nc.Block() as blk:
            blk.sync(runner(self.q["sync"]))
            blk.tensor(runner(self.q["tensor"]))
            blk.vector(runner(self.q["vector"]))
            blk.scalar(runner(self.q["scalar"]))
            blk.gpsimd(runner(self.q["gpsimd"]))

    def dma(self, queue, out, in_, waits=(), incs=(), **kw):
        self.emit(queue, lambda e, o=out, i=in_, k=kw: e.dma_start(o, i, **k),
                  waits=waits, incs=incs)

    def mm(self, out, lhsT, rhs, start, stop, waits=(), incs=()):
        self.emit("tensor",
                  lambda e, o=out, l=lhsT, r=rhs, s1=start, s2=stop: e.matmul(o, l, r, start=s1, stop=s2),
                  waits=waits, incs=incs)

    def tr(self, out, in_, ident, waits=(), incs=()):
        self.emit("tensor", lambda e, o=out, i=in_, d=ident: e.transpose(o, i, d),
                  waits=waits, incs=incs)

    def act(self, out, in_, func, bias=0.0, scale=1.0, accum_out=None, waits=(), incs=()):
        self.emit("scalar",
                  lambda e, o=out, i=in_, f=func, b=bias, s=scale, a=accum_out:
                  e.activation(o, i, f, bias=b, scale=s, accum_out=a),
                  waits=waits, incs=incs)

    def vop(self, fn, waits=(), incs=()):
        self.emit("vector", fn, waits=waits, incs=incs)

    def pop(self, fn, waits=(), incs=()):
        self.emit("gpsimd", fn, waits=waits, incs=incs)


def host_constants():
    c = {}
    c["ident"] = np.eye(128, dtype=np.float32)
    pos = np.arange(N, dtype=np.float32)
    ang = pos[:, None] / (10000.0 ** (np.arange(0, DH, 2, dtype=np.float32) / DH))
    rope = np.stack([np.cos(ang), np.sin(ang)], axis=-1).reshape(N, DH).astype(np.float32)
    a_fm = rope.T
    c["rope_a"] = np.concatenate([a_fm, a_fm], axis=0).astype(np.float32)
    c["rope_b"] = (1.0 - c["rope_a"]).astype(np.float32)
    prot = np.zeros((128, 128), dtype=np.float32)
    for i in range(64):
        prot[2 * i + 1, 2 * i] = -1.0
        prot[2 * i, 2 * i + 1] = 1.0
    c["prot"] = prot
    tri = np.where(np.arange(128)[None, :] <= np.arange(128)[:, None], 0.0, NEG)
    c["tri"] = tri.astype(np.float32)
    m = np.arange(576)[None, :] - np.arange(128)[:, None] - 96
    c["mband"] = ((m >= 0) & (m <= 64)).astype(np.float32)
    c["mleft"] = (m < 0).astype(np.float32)
    c["mright"] = (m > 64).astype(np.float32)
    c["ones_1x128"] = np.ones((1, 128), dtype=np.float32)
    c["ones_col"] = np.ones((128, 1), dtype=np.float32)
    sel = np.zeros((8, 8 * 128), dtype=np.float32)
    for e in range(8):
        sel[e, e * 128:(e + 1) * 128] = 1.0
    c["sel8"] = sel
    return c


def relayout_weights(w):
    r = {}
    pp = lambda v: np.ascontiguousarray(v.reshape(-1, 128).T)
    perm = []
    for hg in range(HGn):
        for s in range(3):
            for h2 in range(2):
                o0 = (2 * hg + h2) * 192 + s * 64
                perm.extend(range(o0, o0 + 64))
    r["l0_WqkvT"] = np.ascontiguousarray(w["L0_attn_Wqkv"].T[:, perm])
    r["l0_WprojT"] = np.ascontiguousarray(w["L0_attn_Wproj"].T)
    r["l0_bproj"] = pp(w["L0_attn_bproj"])
    bqkv = w["L0_attn_bqkv"]
    for si, nm in enumerate(("bq", "bk", "bv")):
        arr = np.zeros((128, 8), dtype=np.float32)
        for h in range(NH):
            arr[(h % 2) * 64:(h % 2) * 64 + 64, h // 2] = bqkv[h * 192 + si * 64: h * 192 + si * 64 + 64]
        r[f"l0_{nm}_pp"] = arr
    for k in ("w1", "w2", "h1", "h2", "w3"):
        r[f"l0_{k}T"] = np.ascontiguousarray(w[f"L0_ffn_{k}_w"].T)
        r[f"l0_{k}s"] = pp(w[f"L0_ffn_{k}_s"])
    for k in ("Wq", "Wk", "Wv", "Wo"):
        r[f"l1_{k}"] = np.ascontiguousarray(w[f"L1_attn_{k}"])
    for k in ("bq", "bk", "bv", "bo"):
        r[f"l1_{k}"] = pp(w[f"L1_attn_{k}"])
    rt = np.ascontiguousarray(w["L1_attn_rel_emb"][::-1, :].T)
    r["l1_relT"] = np.concatenate([rt, rt], axis=0)
    r["l1_ln1w"] = pp(w["L1_ffn_ln1_w"]); r["l1_ln1b"] = pp(w["L1_ffn_ln1_b"])
    r["l1_ln2w"] = pp(w["L1_ffn_ln2_w"]); r["l1_ln2b"] = pp(w["L1_ffn_ln2_b"])
    r["l1_wqkv"] = np.ascontiguousarray(w["L1_ffn_w_qkv"])
    r["l1_wproj"] = np.ascontiguousarray(w["L1_ffn_w_proj"])
    r["l1_gatew"] = np.ascontiguousarray(w["L1_ffn_gate_w"])
    r["l1_gateb"] = np.ascontiguousarray(w["L1_ffn_gate_b"].reshape(1, 8))
    r["l1_moe_w1"] = np.ascontiguousarray(w["L1_ffn_w1"])
    r["l1_moe_w2"] = np.ascontiguousarray(w["L1_ffn_w2"])
    return r


def alloc_common(nc, want_consts=True, want_rope=True, want_rel=True, want_io=True,
                 want_x=True, want_ln=True):
    t = {}
    if want_x:
        t["X"] = nc.alloc_sbuf_tensor("X", [128, DC * N], F32)
    t["XN"] = nc.alloc_sbuf_tensor("XN", [128, DC * N], BF16)
    if want_consts:
        t["ident_f"] = nc.alloc_sbuf_tensor("ident_f", [128, 128], F32)
        t["ident_b"] = nc.alloc_sbuf_tensor("ident_b", [128, 128], BF16)
        t["ones_1x128_b"] = nc.alloc_sbuf_tensor("ones_1x128_b", [1, 128], BF16)
        t["ones_1x128_f"] = nc.alloc_sbuf_tensor("ones_1x128_f", [1, 128], F32)
        t["ones_col_f"] = nc.alloc_sbuf_tensor("ones_col_f", [128, 1], F32)
        t["ones_col_b"] = nc.alloc_sbuf_tensor("ones_col_b", [128, 1], BF16)
        t["tri_f"] = nc.alloc_sbuf_tensor("tri_f", [128, 128], F32)
        t["prot_b"] = nc.alloc_sbuf_tensor("prot_b", [128, 128], BF16)
        if want_rope:
            t["rope_a"] = nc.alloc_sbuf_tensor("rope_a", [128, N], BF16)
            t["rope_b"] = nc.alloc_sbuf_tensor("rope_b", [128, N], BF16)
        for i in range(4):
            t[f"s512_{i}"] = nc.alloc_sbuf_tensor(f"s512_{i}", [128, 512], F32)
        t["rtmp0"] = t["s512_2"]
        t["rtmp1"] = t["s512_3"]
        if want_ln:
            for i, k in enumerate(("m", "tA", "tB", "a", "mr")):
                t[f"row_{k}"] = nc.alloc_sbuf_tensor(f"row_{k}", [1, 512], F32)
        if want_io:
            t["xio0"] = nc.alloc_sbuf_tensor("xio0", [128, D], F32)
            t["xio1"] = nc.alloc_sbuf_tensor("xio1", [128, D], F32)
        if want_rel:
            t["mband_b"] = nc.alloc_sbuf_tensor("mband_b", [128, 576], BF16)
            t["mleft_b"] = nc.alloc_sbuf_tensor("mleft_b", [128, 576], BF16)
            t["mright_b"] = nc.alloc_sbuf_tensor("mright_b", [128, 576], BF16)
    return t


def load_consts(p, t, dr):
    cst = p.sem("cst")
    p.dma("sync", t["ident_f"][:], dr["ident"], incs=[(cst, 16)])
    p.dma("gpsimd", t["ident_b"][:], dr["ident"], incs=[(cst, 16)])
    p.dma("gpsimd", t["ones_1x128_b"][:], dr["ones_1x128"], incs=[(cst, 16)])
    p.dma("sync", t["ones_1x128_f"][:], dr["ones_1x128"], incs=[(cst, 16)])
    p.dma("sync", t["ones_col_f"][:], dr["ones_col"], incs=[(cst, 16)])
    p.dma("gpsimd", t["ones_col_b"][:], dr["ones_col"], incs=[(cst, 16)])
    p.dma("sync", t["tri_f"][:], dr["tri"], incs=[(cst, 16)])
    p.dma("gpsimd", t["prot_b"][:], dr["prot"], incs=[(cst, 16)])
    if "rope_a" in t:
        p.dma("gpsimd", t["rope_a"][:], dr["rope_a"], incs=[(cst, 16)])
        p.dma("gpsimd", t["rope_b"][:], dr["rope_b"], incs=[(cst, 16)])
    if "mband_b" in t:
        p.dma("gpsimd", t["mband_b"][:], dr["mband"], incs=[(cst, 16)])
        p.dma("gpsimd", t["mleft_b"][:], dr["mleft"], incs=[(cst, 16)])
        p.dma("gpsimd", t["mright_b"][:], dr["mright"], incs=[(cst, 16)])
    t["cst_all"] = cst.now()


def ph_load_x(p, t, dr, psum):
    nc = p.nc
    cst, dx, tsem, vsem = p.sem("cst"), p.sem("dx"), p.sem("t"), p.sem("v")
    xtm = [t["xio0"], t["xio1"]]
    ps = psum["s"]
    for tt in range(16):
        buf = xtm[tt % 2]
        w = [(tsem, tsem.now())] if tt >= 2 else []
        p.dma("sync", buf[:], dr["x_in"][tt * 128:(tt + 1) * 128, :],
              waits=w, incs=[(dx, 16)])
        for half in range(2):
            pb = ps[:, half * 512:(half + 1) * 512]
            for ci in range(4):
                c = half * 4 + ci
                w = [(dx, dx.now()), (cst, t.get('cst_all', 16))]
                if tt >= 1 or half >= 1:
                    w.append((vsem, vsem.now()))
                p.tr(pb[:, ci * 128:(ci + 1) * 128], buf[:, c * 128:(c + 1) * 128],
                     t["ident_f"][:], waits=w,
                     incs=[(tsem, 1)] if ci == 3 else [])
            for ci in range(4):
                c = half * 4 + ci
                p.vop(lambda e, dst=t["X"][:, c * N + tt * 128: c * N + tt * 128 + 128],
                      src=pb[:, ci * 128:(ci + 1) * 128]: e.tensor_copy(dst, src),
                      waits=[(tsem, tsem.now())],
                      incs=[(vsem, 1)] if ci == 3 else [])


def ph_store_out(p, t, dr, psum, src_key="X"):
    nc = p.nc
    tsem, vsem, do = p.sem("t"), p.sem("v"), p.sem("do")
    xout = [t["xio0"], t["xio1"]]
    ps = psum["s"]
    store_counts = {}
    for tt in range(16):
        buf = xout[tt % 2]
        for half in range(2):
            pb = ps[:, half * 512:(half + 1) * 512]
            for ci in range(4):
                c = half * 4 + ci
                w = [(vsem, vsem.now())]
                p.tr(pb[:, ci * 128:(ci + 1) * 128],
                     t[src_key][:, c * N + tt * 128: c * N + tt * 128 + 128],
                     t["ident_f"][:], waits=w,
                     incs=[(tsem, 1)] if ci == 3 else [])
            for ci in range(4):
                c = half * 4 + ci
                w = [(tsem, tsem.now())]
                if tt >= 2 and half == 0 and ci == 0:
                    w.append((do, store_counts[tt - 2]))
                p.vop(lambda e, dst=buf[:, c * 128:(c + 1) * 128],
                      src=pb[:, ci * 128:(ci + 1) * 128]: e.tensor_copy(dst, src),
                      waits=w, incs=[(vsem, 1)] if ci == 3 else [])
        p.dma("sync", dr["out"][tt * 128:(tt + 1) * 128, :], buf[:],
              waits=[(vsem, vsem.now())], incs=[(do, 16)])
        store_counts[tt] = do.now()
    p.emit("sync", lambda e: e.nop(), waits=[(do, do.now())])


def ph_ln(p, t, psum, src_key, dst_key, wb=None, entry_waits=(), src_bf16=False):
    nc = p.nc
    tsem, vsem, ssem = p.sem("t"), p.sem("v"), p.sem("s")
    sq = [t["s512_0"], t["s512_1"]]
    rows = {k: t[f"row_{k}"] for k in ("m", "tA", "tB", "a", "mr")}
    rows["t1"] = rows["tA"]; rows["v1"] = rows["tB"]
    rows["v2"] = rows["tB"]; rows["sd"] = rows["tA"]
    tmp = [t["s512_2"], t["s512_3"]]
    ps_a, ps_b, ps_c, ps_d = psum["a"], psum["b"], psum["c"], psum["d_ln"]
    src, dst = t[src_key], t[dst_key]
    ones = t["ones_col_b"] if src_bf16 else t["ones_col_f"]
    cst = p.sem("cst")

    prev_apply_done = None
    sumsq_mm_done = {}
    for n in range(NB):
        sl = lambda c, n=n: slice(c * N + n * 512, c * N + n * 512 + 512)
        sq_counts = {}

        def emit_square(c, n=n, sq_counts=sq_counts):
            g = n * DC + c
            w = list(entry_waits) + [(cst, t.get('cst_all', 16))]
            if g - 2 in sumsq_mm_done:
                w.append((tsem, sumsq_mm_done[g - 2]))
            p.act(sq[c % 2][:], src[:, sl(c)], AF.Square, waits=w, incs=[(ssem, 1)])
            sq_counts[c] = ssem.now()

        emit_square(0)
        emit_square(1)
        for c in range(DC):
            w = list(entry_waits) + [(cst, t.get('cst_all', 16))]
            if c == 0 and prev_apply_done is not None:
                w.append((vsem, prev_apply_done))
            p.mm(ps_a[0:1, 0:512], ones[:], src[:, sl(c)], c == 0, c == DC - 1,
                 waits=w, incs=[(tsem, 1)] if c == DC - 1 else [])
        sum_done = tsem.now()
        for c in range(DC):
            w = [(ssem, sq_counts[c])]
            if c == 0 and prev_apply_done is not None:
                w.append((vsem, prev_apply_done))
            p.mm(ps_b[0:1, 0:512], t["ones_col_f"][:], sq[c % 2][:], c == 0, c == DC - 1,
                 waits=w, incs=[(tsem, 1)])
            sumsq_mm_done[n * DC + c] = tsem.now()
            if c + 2 < DC:
                emit_square(c + 2)
        sumsq_done = tsem.now()
        p.vop(lambda e, o=rows["m"], i=ps_a[0:1, 0:512]:
              e.tensor_scalar(o[:], i, 1.0 / D, None, ALU.mult),
              waits=[(tsem, sum_done)], incs=[(vsem, 1)])
        p.vop(lambda e, o=rows["t1"], i=rows["m"]: e.tensor_mul(o[:], i[:], i[:]))
        p.vop(lambda e, o=rows["v1"], i=ps_b[0:1, 0:512]:
              e.tensor_scalar(o[:], i, 1.0 / D, 1e-5, ALU.mult, ALU.add),
              waits=[(tsem, sumsq_done)], incs=[(vsem, 1)])
        p.vop(lambda e, o=rows["v2"], a=rows["v1"], b=rows["t1"]:
              e.tensor_sub(o[:], a[:], b[:]), incs=[(vsem, 1)])
        vv = vsem.now()
        p.act(rows["sd"][:], rows["v2"][:], AF.Sqrt, waits=[(vsem, vv)], incs=[(ssem, 1)])
        sv = ssem.now()
        p.vop(lambda e, o=rows["a"], i=rows["sd"]: e.reciprocal(o[:], i[:]),
              waits=[(ssem, sv)], incs=[(vsem, 1)])
        p.vop(lambda e, o=rows["mr"], a=rows["m"], b=rows["a"]:
              e.tensor_mul(o[:], a[:], b[:]), incs=[(vsem, 1)])
        rows_done = vsem.now()
        w = [(vsem, rows_done)]
        if prev_apply_done is not None:
            w.append((vsem, prev_apply_done))
        p.mm(ps_c[:, 0:512], t["ones_1x128_f"][:], rows["a"][:], True, True,
             waits=w, incs=[(tsem, 1)])
        p.mm(ps_d[:, 0:512], t["ones_1x128_f"][:], rows["mr"][:], True, True,
             incs=[(tsem, 1)])
        bc_done = tsem.now()
        for c in range(DC):
            w = [(tsem, bc_done)]
            p.vop(lambda e, o=tmp[c % 2], i=src[:, sl(c)], b=ps_c[:, 0:512]:
                  e.tensor_mul(o[:], i, b), waits=w)
            if wb is None:
                p.vop(lambda e, o=dst[:, sl(c)], a=tmp[c % 2], b=ps_d[:, 0:512]:
                      e.tensor_sub(o, a[:], b),
                      incs=[(vsem, 1)] if c == DC - 1 else [])
            else:
                w_pp, b_pp = wb
                p.vop(lambda e, o=tmp[c % 2], a=tmp[c % 2], b=ps_d[:, 0:512]:
                      e.tensor_sub(o[:], a[:], b))
                p.vop(lambda e, o=dst[:, sl(c)], i=tmp[c % 2],
                      s1=w_pp[:, c:c + 1], s2=b_pp[:, c:c + 1]:
                      e.tensor_scalar(o, i[:], s1, s2, ALU.mult, ALU.add),
                      waits=list(entry_waits),
                      incs=[(vsem, 1)] if c == DC - 1 else [])
        prev_apply_done = vsem.now()
    return prev_apply_done


def alloc_attn(p, rel=False):
    a = {}
    for k in ("Q", "K", "V", "VT", "P", "PT"):
        a[k] = p.alloc(f"at_{k}", [128, N], BF16)
    a["CTX"] = p.alloc("at_CTX", [128, DC * N], BF16)
    a["negmax"] = p.alloc("at_negmax", [128, 1], F32)
    a["sums"] = p.alloc("at_sums", [128, 4], F32)
    a["rinv"] = p.alloc("at_rinv", [128, 1], F32)
    if rel:
        a["qrel"] = p.alloc("at_qrel", [128, 16 * 65], F32)
        a["win"] = p.alloc("at_win", [128, 8 * 384], BF16)
        a["bw"] = p.alloc("at_bw", [128, 384], BF16)
        a["bw2"] = p.alloc("at_bw2", [128, 384], BF16)
        a["sw"] = p.alloc("at_sw", [128, 384], F32)
        a["biasL"] = p.alloc("at_biasL", [128, 1], F32)
        a["biasR"] = p.alloc("at_biasR", [128, 1], F32)
        a["mrow"] = p.alloc("at_mrow", [128, 3], F32)
    return a


def _rel_zone(qt):
    kw = max(0, (qt - 1) * 128)
    zw = min(2048, (qt + 2) * 128) - kw
    q0 = qt * 128
    delta = q0 - 32 - kw
    return kw, zw, delta


REL_BUF = 50100


def ph_attn(p, t, a, dr, psum, cfg):
    nc = p.nc
    name = cfg["name"]
    tsem, vsem, ssem = p.sem("t"), p.sem("v"), p.sem("s")
    dw, dq, gsem = p.sem("dw"), p.sem("dq"), p.sem("g")
    causal, rel, rope = cfg["causal"], cfg["rel"], cfg["rope"]
    XNs = t[cfg["xn_key"]]
    Q, K, V, VT, P_, PT, CTX = (a["Q"], a["K"], a["V"], a["VT"], a["P"], a["PT"], a["CTX"])
    ps_s, ps_a, ps_b, ps_t, ps_d = psum["s"], psum["a"], psum["b"], psum["t"], psum["d"]
    entry = list(cfg.get("entry_waits", ()))
    cst = p.sem("cst")

    wpb = [p.alloc(f"{name}_wpb{i}", [128, DC * 128], BF16) for i in range(2)]

    if rel:
        relT = p.alloc(f"{name}_relT", [128, 65], BF16)
        p.dma("gpsimd", relT[:], dr["l1_relT"], incs=[(dw, 16)])
        relT_ready = dw.now()
        scratch = nc.dram_tensor(f"{name}_scr", [16 * REL_BUF], F32, kind="Internal").ap()
        scr_read = {}
        win_used = {}

    pp_tiles = [ps_a, ps_b]
    evac_done_for_pp = {}
    prev_scores_exp = None
    prev_mask_v = None
    pt_evac_count = None
    ctx_evac_count = None
    qkv_mm_done = {}

    wq_depth = cfg.get("wq_depth", 2)
    for hg in range(HGn):
        war = []
        if hg >= wq_depth:
            war.append((tsem, qkv_mm_done[hg - wq_depth]))
        lhsT_fn = cfg["wq_dma"](p, hg, hg % 2, war)
        w_ready = dw.now()

        for oc3 in range(cfg["nchunks"]):
            for nb in range(NB):
                ppid = (oc3 * NB + nb) % 2
                pp = pp_tiles[ppid]
                for kc in range(DC):
                    w = []
                    if kc == 0:
                        w = entry + [(dw, w_ready)]
                        if ppid in evac_done_for_pp:
                            w.append((ssem, evac_done_for_pp[ppid]))
                    p.mm(pp[:, 0:512], lhsT_fn(oc3, kc),
                         XNs[:, kc * N + nb * 512: kc * N + nb * 512 + 512],
                         kc == 0, kc == DC - 1,
                         waits=w, incs=[(tsem, 1)] if kc == DC - 1 else [])
                mm_done = tsem.now()
                qkv_mm_done[hg] = mm_done
                for (r0, nr, dk, dr0, bias, scale) in cfg["evac_spec"](hg, oc3):
                    dst = a[dk][dr0:dr0 + nr, nb * 512:nb * 512 + 512]
                    b = bias if bias is not None else 0.0
                    p.act(dst, pp[r0:r0 + nr, 0:512], AF.Identity, bias=b, scale=scale,
                          waits=[(tsem, mm_done)] + cfg.get("bias_waits", []),
                          incs=[(ssem, 1)])
                evac_done_for_pp[ppid] = ssem.now()
        qkv_evac_done = ssem.now()

        if rope:
            rope_done_v = None
            rope_pp_free = {}
            for wkey in ("Q", "K"):
                wt = a[wkey]
                for nb in range(NB):
                    ppid = nb % 2
                    pp = pp_tiles[ppid]
                    sl = slice(nb * 512, nb * 512 + 512)
                    w = [(ssem, qkv_evac_done), (cst, t.get('cst_all', 16))]
                    if rope_done_v is not None:
                        w.append((vsem, rope_done_v))
                    if ppid in rope_pp_free:
                        w.append((vsem, rope_pp_free[ppid]))
                    elif ppid in evac_done_for_pp:
                        w.append((ssem, evac_done_for_pp[ppid]))
                    p.mm(pp[:, 0:512], t["prot_b"][:], wt[:, sl], True, True,
                         waits=w, incs=[(tsem, 1)])
                    rot_done = tsem.now()
                    p.vop(lambda e, o=t["rtmp0"], i=wt[:, sl], b=t["rope_a"][:, sl]:
                          e.tensor_mul(o[:], i, b), waits=[(ssem, qkv_evac_done)])
                    p.vop(lambda e, o=t["rtmp1"], i=pp[:, 0:512], b=t["rope_b"][:, sl]:
                          e.tensor_mul(o[:], i, b), waits=[(tsem, rot_done)],
                          incs=[(vsem, 1)])
                    rope_pp_free[ppid] = vsem.now()
                    p.vop(lambda e, o=wt[:, sl], x=t["rtmp0"], y=t["rtmp1"]:
                          e.tensor_add(o, x[:], y[:]), incs=[(vsem, 1)])
                    rope_done_v = vsem.now()
            evac_done_for_pp = {}
            qk_ready = (vsem, rope_done_v)
        else:
            qk_ready = (ssem, qkv_evac_done)

        vt_evac = None
        for kb in range(16):
            w = [(ssem, qkv_evac_done), (cst, t.get('cst_all', 16))]
            if kb % 8 == 0:
                if vt_evac is not None:
                    w.append((vsem, vt_evac))
                elif pt_evac_count is not None:
                    w.append((vsem, pt_evac_count))
            p.tr(ps_t[:, (kb % 8) * 128:(kb % 8) * 128 + 128],
                 V[:, kb * 128:kb * 128 + 128],
                 t["ident_b"][:], waits=w, incs=[(tsem, 1)] if kb % 8 == 7 else [])
            if kb % 8 == 7:
                p.vop(lambda e, o=VT[:, (kb - 7) * 128:(kb + 1) * 128], i=ps_t[:]:
                      e.tensor_copy(o, i[:]), waits=[(tsem, tsem.now())],
                      incs=[(vsem, 1)])
                vt_evac = vsem.now()
        vt_ready = vt_evac
        pt_evac_count = vt_evac

        for h2 in range(2):
            if rel:
                rel_read = {}
                scr_write = {}
                rviews = {}
                for qt in range(16):
                    kw, zw, delta = _rel_zone(qt)
                    ppid = qt % 2
                    pp = pp_tiles[ppid]
                    w = [(dw, relT_ready), (ssem, qkv_evac_done), qk_ready]
                    if ppid in evac_done_for_pp:
                        w.append((ssem, evac_done_for_pp[ppid]))
                    p.mm(pp[:, 0:65],
                         Q[h2 * 64:h2 * 64 + 64, qt * 128:qt * 128 + 128],
                         relT[h2 * 64:h2 * 64 + 64, :], True, True,
                         waits=w, incs=[(tsem, 1)])
                    p.act(a["qrel"][:, qt * 65:qt * 65 + 65], pp[:, 0:65], AF.Copy,
                          waits=[(tsem, tsem.now())], incs=[(ssem, 1)])
                    evac_done_for_pp[ppid] = ssem.now()
                    qrel_ready = ssem.now()
                    base = qt * REL_BUF
                    H = 64
                    wv = scratch[base + H + delta: base + H + delta + 128 * 386] \
                        .rearrange("(p m) -> p m", m=386)[:, 0:65]
                    rviews[qt] = scratch[base + H: base + H + 128 * 385] \
                        .rearrange("(p m) -> p m", m=385)[:, 0:zw]
                    w = [(ssem, qrel_ready)]
                    if qt in scr_read:
                        w.append((dq, scr_read[qt]))
                    p.dma("sync", wv, a["qrel"][:, qt * 65:qt * 65 + 65],
                          waits=w, incs=[(dq, 16)])
                    scr_write[qt] = dq.now()

                next_read = [0]

                def emit_win_read(stop_qt):
                    while next_read[0] <= min(stop_qt, 15):
                        q_ = next_read[0]
                        _, zw_, _ = _rel_zone(q_)
                        w = [(dq, scr_write[q_])]
                        if q_ % 8 in win_used:
                            w.append((gsem, win_used[q_ % 8]))
                        p.dma("gpsimd",
                              a["win"][:, (q_ % 8) * 384: (q_ % 8) * 384 + zw_],
                              rviews[q_], waits=w, incs=[(dq, 16)])
                        scr_read[q_] = dq.now()
                        rel_read[q_] = dq.now()
                        next_read[0] += 1

            for qt in range(16):
                if rel:
                    emit_win_read(qt + 3)
                kmaxb = (qt + 1) if causal else 16
                kmax = kmaxb * 128
                q0 = qt * 128
                ngroups = (kmaxb + 3) // 4
                for g in range(ngroups):
                    ng = min(512, kmax - g * 512)
                    w = []
                    if g == 0:
                        w = [qk_ready]
                        if prev_scores_exp is not None:
                            w.append((ssem, prev_scores_exp))
                        if prev_mask_v is not None:
                            w.append((vsem, prev_mask_v))
                    p.mm(ps_s[:, g * 512:g * 512 + ng],
                         Q[h2 * 64:h2 * 64 + 64, q0:q0 + 128],
                         K[h2 * 64:h2 * 64 + 64, g * 512:g * 512 + ng],
                         True, True, waits=w,
                         incs=[(tsem, 1)] if g == ngroups - 1 else [])
                scores_done = tsem.now()

                if causal:
                    p.vop(lambda e, o=ps_s[:, q0:q0 + 128], m=t["tri_f"]:
                          e.tensor_add(o, o, m[:]),
                          waits=[(tsem, scores_done), (cst, t.get('cst_all', 16))], incs=[(vsem, 1)])
                    prev_mask_v = vsem.now()
                    p.vop(lambda e, o=a["negmax"], i=ps_s[:, 0:kmax]:
                          e.reduce_max(o[:], i, axis=AX.X, negate=True),
                          waits=[(vsem, prev_mask_v)], incs=[(vsem, 1)])
                    mx = vsem.now()
                    p.act(P_[:, 0:kmax], ps_s[:, 0:kmax], AF.Exp,
                          bias=a["negmax"][:, 0:1], scale=1.0,
                          accum_out=a["sums"][:, 0:1],
                          waits=[(vsem, mx)], incs=[(ssem, 1)])
                    exp_done = ssem.now()
                    p.vop(lambda e, o=a["rinv"], i=a["sums"][:, 0:1]:
                          e.reciprocal(o[:], i), waits=[(ssem, exp_done)],
                          incs=[(vsem, 1)])
                elif not rel:
                    p.vop(lambda e, o=a["negmax"], i=ps_s[:]:
                          e.reduce_max(o[:], i[:], axis=AX.X, negate=True),
                          waits=[(tsem, scores_done)], incs=[(vsem, 1)])
                    mx = vsem.now()
                    p.act(P_[:], ps_s[:], AF.Exp, bias=a["negmax"][:, 0:1], scale=1.0,
                          accum_out=a["sums"][:, 0:1],
                          waits=[(vsem, mx)], incs=[(ssem, 1)])
                    exp_done = ssem.now()
                    p.vop(lambda e, o=a["rinv"], i=a["sums"][:, 0:1]:
                          e.reciprocal(o[:], i), waits=[(ssem, exp_done)],
                          incs=[(vsem, 1)])
                else:
                    exp_done, wu = _rel_softmax(p, t, a, psum, qt, rel_read[qt],
                                                scores_done)
                    win_used[qt % 8] = wu
                prev_scores_exp = exp_done
                rv = vsem.now()
                p.vop(lambda e, o=P_[:, 0:kmax], s=a["rinv"][:, 0:1]:
                      e.tensor_scalar(o, o, s, None, ALU.mult),
                      waits=[(vsem, rv)], incs=[(vsem, 1)])
                pnorm = vsem.now()

                for kb in range(kmaxb):
                    w = [(vsem, pnorm)]
                    if kb % 8 == 0 and pt_evac_count is not None:
                        w.append((vsem, pt_evac_count))
                    last_in_grp = (kb % 8 == 7) or (kb == kmaxb - 1)
                    p.tr(ps_t[:, (kb % 8) * 128:(kb % 8) * 128 + 128],
                         P_[:, kb * 128:kb * 128 + 128], t["ident_b"][:],
                         waits=w, incs=[(tsem, 1)] if last_in_grp else [])
                    if last_in_grp:
                        g8 = kb // 8
                        wdt = min(8, kmaxb - g8 * 8) * 128
                        p.vop(lambda e, o=PT[:, g8 * 1024:g8 * 1024 + wdt],
                              i=ps_t[:, 0:wdt]: e.tensor_copy(o, i),
                              waits=[(tsem, tsem.now())], incs=[(vsem, 1)])
                        pt_evac_count = vsem.now()
                pt_done = pt_evac_count

                for kb in range(kmaxb):
                    w = []
                    if kb == 0:
                        w = [(vsem, pt_done), (vsem, vt_ready)]
                        if ctx_evac_count is not None and qt % 4 == 0:
                            w.append((ssem, ctx_evac_count))
                    p.mm(ps_d[h2 * 64:h2 * 64 + 64,
                              (qt % 4) * 128:(qt % 4) * 128 + 128],
                         VT[:, kb * 128 + h2 * 64: kb * 128 + h2 * 64 + 64],
                         PT[:, kb * 128:kb * 128 + 128],
                         kb == 0, kb == kmaxb - 1,
                         waits=w, incs=[(tsem, 1)] if kb == kmaxb - 1 else [])
                if qt % 4 == 3:
                    p.act(CTX[h2 * 64:h2 * 64 + 64,
                              hg * N + (qt - 3) * 128: hg * N + (qt - 3) * 128 + 512],
                          ps_d[h2 * 64:h2 * 64 + 64, 0:512], AF.Copy,
                          waits=[(tsem, tsem.now())], incs=[(ssem, 1)])
                    ctx_evac_count = ssem.now()

    ctx_all_ready = ssem.now()
    evac_pp = {}
    pb = cfg.get("proj_bias")
    wp_mm_done = {}
    _pj_state = {}
    for oc in range(DC):
        war = [(tsem, wp_mm_done[oc - 2])] if oc >= 2 else []
        p.dma("gpsimd", wpb[oc % 2].rearrange("p (kc o) -> p kc o", o=128),
              cfg["proj_w"][:, oc * 128:(oc + 1) * 128].rearrange("(kc p) o -> p kc o", p=128),
              waits=war, incs=[(dw, 16)])
        wp_ready = dw.now()
        for nb in range(NB):
            ppid = (oc * NB + nb) % 2
            pp = pp_tiles[ppid]
            for kc in range(DC):
                w = []
                if kc == 0:
                    w = [(ssem, ctx_all_ready), (dw, wp_ready)]
                    if ppid in evac_pp:
                        w.append((vsem, evac_pp[ppid]))
                    elif ppid in evac_done_for_pp:
                        w.append((ssem, evac_done_for_pp[ppid]))
                p.mm(pp[:, 0:512],
                     wpb[oc % 2][:, kc * 128: kc * 128 + 128],
                     CTX[:, kc * N + nb * 512: kc * N + nb * 512 + 512],
                     kc == 0, kc == DC - 1, waits=w,
                     incs=[(tsem, 1)] if kc == DC - 1 else [])
            mm_done = tsem.now()
            wp_mm_done[oc] = mm_done
            xsl = slice(oc * N + nb * 512, oc * N + nb * 512 + 512)
            od = cfg.get("out_dram")
            if od is not None:
                if cfg.get("out_extra") == "x1bf":
                    p.vop(lambda e, o=t["x1bf"][:, xsl], i=pp[:, 0:512],
                          x=t[cfg["x1_src"]][:, xsl]: e.tensor_add(o, i, x),
                          waits=[(tsem, mm_done)], incs=[(vsem, 1)])
                dacc = p.sem("dacc")
                yi = (oc * NB + nb) % 2
                ytile = t[f"s512_{yi}"]
                aw = [(tsem, mm_done)]
                if ("proj_dma", yi) in _pj_state:
                    aw.append((dacc, _pj_state[("proj_dma", yi)]))
                p.vop(lambda e, o=ytile, i=pp[:, 0:512]: e.tensor_copy(o[:], i),
                      waits=aw, incs=[(vsem, 1)])
                cw = cfg.get("copy_wait")
                dmaw = [cw] if cw is not None else []
                p.dma("gpsimd", od[oc * 128:(oc + 1) * 128, nb * 512:nb * 512 + 512],
                      ytile[:], waits=[(vsem, vsem.now())] + dmaw, incs=[(dacc, 16)],
                      accum_op=ALU.add)
                _pj_state[("proj_dma", yi)] = dacc.now()
                evac_pp[ppid] = vsem.now()
                continue
            if pb is not None:
                p.vop(lambda e, o=t["X"][:, xsl], i=pp[:, 0:512],
                      s=pb[:, oc:oc + 1]:
                      e.scalar_tensor_tensor(o, i, s, o, ALU.add, ALU.add),
                      waits=[(tsem, mm_done)] + cfg.get("bias_waits", []),
                      incs=[(vsem, 1)])
            else:
                p.vop(lambda e, o=t["X"][:, xsl], i=pp[:, 0:512]:
                      e.tensor_add(o, o, i), waits=[(tsem, mm_done)],
                      incs=[(vsem, 1)])
            if cfg.get("out_extra") == "x1bf":
                p.vop(lambda e, o=t["x1bf"][:, xsl], i=pp[:, 0:512],
                      x=t[cfg["x1_src"]][:, xsl]: e.tensor_add(o, i, x),
                      incs=[(vsem, 1)])
            evac_pp[ppid] = vsem.now()
    return vsem.now()


def _rel_softmax(p, t, a, psum, qt, win_ready, scores_done):
    tsem, vsem, ssem, gsem, dq = (p.sem("t"), p.sem("v"), p.sem("s"),
                                  p.sem("g"), p.sem("dq"))
    cst = p.sem("cst")
    kw, zw, delta = _rel_zone(qt)
    ps_s = psum["s"]
    winap = a["win"][:, (qt % 8) * 384: (qt % 8) * 384 + zw]
    qL = a["qrel"][:, qt * 65 + 0: qt * 65 + 1]
    qR = a["qrel"][:, qt * 65 + 64: qt * 65 + 65]
    moff = 96 - delta
    MB = t["mband_b"][:, moff:moff + zw]
    ML = t["mleft_b"][:, moff:moff + zw]
    MR = t["mright_b"][:, moff:moff + zw]
    qready = ssem.now()
    p.pop(lambda e, o=a["bw"][:, 0:zw], i=winap, m=MB:
          e.tensor_mul(o, i, m),
          waits=[(cst, t.get('cst_all', 16)), (dq, win_ready)], incs=[(gsem, 1)])
    p.pop(lambda e, o=a["bw2"][:, 0:zw], i=ML, s=qL:
          e.tensor_scalar(o, i, s, None, ALU.mult),
          waits=[(ssem, qready)], incs=[(gsem, 1)])
    p.pop(lambda e, o=a["bw"][:, 0:zw], x=a["bw"][:, 0:zw], y=a["bw2"][:, 0:zw]:
          e.tensor_add(o, x, y), incs=[(gsem, 1)])
    p.pop(lambda e, o=a["bw2"][:, 0:zw], i=MR, s=qR:
          e.tensor_scalar(o, i, s, None, ALU.mult), incs=[(gsem, 1)])
    p.pop(lambda e, o=a["bw"][:, 0:zw], x=a["bw"][:, 0:zw], y=a["bw2"][:, 0:zw]:
          e.tensor_add(o, x, y), incs=[(gsem, 1)])
    bw_done = gsem.now()
    p.vop(lambda e, o=a["sw"][:, 0:zw], i=ps_s[:, kw:kw + zw], b=a["bw"][:, 0:zw]:
          e.tensor_add(o, i, b),
          waits=[(gsem, bw_done), (tsem, scores_done)], incs=[(vsem, 1)])
    sw_done = vsem.now()
    p.vop(lambda e, o=a["mrow"][:, 1:2], i=a["sw"][:, 0:zw]:
          e.reduce_max(o, i, axis=AX.X), waits=[(vsem, sw_done)], incs=[(vsem, 1)])
    if kw > 0:
        p.vop(lambda e, o=a["mrow"][:, 0:1], i=ps_s[:, 0:kw]:
              e.reduce_max(o, i, axis=AX.X), waits=[(tsem, scores_done)],
              incs=[(vsem, 1)])
        p.vop(lambda e, o=a["mrow"][:, 0:1], i=a["mrow"][:, 0:1], s=qL:
              e.tensor_scalar(o, i, s, None, ALU.add),
              waits=[(vsem, vsem.now()), (ssem, qready)], incs=[(vsem, 1)])
    if kw + zw < 2048:
        p.vop(lambda e, o=a["mrow"][:, 2:3], i=ps_s[:, kw + zw:2048]:
              e.reduce_max(o, i, axis=AX.X), waits=[(tsem, scores_done)],
              incs=[(vsem, 1)])
        p.vop(lambda e, o=a["mrow"][:, 2:3], i=a["mrow"][:, 2:3], s=qR:
              e.tensor_scalar(o, i, s, None, ALU.add),
              waits=[(vsem, vsem.now()), (ssem, qready)], incs=[(vsem, 1)])
    lo = 0 if kw > 0 else 1
    hi = 3 if kw + zw < 2048 else 2
    p.vop(lambda e, o=a["negmax"], i=a["mrow"][:, lo:hi]:
          e.reduce_max(o[:], i, axis=AX.X, negate=True),
          waits=[(vsem, vsem.now())], incs=[(vsem, 1)])
    nm = vsem.now()
    p.vop(lambda e, o=a["biasL"], i=a["negmax"], s=qL:
          e.tensor_scalar(o[:], i[:], s, None, ALU.add),
          waits=[(vsem, nm)], incs=[(vsem, 1)])
    p.vop(lambda e, o=a["biasR"], i=a["negmax"], s=qR:
          e.tensor_scalar(o[:], i[:], s, None, ALU.add), incs=[(vsem, 1)])
    bias_done = vsem.now()
    nsum = 0
    if kw > 0:
        p.act(a["P"][:, 0:kw], ps_s[:, 0:kw], AF.Exp, bias=a["biasL"][:, 0:1],
              scale=1.0, accum_out=a["sums"][:, 0:1],
              waits=[(vsem, bias_done)], incs=[(ssem, 1)])
        nsum += 1
    p.act(a["P"][:, kw:kw + zw], a["sw"][:, 0:zw], AF.Exp, bias=a["negmax"][:, 0:1],
          scale=1.0, accum_out=a["sums"][:, nsum:nsum + 1],
          waits=[(vsem, bias_done)], incs=[(ssem, 1)])
    nsum += 1
    if kw + zw < 2048:
        p.act(a["P"][:, kw + zw:2048], ps_s[:, kw + zw:2048], AF.Exp,
              bias=a["biasR"][:, 0:1], scale=1.0,
              accum_out=a["sums"][:, nsum:nsum + 1],
              waits=[(vsem, bias_done)], incs=[(ssem, 1)])
        nsum += 1
    exp_done = ssem.now()
    p.vop(lambda e, o=a["rinv"], i=a["sums"][:, 0:nsum]:
          e.reduce_sum(o[:], i, axis=AX.X), waits=[(ssem, exp_done)],
          incs=[(vsem, 1)])
    p.vop(lambda e, o=a["rinv"], i=a["rinv"]: e.reciprocal(o[:], i[:]),
          waits=[(vsem, vsem.now())], incs=[(vsem, 1)])
    return exp_done, bw_done


def ph_l0_ffn(p, t, dr, psum, xdram, entry_waits=()):
    nc = p.nc
    tsem, vsem, ssem = p.sem("t"), p.sem("v"), p.sem("s")
    dw, dacc = p.sem("dw"), p.sem("dacc")
    ps_a, ps_b = psum["a"], psum["b"]
    XNs = t["XN"]
    entry = list(entry_waits)

    a0 = p.alloc("ffn_a0", [128, HCn * 1024], BF16)
    a1 = p.alloc("ffn_a1", [128, HCn * 1024], BF16)
    w12 = [p.alloc(f"ffn_w12_{i}", [128, DC * 128], BF16) for i in range(4)]
    wh = [p.alloc(f"ffn_wh{i}", [128, HCn * 128], BF16) for i in range(2)]
    sil = p.alloc("ffn_sil", [128, 512], BF16)
    acc = [p.alloc(f"ffn_acc{i}", [128, 512], F32) for i in range(2)]
    scales = {}
    for k, w_ in (("w1", 32), ("w2", 32), ("h1", 32), ("h2", 32), ("w3", 8)):
        scales[k] = p.alloc(f"ffn_s_{k}", [128, w_], F32)
        p.dma("sync", scales[k], dr[f"l0_{k}s"], incs=[(dw, 16)])
    sc_ready = dw.now()
    acc_dma = {}
    wh_last_reader = 0

    for pas in range(2):
        tok0 = pas * 1024
        w12_mm_done = {}
        psa_free = None
        psb_free = None
        for oc in range(HCn):
            bi = oc % 2
            if oc >= 2:
                war = [(tsem, w12_mm_done[oc - 2])]
            elif pas > 0:
                war = [(tsem, w12_last)]
            else:
                war = []
            p.dma("gpsimd", w12[bi].rearrange("p (kc o) -> p kc o", o=128),
                  dr["l0_w1T"][:, oc * 128:(oc + 1) * 128].rearrange("(kc p) o -> p kc o", p=128),
                  waits=war, incs=[(dw, 16)])
            p.dma("gpsimd", w12[2 + bi].rearrange("p (kc o) -> p kc o", o=128),
                  dr["l0_w2T"][:, oc * 128:(oc + 1) * 128].rearrange("(kc p) o -> p kc o", p=128),
                  waits=list(war), incs=[(dw, 16)])
            w_ready = dw.now()
            for ns in range(2):
                for kc in range(DC):
                    w = []
                    if kc == 0:
                        w = entry + [(dw, w_ready)]
                        if psa_free is not None:
                            w.append((ssem, psa_free))
                    p.mm(ps_a[:, 0:512], w12[bi][:, kc * 128:kc * 128 + 128],
                         XNs[:, kc * N + tok0 + ns * 512: kc * N + tok0 + ns * 512 + 512],
                         kc == 0, kc == DC - 1, waits=w,
                         incs=[(tsem, 1)] if kc == DC - 1 else [])
                mm1 = tsem.now()
                for kc in range(DC):
                    w = []
                    if kc == 0 and psb_free is not None:
                        w = [(vsem, psb_free)]
                    p.mm(ps_b[:, 0:512], w12[2 + bi][:, kc * 128:kc * 128 + 128],
                         XNs[:, kc * N + tok0 + ns * 512: kc * N + tok0 + ns * 512 + 512],
                         kc == 0, kc == DC - 1, waits=w,
                         incs=[(tsem, 1)] if kc == DC - 1 else [])
                mm2 = tsem.now()
                w12_mm_done[oc] = mm2
                p.act(sil, ps_a[:, 0:512], AF.Silu, scale=scales["w1"][:, oc:oc + 1],
                      waits=[(tsem, mm1), (dw, sc_ready)], incs=[(ssem, 1)])
                psa_free = ssem.now()
                p.vop(lambda e, o=a0[:, oc * 1024 + ns * 512: oc * 1024 + ns * 512 + 512],
                      i=ps_b[:, 0:512], s=scales["w2"][:, oc:oc + 1], m=sil:
                      e.scalar_tensor_tensor(o, i, s, m[:], ALU.mult, ALU.mult),
                      waits=[(tsem, mm2), (ssem, psa_free), (dw, sc_ready)],
                      incs=[(vsem, 1)])
                psb_free = vsem.now()
        w12_last = max(w12_mm_done.values())
        a0_done = vsem.now()

        def h_layer(wkey, src, dst, src_ready, extra_wait, prev_reader, nchunks_out):
            mm_done = {}
            free = {}
            for oc in range(nchunks_out):
                bi = oc % 2
                war = [(tsem, mm_done[oc - 2])] if oc >= 2 else [(tsem, prev_reader)]
                p.dma("gpsimd", wh[bi].rearrange("p (kc o) -> p kc o", o=128),
                      dr[f"l0_{wkey}T"][:, oc * 128:(oc + 1) * 128].rearrange("(kc p) o -> p kc o", p=128),
                      waits=war, incs=[(dw, 16)])
                w_ready = dw.now()
                for ns in range(2):
                    ppid = (oc * 2 + ns) % 2
                    pp = [ps_a, ps_b][ppid]
                    for kc in range(HCn):
                        w = []
                        if kc == 0:
                            w = [src_ready, (dw, w_ready)]
                            if ppid in free:
                                w.append((ssem, free[ppid]))
                            if extra_wait is not None:
                                w.append(extra_wait)
                        p.mm(pp[:, 0:512], wh[bi][:, kc * 128:kc * 128 + 128],
                             src[:, kc * 1024 + ns * 512: kc * 1024 + ns * 512 + 512],
                             kc == 0, kc == HCn - 1, waits=w,
                             incs=[(tsem, 1)] if kc == HCn - 1 else [])
                    mmd = tsem.now()
                    mm_done[oc] = mmd
                    p.act(dst[:, oc * 1024 + ns * 512: oc * 1024 + ns * 512 + 512],
                          pp[:, 0:512], AF.Gelu, scale=scales[wkey][:, oc:oc + 1],
                          waits=[(tsem, mmd), (dw, sc_ready)], incs=[(ssem, 1)])
                    free[ppid] = ssem.now()
            return ssem.now(), max(mm_done.values())

        a1_done, h1_last = h_layer("h1", a0, a1, (vsem, a0_done),
                                   (ssem, psa_free), wh_last_reader, HCn)
        a2_done, h2_last = h_layer("h2", a1, a0, (ssem, a1_done),
                                   (tsem, h1_last), h1_last, HCn)
        wh_last_reader = h2_last

        mm_done = {}
        free_v = {}
        for oc in range(DC):
            bi = oc % 2
            war = [(tsem, mm_done[oc - 2])] if oc >= 2 else [(tsem, wh_last_reader)]
            p.dma("gpsimd", wh[bi].rearrange("p (kc o) -> p kc o", o=128),
                  dr["l0_w3T"][:, oc * 128:(oc + 1) * 128].rearrange("(kc p) o -> p kc o", p=128),
                  waits=war, incs=[(dw, 16)])
            w_ready = dw.now()
            for ns in range(2):
                ppid = (oc * 2 + ns) % 2
                pp = [ps_a, ps_b][ppid]
                for kc in range(HCn):
                    w = []
                    if kc == 0:
                        w = [(ssem, a2_done), (dw, w_ready)]
                        if ppid in free_v:
                            w.append((vsem, free_v[ppid]))
                        else:
                            w.append((ssem, a2_done))
                    p.mm(pp[:, 0:512], wh[bi][:, kc * 128:kc * 128 + 128],
                         a0[:, kc * 1024 + ns * 512: kc * 1024 + ns * 512 + 512],
                         kc == 0, kc == HCn - 1, waits=w,
                         incs=[(tsem, 1)] if kc == HCn - 1 else [])
                mmd = tsem.now()
                mm_done[oc] = mmd
                ai = (oc * 2 + ns) % 2
                aw = [(tsem, mmd), (dw, sc_ready)]
                if ai in acc_dma:
                    aw.append((dacc, acc_dma[ai]))
                p.vop(lambda e, o=acc[ai], i=pp[:, 0:512], s=scales["w3"][:, oc:oc + 1]:
                      e.tensor_scalar(o[:], i, s, None, ALU.mult),
                      waits=aw, incs=[(vsem, 1)])
                free_v[ppid] = vsem.now()
                vd = vsem.now()
                p.dma("gpsimd", xdram[oc * 128:(oc + 1) * 128,
                                      tok0 + ns * 512: tok0 + ns * 512 + 512],
                      acc[ai][:], waits=[(vsem, vd)], incs=[(dacc, 16)],
                      accum_op=ALU.add)
                acc_dma[ai] = dacc.now()
        wh_last_reader = max(mm_done.values())
    p.emit("sync", lambda e: e.nop(), waits=[(dacc, p.sem("dacc").now())])


def _alloc_psum(nc):
    ps = {
        "s": nc.alloc_psum_tensor("ps_s", [128, 2048], F32),
        "a": nc.alloc_psum_tensor("ps_a", [128, 512], F32),
        "b": nc.alloc_psum_tensor("ps_b", [128, 512], F32),
        "t": nc.alloc_psum_tensor("ps_t", [128, 1024], BF16),
        "d": nc.alloc_psum_tensor("ps_d", [128, 512], F32),
    }
    ps["c"] = ps["s"][:, 0:512]
    ps["d_ln"] = ps["s"][:, 512:1024]
    return ps


def _build_l0attn_test(weights):
    nc = bass.Bass("TRN2", target_bir_lowering=False)
    dr = {}
    dr["x_in"] = nc.dram_tensor("x_in", [N, D], F32, kind="ExternalInput").ap()
    dr["out"] = nc.dram_tensor("out", [N, D], F32, kind="ExternalOutput").ap()
    for k, v in host_constants().items():
        dr[k] = nc.dram_tensor("c_" + k, list(v.shape), F32, kind="ExternalInput").ap()
    for k in ("l0_WqkvT", "l0_bq_pp", "l0_bk_pp", "l0_bv_pp", "l0_WprojT", "l0_bproj"):
        dr[k] = nc.dram_tensor(k, list(weights[k].shape), F32, kind="ExternalInput").ap()
    p = P(nc)
    t = alloc_common(nc, want_rel=False)
    a = alloc_attn(p, rel=False)
    psum = _alloc_psum(nc)
    load_consts(p, t, dr)
    ph_load_x(p, t, dr, psum)
    x_ready = [(p.sem("v"), p.sem("v").now())]
    ln_done = ph_ln(p, t, psum, "X", "XN", entry_waits=x_ready)
    cfg = cfg_l0_attn(p, nc, t, dr, entry_waits=[(p.sem("v"), ln_done)])
    ph_attn(p, t, a, dr, psum, cfg)
    ph_store_out(p, t, dr, psum)
    p.replay()
    return nc


def _build_identity_test(with_ln=False):
    nc = bass.Bass("TRN2", target_bir_lowering=False)
    dr = {}
    dr["x_in"] = nc.dram_tensor("x_in", [N, D], F32, kind="ExternalInput").ap()
    dr["out"] = nc.dram_tensor("out", [N, D], F32, kind="ExternalOutput").ap()
    if with_ln:
        dr["xn_out"] = nc.dram_tensor("xn_out", [128, DC * N], BF16, kind="ExternalOutput").ap()
    for k, v in host_constants().items():
        dr[k] = nc.dram_tensor("c_" + k, list(v.shape), F32, kind="ExternalInput").ap()
    p = P(nc)
    t = alloc_common(nc)
    psum = _alloc_psum(nc)
    load_consts(p, t, dr)
    ph_load_x(p, t, dr, psum)
    if with_ln:
        x_ready = [(p.sem("v"), p.sem("v").now())]
        done = ph_ln(p, t, psum, "X", "XN", entry_waits=x_ready)
        do = p.sem("do")
        p.dma("sync", dr["xn_out"], t["XN"][:], waits=[(p.sem("v"), done)],
              incs=[(do, 16)])
    ph_store_out(p, t, dr, psum)
    p.replay()
    return nc


def cfg_l0_attn(p, nc, t, dr, entry_waits):
    dw = p.sem("dw")
    wq = [p.alloc(f"l0wq{i}", [128, 8 * 384], BF16) for i in range(2)]
    bq = p.alloc("l0bq", [128, 8], F32)
    bk = p.alloc("l0bk", [128, 8], F32)
    bv = p.alloc("l0bv", [128, 8], F32)
    bqs = p.alloc("l0bqs", [128, 8], F32)
    bp = p.alloc("l0bp", [128, 8], F32)
    for nm, h in (("l0_bq_pp", bq), ("l0_bk_pp", bk), ("l0_bv_pp", bv), ("l0_bproj", bp)):
        p.dma("sync", h[:], dr[nm], incs=[(dw, 16)])
    bias_ready = dw.now()
    vsem = p.sem("v")
    p.vop(lambda e, o=bqs, i=bq: e.tensor_scalar(o[:], i[:], 0.125, None, ALU.mult),
          waits=[(dw, bias_ready)], incs=[(vsem, 1)])
    bias_v = vsem.now()

    def wq_dma(p_, hg, bufi, war):
        p_.dma("gpsimd", wq[bufi].rearrange("p (kc o) -> p kc o", o=384),
               dr["l0_WqkvT"][:, hg * 384:(hg + 1) * 384].rearrange("(kc p) o -> p kc o", p=128),
               waits=war, incs=[(dw, 16)])
        return lambda oc3, kc, b=wq[bufi]: b[:, kc * 384 + oc3 * 128: kc * 384 + oc3 * 128 + 128]

    def evac_spec(hg, oc3):
        dk = ("Q", "K", "V")[oc3]
        bias = (bqs, bk, bv)[oc3][:, hg:hg + 1]
        sc = 0.125 if oc3 == 0 else 1.0
        return [(0, 128, dk, 0, bias, sc)]

    return dict(name="l0at", xn_key="XN", causal=True, rope=True, rel=False,
                nchunks=3, wq_dma=wq_dma, evac_spec=evac_spec,
                proj_w=dr["l0_WprojT"], proj_bias=bp,
                bias_waits=[(dw, bias_ready), (vsem, bias_v)],
                entry_waits=entry_waits)


def cfg_l1_attn(p, nc, t, dr, entry_waits):
    dw = p.sem("dw")
    wq = [p.alloc("l1wq0", [128, 3 * DC * 128], BF16)]
    bq = p.alloc("l1bq", [128, 8], F32)
    bk = p.alloc("l1bk", [128, 8], F32)
    bv = p.alloc("l1bv", [128, 8], F32)
    bqs = p.alloc("l1bqs", [128, 8], F32)
    bo = p.alloc("l1bo", [128, 8], F32)
    for nm, h in (("l1_bq", bq), ("l1_bk", bk), ("l1_bv", bv), ("l1_bo", bo)):
        p.dma("sync", h[:], dr[nm], incs=[(dw, 16)])
    bias_ready = dw.now()
    vsem = p.sem("v")
    p.vop(lambda e, o=bqs, i=bq: e.tensor_scalar(o[:], i[:], 0.125, None, ALU.mult),
          waits=[(dw, bias_ready)], incs=[(vsem, 1)])
    bias_v = vsem.now()

    def wq_dma(p_, hg, bufi, war):
        for wi, wname in enumerate(("l1_Wq", "l1_Wk", "l1_Wv")):
            p_.dma("gpsimd",
                   wq[0].rearrange("p (w kc o) -> p w kc o", w=3, o=128)[:, wi],
                   dr[wname][:, hg * 128:(hg + 1) * 128].rearrange("(kc p) o -> p kc o", p=128),
                   waits=list(war) if wi == 0 else [], incs=[(dw, 16)])
        return lambda oc3, kc, b=wq[0]: b[:, oc3 * 1024 + kc * 128: oc3 * 1024 + kc * 128 + 128]

    def evac_spec(hg, oc3):
        dk = ("Q", "K", "V")[oc3]
        bias = (bqs, bk, bv)[oc3][:, hg:hg + 1]
        sc = 0.125 if oc3 == 0 else 1.0
        return [(0, 128, dk, 0, bias, sc)]

    return dict(name="l1at", xn_key="XN", causal=False, rope=False, rel=True,
                nchunks=3, wq_dma=wq_dma, evac_spec=evac_spec, wq_depth=1,
                proj_w=dr["l1_Wo"], proj_bias=bo,
                bias_waits=[(dw, bias_ready), (vsem, bias_v)],
                entry_waits=entry_waits)


def cfg_msa(p, nc, t, dr, entry_waits):
    dw = p.sem("dw")
    wq = [p.alloc("msawq0", [128, 3 * DC * 128], BF16)]

    def wq_dma(p_, hg, bufi, war):
        for wi in range(3):
            p_.dma("gpsimd",
                   wq[0].rearrange("p (w kc o) -> p w kc o", w=3, o=128)[:, wi],
                   dr["l1_wqkv"][:, wi * 1024 + hg * 128: wi * 1024 + (hg + 1) * 128]
                   .rearrange("(kc p) o -> p kc o", p=128),
                   waits=list(war) if wi == 0 else [], incs=[(dw, 16)])
        return lambda oc3, kc, b=wq[0]: b[:, oc3 * 1024 + kc * 128: oc3 * 1024 + kc * 128 + 128]

    def evac_spec(hg, oc3):
        dk = ("Q", "K", "V")[oc3]
        sc = 0.125 if oc3 == 0 else 1.0
        return [(0, 128, dk, 0, None, sc)]

    return dict(name="msa", xn_key="XNa", causal=False, rope=False, rel=False,
                nchunks=3, wq_dma=wq_dma, evac_spec=evac_spec, wq_depth=1,
                proj_w=dr["l1_wproj"], proj_bias=None,
                out_extra="x1bf", x1_src="XN",
                bias_waits=[], entry_waits=entry_waits)


def ph_moe(p, t, dr, psum, arena_a, xdram_out, entry_waits=()):
    nc = p.nc
    tsem, vsem, ssem = p.sem("t"), p.sem("v"), p.sem("s")
    dw = p.sem("dw")
    ps_a, ps_b = psum["a"], psum["b"]
    X2 = t["XN2"]
    entry = list(entry_waits)
    cst = p.sem("cst")

    sel8 = p.alloc("moe_sel8", [8, 8 * 128], BF16)
    p.dma("gpsimd", sel8, dr["sel8"], incs=[(p.sem("dw"), 16)])
    gw = p.alloc("moe_gw", [128, DC * 8], BF16)
    gb = p.alloc("moe_gb", [1, 8], F32)
    gbb = p.alloc("moe_gbb", [1, 8], BF16)
    gbc = p.alloc("moe_gbc", [128, 8], F32)
    glog = p.alloc("moe_glog", [128, 8], F32)
    gmax = p.alloc("moe_gmax", [128, 1], F32)
    oh_tm = p.alloc("moe_ohtm", [128, 16 * 8], BF16)
    ohfm = p.alloc("moe_ohfm", [8, N], BF16)
    maskbc = p.alloc("moe_mask", [128, N], BF16)
    w1b = [p.alloc(f"moe_w1{i}", [128, DC * 128], BF16) for i in range(2)]
    w2b_t = p.alloc("moe_w2", [128, HCn * 128], BF16)
    w2b = [w2b_t, w2b_t]
    ytmp = [t["s512_0"], t["s512_1"]]
    p.dma("gpsimd", gw.rearrange("p (kc e) -> p kc e", e=8),
          dr["l1_gatew"].rearrange("(kc p) e -> p kc e", p=128), incs=[(dw, 16)])
    p.dma("sync", gb, dr["l1_gateb"], incs=[(dw, 16)])
    gw_ready = dw.now()
    p.vop(lambda e, o=gbb, i=gb: e.tensor_copy(o[:], i[:]),
          waits=[(dw, gw_ready)], incs=[(vsem, 1)])
    p.mm(psum["d"][:, 0:8], t["ones_1x128_b"][:], gbb[:], True, True,
         waits=[(vsem, vsem.now()), (cst, t.get('cst_all', 16))], incs=[(tsem, 1)])
    p.vop(lambda e, o=gbc, i=psum["d"][:, 0:8]: e.tensor_copy(o[:], i),
          waits=[(tsem, tsem.now())], incs=[(vsem, 1)])
    gbc_done = vsem.now()

    glog_done = {}
    for tt in range(16):
        for kc in range(DC):
            w = []
            if kc == 0:
                w = entry + [(dw, gw_ready), (vsem, glog_done.get(tt - 1, 0))]
                w.append((vsem, gbc_done))
            p.mm(ps_a[:, 0:8], X2[:, kc * N + tt * 128: kc * N + tt * 128 + 128],
                 gw[:, kc * 8: kc * 8 + 8], kc == 0, kc == DC - 1,
                 waits=w, incs=[(tsem, 1)] if kc == DC - 1 else [])
        mmd = tsem.now()
        p.vop(lambda e, o=glog, i=ps_a[:, 0:8], b=gbc:
              e.tensor_add(o[:], i, b[:]), waits=[(tsem, mmd)], incs=[(vsem, 1)])
        ga = vsem.now()
        p.vop(lambda e, o=gmax, i=glog: e.reduce_max(o[:], i[:], axis=AX.X),
              waits=[(vsem, ga)], incs=[(vsem, 1)])
        gm = vsem.now()
        p.vop(lambda e, o=oh_tm[:, tt * 8:tt * 8 + 8], i=glog, s=gmax[:, 0:1]:
              e.tensor_scalar(o, i[:], s, None, ALU.is_equal),
              waits=[(vsem, gm)], incs=[(vsem, 1)])
        glog_done[tt] = vsem.now()
    pt_free = None
    for tt in range(16):
        w = [(vsem, glog_done[tt]), (cst, t.get('cst_all', 16))]
        if pt_free is not None:
            w.append((vsem, pt_free))
        p.tr(psum["t"][0:8, 0:128], oh_tm[:, tt * 8:tt * 8 + 8], t["ident_b"][:],
             waits=w, incs=[(tsem, 1)])
        p.vop(lambda e, o=ohfm[0:8, tt * 128:(tt + 1) * 128], i=psum["t"][0:8, 0:128]:
              e.tensor_copy(o, i), waits=[(tsem, tsem.now())], incs=[(vsem, 1)])
        pt_free = vsem.now()
    oh_done = vsem.now()
    if "dbg_oh" in dr:
        dbg = p.sem("dbg")
        p.dma("sync", dr["dbg_oh"], oh_tm[:], waits=[(vsem, oh_done)], incs=[(dbg, 16)])
        p.dma("sync", dr["dbg_fm"], ohfm[:], waits=[(vsem, oh_done)], incs=[(dbg, 16)])
        if "dbg_log" in dr:
            p.dma("sync", dr["dbg_log"], glog[:], waits=[(vsem, oh_done)], incs=[(dbg, 16)])
        if "dbg_max" in dr:
            p.dma("sync", dr["dbg_max"], gmax[:], waits=[(vsem, oh_done)], incs=[(dbg, 16)])
        if "dbg_gbc" in dr:
            p.dma("sync", dr["dbg_gbc"], gbc[:], waits=[(vsem, oh_done)], incs=[(dbg, 16)])

    a = arena_a
    mask_done = 0
    w1_mm_done = {}
    w2_mm_done = {}
    yt_dma = {}
    for e_i in range(8):
        for nb4 in range(4):
            p.mm(psum["d"][:, 0:512], sel8[:, e_i * 128:(e_i + 1) * 128],
                 ohfm[0:8, nb4 * 512:(nb4 + 1) * 512], True, True,
                 waits=[(vsem, vsem.now()), (cst, t.get('cst_all', 16)), (dw, gw_ready)],
                 incs=[(tsem, 1)])
            p.vop(lambda e, o=maskbc[:, nb4 * 512:(nb4 + 1) * 512],
                  i=psum["d"][:, 0:512]: e.tensor_copy(o, i),
                  waits=[(tsem, tsem.now())], incs=[(vsem, 1)])
        mask_ready = vsem.now()

        for pas in range(2):
            tok0 = pas * 1024
            free_s = {}
            for oc in range(HCn):
                bi = oc % 2
                war = [(tsem, w1_mm_done[(e_i, pas, oc - 2)])] if oc >= 2 else                     [(tsem, w1_mm_done.get((e_i, pas - 1, HCn - 1),
                                           w1_mm_done.get((e_i - 1, 1, HCn - 1), 0)))]
                p.dma("gpsimd", w1b[bi].rearrange("p (kc o) -> p kc o", o=128),
                      dr["l1_moe_w1"][e_i][:, oc * 128:(oc + 1) * 128]
                      .rearrange("(kc p) o -> p kc o", p=128),
                      waits=war, incs=[(dw, 16)])
                w_ready = dw.now()
                for ns in range(2):
                    ppid = (oc * 2 + ns) % 2
                    pp = [ps_a, ps_b][ppid]
                    for kc in range(DC):
                        w = []
                        if kc == 0:
                            w = entry + [(dw, w_ready)]
                            if ppid in free_s:
                                w.append((ssem, free_s[ppid]))
                            else:
                                w.append((ssem, ssem.now()))
                        p.mm(pp[:, 0:512], w1b[bi][:, kc * 128:kc * 128 + 128],
                             X2[:, kc * N + tok0 + ns * 512: kc * N + tok0 + ns * 512 + 512],
                             kc == 0, kc == DC - 1, waits=w,
                             incs=[(tsem, 1)] if kc == DC - 1 else [])
                    mmd = tsem.now()
                    w1_mm_done[(e_i, pas, oc)] = mmd
                    p.act(a[:, oc * 1024 + ns * 512: oc * 1024 + ns * 512 + 512],
                          pp[:, 0:512], AF.Gelu_apprx_tanh,
                          waits=[(tsem, mmd)], incs=[(ssem, 1)])
                    free_s[ppid] = ssem.now()
            a_done = ssem.now()
            free_v = {}
            for oc in range(DC):
                bi = oc % 2
                war = [(tsem, w2_mm_done[(e_i, pas, oc - 1)])] if oc >= 1 else                     [(tsem, w2_mm_done.get((e_i, pas - 1, DC - 1),
                                           w2_mm_done.get((e_i - 1, 1, DC - 1), 0)))]
                p.dma("gpsimd", w2b[bi].rearrange("p (kc o) -> p kc o", o=128),
                      dr["l1_moe_w2"][e_i][:, oc * 128:(oc + 1) * 128]
                      .rearrange("(kc p) o -> p kc o", p=128),
                      waits=war, incs=[(dw, 16)])
                w_ready = dw.now()
                for ns in range(2):
                    ppid = (oc * 2 + ns) % 2
                    pp = [ps_a, ps_b][ppid]
                    for kc in range(HCn):
                        w = []
                        if kc == 0:
                            w = [(ssem, a_done), (dw, w_ready)]
                            if ppid in free_v:
                                w.append((vsem, free_v[ppid]))
                        p.mm(pp[:, 0:512], w2b[bi][:, kc * 128:kc * 128 + 128],
                             a[:, kc * 1024 + ns * 512: kc * 1024 + ns * 512 + 512],
                             kc == 0, kc == HCn - 1, waits=w,
                             incs=[(tsem, 1)] if kc == HCn - 1 else [])
                    mmd = tsem.now()
                    w2_mm_done[(e_i, pas, oc)] = mmd
                    yi = (oc * 2 + ns) % 2
                    msl = slice(tok0 + ns * 512, tok0 + ns * 512 + 512)
                    mw = [(tsem, mmd), (vsem, mask_ready)]
                    if yi in yt_dma:
                        mw.append((p.sem("dacc"), yt_dma[yi]))
                    p.vop(lambda e, o=ytmp[yi], i=pp[:, 0:512], m=maskbc[:, msl]:
                          e.tensor_mul(o[:], i, m), waits=mw, incs=[(vsem, 1)])
                    free_v[ppid] = vsem.now()
                    dacc = p.sem("dacc")
                    p.dma("gpsimd", xdram_out[oc * 128:(oc + 1) * 128, msl],
                          ytmp[yi][:], waits=[(vsem, vsem.now())],
                          incs=[(dacc, 16)], accum_op=ALU.add)
                    yt_dma[yi] = dacc.now()
    return vsem.now()


L1A_KEYS = ("l1_Wq", "l1_Wk", "l1_Wv", "l1_Wo", "l1_bq", "l1_bk", "l1_bv",
            "l1_bo", "l1_relT")


def _build_l1attn_test(weights):
    nc = bass.Bass("TRN2", target_bir_lowering=False)
    dr = {}
    dr["x_in"] = nc.dram_tensor("x_in", [N, D], F32, kind="ExternalInput").ap()
    dr["out"] = nc.dram_tensor("out", [N, D], F32, kind="ExternalOutput").ap()
    for k, v in host_constants().items():
        dr[k] = nc.dram_tensor("c_" + k, list(v.shape), F32, kind="ExternalInput").ap()
    for k in L1A_KEYS:
        dr[k] = nc.dram_tensor(k, list(weights[k].shape), F32, kind="ExternalInput").ap()
    p = P(nc)
    t = alloc_common(nc, want_rope=False, want_rel=True)
    a = alloc_attn(p, rel=True)
    psum = _alloc_psum(nc)
    load_consts(p, t, dr)
    ph_load_x(p, t, dr, psum)
    x_ready = [(p.sem("v"), p.sem("v").now())]
    ln_done = ph_ln(p, t, psum, "X", "XN", entry_waits=x_ready)
    cfg = cfg_l1_attn(p, nc, t, dr, entry_waits=[(p.sem("v"), ln_done)])
    ph_attn(p, t, a, dr, psum, cfg)
    ph_store_out(p, t, dr, psum)
    p.replay()
    return nc


def _np_l1attn(x, w):
    T, Dm, H, hd, MD = N, D, 16, 64, 32
    xn = _np_ln(x)
    proj = lambda W, b: (xn @ w[W] + w[b]).reshape(T, H, hd).transpose(1, 0, 2)
    q = proj("L1_attn_Wq", "L1_attn_bq")
    k = proj("L1_attn_Wk", "L1_attn_bk")
    v = proj("L1_attn_Wv", "L1_attn_bv")
    sc = hd ** -0.5
    pos = np.arange(T)
    relm = np.clip(pos[:, None] - pos[None, :], -MD, MD) + MD
    rpe = w["L1_attn_rel_emb"][relm]
    cc = np.einsum("hqd,hkd->hqk", q, k) * sc
    cp = np.einsum("hqd,qkd->hqk", q, rpe) * sc
    att = _np_softmax(cc + cp, -1)
    ctx = np.einsum("hqk,hkd->hqd", att, v)
    ctx = ctx.transpose(1, 0, 2).reshape(T, Dm)
    return x + ctx @ w["L1_attn_Wo"] + w["L1_attn_bo"]


def _test_l1attn():
    w = dict(np.load("/tmp/ref_inputs_0.npz"))
    rel = relayout_weights(w)
    x = np.ascontiguousarray(w["L1_attn_x"][0] if "L1_attn_x" in w else w["L0_attn_x"][0])
    nc = _build_l1attn_test(rel)
    im = {"x_in": x}
    im.update({"c_" + k: v for k, v in host_constants().items()})
    for k in L1A_KEYS:
        im[k] = rel[k]
    res = run_bass_kernel_spmd(nc, [im], core_ids=[0])
    out = res.results[0]["out"]
    wd = {k: v.astype(np.float64) for k, v in w.items()}
    ref = _np_l1attn(x.astype(np.float64), wd).astype(np.float32)
    err = np.abs(out - ref)
    print("l1attn max err:", err.max(), "rel l2:",
          np.sqrt((err ** 2).mean() / (ref ** 2).mean()))


L0A_KEYS = ("l0_WqkvT", "l0_bq_pp", "l0_bk_pp", "l0_bv_pp", "l0_WprojT", "l0_bproj")
L0F_KEYS = ("l0_w1T", "l0_w2T", "l0_h1T", "l0_h2T", "l0_w3T",
            "l0_w1s", "l0_w2s", "l0_h1s", "l0_h2s", "l0_w3s")
MSA_KEYS = ("l1_wqkv", "l1_wproj", "l1_ln1w", "l1_ln1b", "l1_ln2w", "l1_ln2b")
MOE_KEYS = ("l1_moe_w1", "l1_moe_w2", "l1_gatew", "l1_gateb")


def _declare(nc, dr, weights, keys):
    for k in keys:
        dr[k] = nc.dram_tensor(k, list(weights[k].shape), F32, kind="ExternalInput").ap()


def build_launch_A(weights):
    nc = bass.Bass("TRN2", target_bir_lowering=False)
    dr = {}
    dr["x_in"] = nc.dram_tensor("x_in", [N, D], F32, kind="ExternalInput").ap()
    dr["xdram"] = nc.dram_tensor("xdram", [D, N], F32, kind="ExternalOutput").ap()
    dr["xnd"] = nc.dram_tensor("xnd", [128, DC * N], BF16, kind="ExternalOutput").ap()
    for k, v in host_constants().items():
        dr[k] = nc.dram_tensor("c_" + k, list(v.shape), F32, kind="ExternalInput").ap()
    _declare(nc, dr, weights, L0A_KEYS)
    p = P(nc)
    t = alloc_common(nc, want_rel=False)
    a = alloc_attn(p, rel=False)
    psum = _alloc_psum(nc)
    load_consts(p, t, dr)
    ph_load_x(p, t, dr, psum)
    x_ready = [(p.sem("v"), p.sem("v").now())]
    ln_done = ph_ln(p, t, psum, "X", "XN", entry_waits=x_ready)
    cfg = cfg_l0_attn(p, nc, t, dr, entry_waits=[(p.sem("v"), ln_done)])
    attn_done = ph_attn(p, t, a, dr, psum, cfg)
    do = p.sem("do")
    for c in range(DC):
        p.dma("sync", dr["xdram"][c * 128:(c + 1) * 128, :], t["X"][:, c * N:(c + 1) * N],
              waits=[(p.sem("v"), attn_done)], incs=[(do, 16)])
    p.dma("sync", dr["xnd"], t["XN"][:], waits=[(p.sem("v"), ln_done)], incs=[(do, 16)])
    p.emit("sync", lambda e: e.nop(), waits=[(do, do.now())])
    p.replay()
    return nc


def build_launch_B(weights):
    nc = bass.Bass("TRN2", target_bir_lowering=False)
    dr = {}
    dr["xdram_in"] = nc.dram_tensor("xdram_in", [D, N], F32, kind="ExternalInput").ap()
    dr["xnd"] = nc.dram_tensor("xnd", [128, DC * N], BF16, kind="ExternalInput").ap()
    xdram = nc.dram_tensor("xdram", [D, N], F32, kind="ExternalOutput").ap()
    _declare(nc, dr, weights, L0F_KEYS)
    p = P(nc)
    t = {"XN": nc.alloc_sbuf_tensor("XN", [128, DC * N], BF16)}
    psum = _alloc_psum(nc)
    dx, dcp = p.sem("dx"), p.sem("dcp")
    p.dma("sync", t["XN"][:], dr["xnd"], incs=[(dx, 16)])
    p.dma("sync", xdram, dr["xdram_in"], incs=[(dcp, 16)])
    ph_l0_ffn(p, t, dr, psum, xdram, entry_waits=[(dx, 16), (dcp, 16)])
    p.replay()
    return nc


def build_launch_C(weights):
    nc = bass.Bass("TRN2", target_bir_lowering=False)
    dr = {}
    dr["xdram_in"] = nc.dram_tensor("xdram_in", [D, N], F32, kind="ExternalInput").ap()
    dr["xdram"] = nc.dram_tensor("xdram", [D, N], F32, kind="ExternalOutput").ap()
    dr["xnd1"] = nc.dram_tensor("xnd1", [128, DC * N], BF16, kind="ExternalOutput").ap()
    for k, v in host_constants().items():
        dr[k] = nc.dram_tensor("c_" + k, list(v.shape), F32, kind="ExternalInput").ap()
    _declare(nc, dr, weights, L1A_KEYS)
    p = P(nc)
    t = alloc_common(nc, want_rope=False, want_rel=True)
    a = alloc_attn(p, rel=True)
    psum = _alloc_psum(nc)
    load_consts(p, t, dr)
    dx = p.sem("dx")
    for c in range(DC):
        p.dma("sync", t["X"][:, c * N:(c + 1) * N], dr["xdram_in"][c * 128:(c + 1) * 128, :],
              incs=[(dx, 16)])
    x_ready = [(dx, dx.now())]
    ln_done = ph_ln(p, t, psum, "X", "XN", entry_waits=x_ready)
    vsem = p.sem("v")
    for c in range(DC):
        p.vop(lambda e, o=t["X"][:, c * N:(c + 1) * N], i=t["XN"][:, c * N:(c + 1) * N]:
              e.tensor_add(o, o, i), waits=[(vsem, ln_done)], incs=[(vsem, 1)])
    cfg = cfg_l1_attn(p, nc, t, dr, entry_waits=[(vsem, ln_done)])
    attn_done = ph_attn(p, t, a, dr, psum, cfg)
    do = p.sem("do")
    for c in range(DC):
        p.dma("sync", dr["xdram"][c * 128:(c + 1) * 128, :], t["X"][:, c * N:(c + 1) * N],
              waits=[(vsem, attn_done)], incs=[(do, 16)])
    p.dma("sync", dr["xnd1"], t["XN"][:], waits=[(vsem, ln_done)], incs=[(do, 16)])
    p.emit("sync", lambda e: e.nop(), waits=[(do, do.now())])
    p.replay()
    return nc


def build_launch_D(weights):
    nc = bass.Bass("TRN2", target_bir_lowering=False)
    dr = {}
    dr["xdram_in"] = nc.dram_tensor("xdram_in", [D, N], F32, kind="ExternalInput").ap()
    dr["xnd1"] = nc.dram_tensor("xnd1", [128, DC * N], BF16, kind="ExternalInput").ap()
    dr["xdram"] = nc.dram_tensor("xdram", [D, N], F32, kind="ExternalOutput").ap()
    dr["xn2d"] = nc.dram_tensor("xn2d", [128, DC * N], BF16, kind="ExternalOutput").ap()
    for k, v in host_constants().items():
        dr[k] = nc.dram_tensor("c_" + k, list(v.shape), F32, kind="ExternalInput").ap()
    _declare(nc, dr, weights, MSA_KEYS)
    p = P(nc)
    t = alloc_common(nc, want_rope=False, want_rel=False, want_io=False, want_x=False)
    t["XNa"] = nc.alloc_sbuf_tensor("XNa", [128, DC * N], BF16)
    t["x1bf"] = t["XNa"]
    a = alloc_attn(p, rel=False)
    psum = _alloc_psum(nc)
    load_consts(p, t, dr)
    dx, dw, dcp = p.sem("dx"), p.sem("dw"), p.sem("dcp")
    p.dma("sync", dr["xdram"], dr["xdram_in"], incs=[(dcp, 16)])
    p.dma("sync", t["XN"][:], dr["xnd1"], incs=[(dx, 16)])
    x_ready = dx.now()
    lnw = {}
    for k in ("ln1w", "ln1b", "ln2w", "ln2b"):
        lnw[k] = nc.alloc_sbuf_tensor("w_" + k, [128, 8], F32)
        p.dma("sync", lnw[k][:], dr[f"l1_{k}"], incs=[(dw, 16)])
    lnw_ready = dw.now()
    ln1_done = ph_ln(p, t, psum, "XN", "XNa", wb=(lnw["ln1w"], lnw["ln1b"]),
                     entry_waits=[(dx, x_ready), (dw, lnw_ready)], src_bf16=True)
    cfg = cfg_msa(p, nc, t, dr, entry_waits=[(p.sem("v"), ln1_done)])
    cfg["out_dram"] = dr["xdram"]
    cfg["copy_wait"] = (dcp, 16)
    msa_done = ph_attn(p, t, a, dr, psum, cfg)
    ln2_done = ph_ln(p, t, psum, "x1bf", "XN", wb=(lnw["ln2w"], lnw["ln2b"]),
                     entry_waits=[(p.sem("v"), msa_done), (dw, lnw_ready),
                                  (p.sem("dacc"), p.sem("dacc").now())],
                     src_bf16=True)
    do = p.sem("do")
    p.dma("sync", dr["xn2d"], t["XN"][:], waits=[(p.sem("v"), ln2_done)], incs=[(do, 16)])
    p.emit("sync", lambda e: e.nop(),
           waits=[(do, do.now()), (p.sem("dacc"), p.sem("dacc").now())])
    p.replay()
    return nc


def build_launch_E(weights):
    nc = bass.Bass("TRN2", target_bir_lowering=False)
    dr = {}
    dr["xdram_in"] = nc.dram_tensor("xdram_in", [D, N], F32, kind="ExternalInput").ap()
    dr["xn2d"] = nc.dram_tensor("xn2d", [128, DC * N], BF16, kind="ExternalInput").ap()
    dr["out"] = nc.dram_tensor("out", [N, D], F32, kind="ExternalOutput").ap()
    xdram = nc.dram_tensor("xdram_s", [D, N], F32, kind="Internal").ap()
    for k, v in host_constants().items():
        dr[k] = nc.dram_tensor("c_" + k, list(v.shape), F32, kind="ExternalInput").ap()
    _declare(nc, dr, weights, MOE_KEYS)
    p = P(nc)
    t = alloc_common(nc, want_rope=False, want_rel=False, want_ln=False)
    t["XN2"] = t["XN"]
    arena = nc.alloc_sbuf_tensor("arena", [128, HCn * 1024], BF16)
    psum = _alloc_psum(nc)
    load_consts(p, t, dr)
    dx, dcp = p.sem("dx"), p.sem("dcp")
    p.dma("sync", t["XN"][:], dr["xn2d"], incs=[(dx, 16)])
    p.dma("sync", xdram, dr["xdram_in"], incs=[(dcp, 16)])
    ph_moe(p, t, dr, psum, arena[:], xdram,
           entry_waits=[(dx, 16), (dcp, 16)])
    dacc = p.sem("dacc")
    acc_all = dacc.now()
    dx2 = p.sem("dx")
    for c in range(DC):
        p.dma("sync", t["X"][:, c * N:(c + 1) * N], xdram[c * 128:(c + 1) * 128, :],
              waits=[(dacc, acc_all)], incs=[(dx2, 16)])
    vsem = p.sem("v")
    p.emit("vector", lambda e: e.nop(), waits=[(dx2, dx2.now())], incs=[(vsem, 1)])
    ph_store_out(p, t, dr, psum)
    p.replay()
    return nc


ALL_KEYS = L0A_KEYS + L0F_KEYS + tuple(L1A_KEYS) + MSA_KEYS + MOE_KEYS


def build_fused(weights, debug=False, upto=99):
    nc = bass.Bass("TRN2", target_bir_lowering=False)
    dr = {}
    dr["x_in"] = nc.dram_tensor("x_in", [N, D], F32, kind="ExternalInput").ap()
    dr["out"] = nc.dram_tensor("out", [N, D], F32, kind="ExternalOutput").ap()
    xdram = nc.dram_tensor("xdram_s", [D, N], F32, kind="Internal").ap()
    for k, v in host_constants().items():
        dr[k] = nc.dram_tensor("c_" + k, list(v.shape), F32, kind="ExternalInput").ap()
    _declare(nc, dr, weights, ALL_KEYS)
    if debug:
        for nm in ("dbg_x0", "dbg_x1", "dbg_x2", "dbg_x3"):
            dr[nm] = nc.dram_tensor(nm, [D, N], F32, kind="ExternalOutput").ap()
        dr["dbg_xn"] = nc.dram_tensor("dbg_xn", [128, DC * N], BF16, kind="ExternalOutput").ap()

    p = P(nc)
    arena = nc.alloc_sbuf_tensor("arena", [128, 78336], BF16)
    arap = arena[:]

    def carve(off_kb, shape, dtype):
        s = int(off_kb * 512)
        w = shape[1] * (2 if dtype == F32 else 1)
        ap = arap[:shape[0], s:s + w]
        if dtype == F32:
            ap = ap.bitcast(F32)
        return ap

    p.bind = {}
    B = p.bind
    B["at_Q"] = carve(64, [128, N], BF16)
    B["at_K"] = carve(68, [128, N], BF16)
    B["at_V"] = carve(72, [128, N], BF16)
    B["at_VT"] = carve(76, [128, N], BF16)
    B["at_P"] = carve(80, [128, N], BF16)
    B["at_PT"] = carve(84, [128, N], BF16)
    B["at_CTX"] = carve(88, [128, DC * N], BF16)
    B["l0wq0"] = carve(120, [128, 3072], BF16)
    B["l0wq1"] = carve(126, [128, 3072], BF16)
    B["l1wq0"] = carve(120, [128, 3072], BF16)
    B["msawq0"] = carve(120, [128, 3072], BF16)
    B["at_qrel"] = carve(132, [128, 1040], F32)
    B["at_win"] = carve(136.5, [128, 3072], BF16)
    B["at_sw"] = carve(142.5, [128, 384], F32)
    B["at_bw"] = carve(144, [128, 384], BF16)
    B["at_bw2"] = carve(144.75, [128, 384], BF16)
    for nm in ("l0at", "l1at", "msa"):
        B[f"{nm}_wpb0"] = carve(146, [128, DC * 128], BF16)
        B[f"{nm}_wpb1"] = carve(148, [128, DC * 128], BF16)
    B["l1at_relT"] = carve(129.5, [128, 65], BF16)
    B["ffn_a1"] = carve(0, [128, HCn * 1024], BF16)
    B["ffn_a0"] = carve(64, [128, HCn * 1024], BF16)
    for i in range(4):
        B[f"ffn_w12_{i}"] = carve(128 + 2 * i, [128, DC * 128], BF16)
    B["ffn_wh0"] = carve(136, [128, HCn * 128], BF16)
    B["ffn_wh1"] = carve(144, [128, HCn * 128], BF16)
    B["ffn_sil"] = carve(152, [128, 512], BF16)
    B["moe_w10"] = carve(128, [128, DC * 128], BF16)
    B["moe_w11"] = carve(130, [128, DC * 128], BF16)
    B["moe_w2"] = carve(132, [128, HCn * 128], BF16)
    B["moe_mask"] = carve(140, [128, N], BF16)
    B["moe_gw"] = carve(144, [128, DC * 8], BF16)
    B["moe_sel8"] = carve(146, [8, 8 * 128], BF16)
    B["moe_ohfm"] = carve(148, [8, N], BF16)

    t = alloc_common(nc, want_rope=False, want_rel=False, want_io=False, want_x=False)
    t["X"] = carve(0, [128, DC * N], F32)
    t["XNa"] = carve(0, [128, DC * N], BF16)
    t["x1bf"] = t["XNa"]
    t["XN2"] = t["XN"]
    t["xio0"] = carve(128, [128, D], F32)
    t["xio1"] = carve(132, [128, D], F32)
    a = alloc_attn(p, rel=True)
    psum = _alloc_psum(nc)
    ffn_acc0 = t["s512_0"]
    ffn_acc1 = t["s512_1"]
    B["ffn_acc0"] = ffn_acc0[:]
    B["ffn_acc1"] = ffn_acc1[:]

    load_consts(p, t, dr)
    t["rope_a"] = carve(132, [128, N], BF16)
    t["rope_b"] = carve(136, [128, N], BF16)
    t["mband_b"] = carve(126, [128, 576], BF16)
    t["mleft_b"] = carve(127.25, [128, 576], BF16)
    t["mright_b"] = carve(130, [128, 576], BF16)
    cst = p.sem("cst")
    vsem = p.sem("v")

    ph_load_x(p, t, dr, psum)
    p.barrier()
    p.dma("gpsimd", t["rope_a"], dr["rope_a"], incs=[(cst, 16)])
    p.dma("gpsimd", t["rope_b"], dr["rope_b"], incs=[(cst, 16)])
    t["cst_all"] = cst.now()
    ln_done = ph_ln(p, t, psum, "X", "XN", entry_waits=[])
    cfg = cfg_l0_attn(p, nc, t, dr, entry_waits=[(vsem, ln_done)])
    ph_attn(p, t, a, dr, psum, cfg)
    p.barrier()
    do = p.sem("do")
    for c in range(DC):
        p.dma("sync", xdram[c * 128:(c + 1) * 128, :], t["X"][:, c * N:(c + 1) * N],
              incs=[(do, 16)])
    p.emit("sync", lambda e: e.nop(), waits=[(do, do.now())])
    p.barrier()
    if debug:
        dbg = p.sem("dbg")
        p.dma("sync", dr["dbg_x3"], xdram, incs=[(dbg, 16)])
        p.dma("sync", dr["dbg_xn"], t["XN"][:], incs=[(dbg, 16)])
        p.emit("sync", lambda e: e.nop(), waits=[(dbg, p.sem("dbg").now())])
        p.barrier()
    ph_l0_ffn(p, t, dr, psum, xdram, entry_waits=[])
    p.barrier()
    if debug:
        dbg = p.sem("dbg")
        p.dma("sync", dr["dbg_x0"], xdram, incs=[(dbg, 16)])
        p.emit("sync", lambda e: e.nop(), waits=[(dbg, p.sem("dbg").now())])
        p.barrier()
    if upto <= 3:
        p.replay()
        return nc
    dx = p.sem("dx")
    for c in range(DC):
        p.dma("sync", t["X"][:, c * N:(c + 1) * N], xdram[c * 128:(c + 1) * 128, :],
              incs=[(dx, 16)])
    p.emit("sync", lambda e: e.nop(), waits=[(dx, dx.now())])
    p.barrier()
    p.dma("gpsimd", t["mband_b"], dr["mband"], incs=[(cst, 16)])
    p.dma("gpsimd", t["mleft_b"], dr["mleft"], incs=[(cst, 16)])
    p.dma("gpsimd", t["mright_b"], dr["mright"], incs=[(cst, 16)])
    t["cst_all"] = cst.now()
    ln1_done = ph_ln(p, t, psum, "X", "XN", entry_waits=[])
    for c in range(DC):
        p.vop(lambda e, o=t["X"][:, c * N:(c + 1) * N], i=t["XN"][:, c * N:(c + 1) * N]:
              e.tensor_add(o, o, i), waits=[(vsem, ln1_done)], incs=[(vsem, 1)])
    xadd_done = vsem.now()
    cfg = cfg_l1_attn(p, nc, t, dr, entry_waits=[(vsem, xadd_done)])
    ph_attn(p, t, a, dr, psum, cfg)
    p.barrier()
    if upto <= 4:
        p.replay()
        return nc
    for c in range(DC):
        p.dma("sync", xdram[c * 128:(c + 1) * 128, :], t["X"][:, c * N:(c + 1) * N],
              incs=[(do, 16)])
    p.emit("sync", lambda e: e.nop(), waits=[(do, do.now())])
    p.barrier()
    if debug:
        dbg = p.sem("dbg")
        p.dma("sync", dr["dbg_x1"], xdram, incs=[(dbg, 16)])
        p.emit("sync", lambda e: e.nop(), waits=[(dbg, p.sem("dbg").now())])
        p.barrier()
    dw = p.sem("dw")
    lnw = {}
    for k in ("ln1w", "ln1b", "ln2w", "ln2b"):
        lnw[k] = nc.alloc_sbuf_tensor("w_" + k, [128, 8], F32)
        p.dma("sync", lnw[k][:], dr[f"l1_{k}"], incs=[(dw, 16)])
    lnw_ready = dw.now()
    ln1i_done = ph_ln(p, t, psum, "XN", "XNa", wb=(lnw["ln1w"], lnw["ln1b"]),
                      entry_waits=[(dw, lnw_ready)], src_bf16=True)
    cfg = cfg_msa(p, nc, t, dr, entry_waits=[(vsem, ln1i_done)])
    cfg["out_dram"] = xdram
    msa_done = ph_attn(p, t, a, dr, psum, cfg)
    p.barrier()
    if debug:
        dbg = p.sem("dbg")
        p.dma("sync", dr["dbg_x2"], xdram, incs=[(dbg, 16)])
        p.emit("sync", lambda e: e.nop(), waits=[(dbg, p.sem("dbg").now())])
        p.barrier()
    if upto <= 5:
        p.replay()
        return nc
    ln2_done = ph_ln(p, t, psum, "x1bf", "XN", wb=(lnw["ln2w"], lnw["ln2b"]),
                     entry_waits=[(dw, lnw_ready)], src_bf16=True)
    p.barrier()
    ph_moe(p, t, dr, psum, B["ffn_a0"], xdram, entry_waits=[])
    p.barrier()
    if upto <= 6:
        p.replay()
        return nc
    for c in range(DC):
        p.dma("sync", t["X"][:, c * N:(c + 1) * N], xdram[c * 128:(c + 1) * 128, :],
              incs=[(dx, 16)])
    p.emit("sync", lambda e: e.nop(), waits=[(dx, dx.now())])
    p.barrier()
    p.emit("vector", lambda e: e.nop(), incs=[(vsem, 1)])
    ph_store_out(p, t, dr, psum)
    p.replay()
    return nc


def _shard_inputs(inputs):
    x = np.asarray(inputs["L0_attn_x"])
    shards = [np.ascontiguousarray(x[i]) for i in range(8)]
    return shards


_CACHED = {}


def kernel(**inputs):
    rel = relayout_weights({k: np.asarray(v) for k, v in inputs.items()})
    consts = {"c_" + k: v for k, v in host_constants().items()}
    shards = _shard_inputs(inputs)
    nc = build_fused(rel)
    ims = []
    for i in range(8):
        im = {**consts, "x_in": shards[i]}
        for k in ALL_KEYS:
            im[k] = rel[k]
        ims.append(im)
    res = run_bass_kernel_spmd(nc, ims, core_ids=list(range(8)))
    return np.stack([res.results[i]["out"] for i in range(8)], axis=0)


def _np_softmax(x, axis=-1):
    m = x.max(axis=axis, keepdims=True)
    e = np.exp(x - m)
    return e / e.sum(axis=axis, keepdims=True)


def _np_ln(x, eps=1e-5):
    m = x.mean(-1, keepdims=True)
    v = x.var(-1, keepdims=True)
    return (x - m) / np.sqrt(v + eps)


def _np_l0attn(x, w):
    import math
    T, Dm, H, DHl = N, D, 16, 64
    xn = _np_ln(x)
    qkv = xn @ w["L0_attn_Wqkv"].T + w["L0_attn_bqkv"]
    qkv = qkv.reshape(T, H, 3 * DHl).transpose(1, 0, 2)
    q, k, vv = qkv[:, :, :64], qkv[:, :, 64:128], qkv[:, :, 128:]
    pos = np.arange(T, dtype=np.float32)
    ang = pos[:, None] / 10000.0 ** (np.arange(0, DHl, 2, dtype=np.float32) / DHl)
    rope = np.stack([np.cos(ang), np.sin(ang)], -1).reshape(T, DHl).astype(np.float32)
    rot = lambda z: np.stack([-z[..., 1::2], z[..., 0::2]], -1).reshape(z.shape)
    q = q * rope + rot(q) * (1 - rope)
    k = k * rope + rot(k) * (1 - rope)
    att = np.einsum("htd,hsd->hts", q, k) / math.sqrt(DHl)
    causal = np.tril(np.ones((T, T), dtype=bool))
    att = np.where(causal, att, -np.inf)
    att = _np_softmax(att, -1)
    y = np.einsum("hts,hsd->htd", att, vv)
    y = y.transpose(1, 0, 2).reshape(T, Dm)
    return x + (y @ w["L0_attn_Wproj"].T + w["L0_attn_bproj"])


def _build_spill_test():
    nc = bass.Bass("TRN2", target_bir_lowering=False)
    dr = {}
    dr["x_in"] = nc.dram_tensor("x_in", [N, D], F32, kind="ExternalInput").ap()
    dr["xdram"] = nc.dram_tensor("xdram", [D, N], F32, kind="ExternalOutput").ap()
    dr["xnd"] = nc.dram_tensor("xnd", [128, DC * N], BF16, kind="ExternalOutput").ap()
    for k, v in host_constants().items():
        dr[k] = nc.dram_tensor("c_" + k, list(v.shape), F32, kind="ExternalInput").ap()
    p = P(nc)
    t = alloc_common(nc, want_rope=False, want_rel=False)
    psum = _alloc_psum(nc)
    load_consts(p, t, dr)
    ph_load_x(p, t, dr, psum)
    x_ready = [(p.sem("v"), p.sem("v").now())]
    ln_done = ph_ln(p, t, psum, "X", "XN", entry_waits=x_ready)
    do = p.sem("do")
    for c in range(DC):
        p.dma("sync", dr["xdram"][c * 128:(c + 1) * 128, :], t["X"][:, c * N:(c + 1) * N],
              waits=x_ready, incs=[(do, 16)])
    p.dma("sync", dr["xnd"], t["XN"][:], waits=[(p.sem("v"), ln_done)], incs=[(do, 16)])
    p.emit("sync", lambda e: e.nop(), waits=[(do, do.now())])
    p.replay()
    return nc


def _build_l0ffn_test(weights):
    nc = bass.Bass("TRN2", target_bir_lowering=False)
    dr = {}
    dr["xdram_in"] = nc.dram_tensor("xdram_in", [D, N], F32, kind="ExternalInput").ap()
    dr["xnd"] = nc.dram_tensor("xnd", [128, DC * N], BF16, kind="ExternalInput").ap()
    xdram = nc.dram_tensor("xdram", [D, N], F32, kind="ExternalOutput").ap()
    for k in ("l0_w1T", "l0_w2T", "l0_h1T", "l0_h2T", "l0_w3T"):
        dr[k] = nc.dram_tensor(k, list(weights[k].shape), F32, kind="ExternalInput").ap()
    for k in ("l0_w1s", "l0_w2s", "l0_h1s", "l0_h2s", "l0_w3s"):
        dr[k] = nc.dram_tensor(k, list(weights[k].shape), F32, kind="ExternalInput").ap()
    p = P(nc)
    t = {"XN": nc.alloc_sbuf_tensor("XN", [128, DC * N], BF16)}
    psum = _alloc_psum(nc)
    dx, dcp = p.sem("dx"), p.sem("dcp")
    p.dma("sync", t["XN"][:], dr["xnd"], incs=[(dx, 16)])
    p.dma("sync", xdram, dr["xdram_in"], incs=[(dcp, 16)])
    ph_l0_ffn(p, t, dr, psum, xdram,
              entry_waits=[(dx, 16), (dcp, 16)])
    p.replay()
    return nc


def _test_l0ffn():
    import scipy.special as sp
    w = dict(np.load("/tmp/ref_inputs_0.npz"))
    rel = relayout_weights(w)
    x = np.ascontiguousarray(w["L0_attn_x"][0])
    ncA = _build_spill_test()
    im = {"x_in": x}
    im.update({"c_" + k: v for k, v in host_constants().items()})
    rA = run_bass_kernel_spmd(ncA, [im], core_ids=[0]).results[0]
    ncB = _build_l0ffn_test(rel)
    imB = {"xdram_in": rA["xdram"], "xnd": rA["xnd"]}
    for k in ("l0_w1T", "l0_w2T", "l0_h1T", "l0_h2T", "l0_w3T",
              "l0_w1s", "l0_w2s", "l0_h1s", "l0_h2s", "l0_w3s"):
        imB[k] = rel[k]
    rB = run_bass_kernel_spmd(ncB, [imB], core_ids=[0]).results[0]
    out_fm = rB["xdram"]
    xd = x.astype(np.float64)
    xn = _np_ln(xd)
    tl = lambda z, wk, sk: (z @ w[wk].astype(np.float64).T) * w[sk].astype(np.float64)
    h = (tl(xn, "L0_ffn_w1_w", "L0_ffn_w1_s") / (1 + np.exp(-tl(xn, "L0_ffn_w1_w", "L0_ffn_w1_s")))) \
        * tl(xn, "L0_ffn_w2_w", "L0_ffn_w2_s")
    ge = lambda z: 0.5 * z * (1 + sp.erf(z / np.sqrt(2)))
    h = ge(tl(h, "L0_ffn_h1_w", "L0_ffn_h1_s"))
    h = ge(tl(h, "L0_ffn_h2_w", "L0_ffn_h2_s"))
    ref = (xd + tl(h, "L0_ffn_w3_w", "L0_ffn_w3_s")).astype(np.float32)
    ref_fm = np.ascontiguousarray(ref.T)
    err = np.abs(out_fm - ref_fm)
    print("l0ffn max err:", err.max(), "rel l2:",
          np.sqrt((err ** 2).mean() / (ref_fm ** 2).mean()))


def _np_msa_moe(x, xn1, w):
    T, Dm, H, DK, E = N, D, 16, 64, 8
    lnw = lambda z, wk, bk: _np_ln(z) * w[wk] + w[bk]
    xa = lnw(xn1, "L1_ffn_ln1_w", "L1_ffn_ln1_b")
    qkv = (xa @ w["L1_ffn_w_qkv"]).reshape(T, 3, H, DK)
    q = qkv[:, 0].transpose(1, 0, 2)
    k = qkv[:, 1].transpose(1, 0, 2)
    v = qkv[:, 2].transpose(1, 0, 2)
    att = _np_softmax(np.einsum("hnd,hmd->hnm", q, k) / np.sqrt(DK), -1)
    y = np.einsum("hnm,hmd->hnd", att, v).transpose(1, 0, 2).reshape(T, Dm)
    msa = y @ w["L1_ffn_w_proj"]
    x1 = xn1 + msa
    x2 = lnw(x1, "L1_ffn_ln2_w", "L1_ffn_ln2_b")
    logits = x2 @ w["L1_ffn_gate_w"] + w["L1_ffn_gate_b"]
    sel = logits.argmax(-1)
    h = np.einsum("td,edh->teh", x2, w["L1_ffn_w1"])
    h = 0.5 * h * (1 + np.tanh(0.7978845608028654 * (h + 0.044715 * h ** 3)))
    y2 = np.einsum("teh,ehd->ted", h, w["L1_ffn_w2"])
    moe = y2[np.arange(T), sel]
    return msa, moe


def _test_l1ffn():
    import ml_dtypes
    w = dict(np.load("/tmp/ref_inputs_0.npz"))
    rel = relayout_weights(w)
    consts = {"c_" + k: v for k, v in host_constants().items()}
    x = np.ascontiguousarray(w["L0_attn_x"][0]).astype(np.float32)
    xn1 = _np_ln(x.astype(np.float64)).astype(np.float32)
    xn1_bf = xn1.astype(ml_dtypes.bfloat16)
    xnd1 = np.zeros((128, DC * N), dtype=ml_dtypes.bfloat16)
    for c in range(DC):
        xnd1[:, c * N:(c + 1) * N] = xn1_bf[:, c * 128:(c + 1) * 128].T
    xfm = np.ascontiguousarray(x.T)
    ncD = build_launch_D(rel)
    imD = {**consts, "xdram_in": xfm, "xnd1": xnd1}
    for k in MSA_KEYS:
        imD[k] = rel[k]
    rD = run_bass_kernel_spmd(ncD, [imD], core_ids=[0]).results[0]
    ncE = build_launch_E(rel)
    imE = {**consts, "xdram_in": rD["xdram"], "xn2d": rD["xn2d"]}
    for k in MOE_KEYS:
        imE[k] = rel[k]
    rE = run_bass_kernel_spmd(ncE, [imE], core_ids=[0]).results[0]
    out = rE["out"]
    wd = {k: v.astype(np.float64) for k, v in w.items()}
    msa, moe = _np_msa_moe(x.astype(np.float64),
                           xn1_bf.astype(np.float64), wd)
    refD = (x.astype(np.float64) + msa).astype(np.float32).T
    errD = np.abs(rD["xdram"] - refD)
    print("launchD rel l2:", np.sqrt((errD ** 2).mean() / (refD ** 2).mean()))
    ref = (x.astype(np.float64) + msa + moe).astype(np.float32)
    err = np.abs(out - ref)
    print("l1ffn max err:", err.max(), "rel l2:",
          np.sqrt((err ** 2).mean() / (ref ** 2).mean()))


def _np_full_ref(x, w):
    import scipy.special as sp
    xd = x.astype(np.float64)
    att0 = _np_l0attn(xd, w) - xd
    xn0 = _np_ln(xd)
    tl = lambda z, wk, sk: (z @ w[wk].T) * w[sk]
    g1 = tl(xn0, "L0_ffn_w1_w", "L0_ffn_w1_s")
    h = (g1 / (1 + np.exp(-g1))) * tl(xn0, "L0_ffn_w2_w", "L0_ffn_w2_s")
    ge = lambda z: 0.5 * z * (1 + sp.erf(z / np.sqrt(2)))
    h = ge(tl(h, "L0_ffn_h1_w", "L0_ffn_h1_s"))
    h = ge(tl(h, "L0_ffn_h2_w", "L0_ffn_h2_s"))
    ffn0 = tl(h, "L0_ffn_w3_w", "L0_ffn_w3_s")
    x1 = xd + att0 + ffn0
    att1 = _np_l1attn(x1, w) - x1
    xn1 = _np_ln(x1)
    msa, moe = _np_msa_moe(x1, xn1, w)
    return (x1 + att1 + xn1 + msa + moe).astype(np.float32)


def _test_fused():
    w = dict(np.load("/tmp/ref_inputs_0.npz"))
    rel = relayout_weights(w)
    consts = {"c_" + k: v for k, v in host_constants().items()}
    x = np.ascontiguousarray(w["L0_attn_x"][0]).astype(np.float32)
    nc = build_fused(rel)
    im = {**consts, "x_in": x}
    for k in ALL_KEYS:
        im[k] = rel[k]
    res = run_bass_kernel_spmd(nc, [im], core_ids=[0])
    out = res.results[0]["out"]
    wd = {k: v.astype(np.float64) for k, v in w.items()}
    ref = _np_full_ref(x, wd)
    err = np.abs(out - ref)
    print("fused max err:", err.max(), "rel l2:",
          np.sqrt((err ** 2).mean() / (ref ** 2).mean()))


def _test_l0attn():
    w = dict(np.load("/tmp/ref_inputs_0.npz"))
    rel = relayout_weights(w)
    x = np.ascontiguousarray(w["L0_attn_x"][0])
    nc = _build_l0attn_test(rel)
    im = {"x_in": x}
    im.update({"c_" + k: v for k, v in host_constants().items()})
    for k in ("l0_WqkvT", "l0_bq_pp", "l0_bk_pp", "l0_bv_pp", "l0_WprojT", "l0_bproj"):
        im[k] = rel[k]
    res = run_bass_kernel_spmd(nc, [im], core_ids=[0])
    out = res.results[0]["out"]
    ref = _np_l0attn(x.astype(np.float64), {k: v.astype(np.float64) for k, v in w.items()}).astype(np.float32)
    err = np.abs(out - ref)
    print("l0attn max err:", err.max(), "rel l2:",
          np.sqrt((err ** 2).mean() / (ref ** 2).mean()))


def _run_idln_test():
    rng = np.random.default_rng(0)
    x = rng.standard_normal((N, D)).astype(np.float32)
    nc = _build_identity_test(with_ln=True)
    im = {"x_in": x}
    im.update({"c_" + k: v for k, v in host_constants().items()})
    res = run_bass_kernel_spmd(nc, [im], core_ids=[0])
    out = res.results[0]["out"]
    print("roundtrip err:", np.abs(out - x).max())
    xn = res.results[0]["xn_out"].astype(np.float32)
    m = x.mean(axis=1, keepdims=True)
    v = x.var(axis=1, keepdims=True)
    ref = (x - m) / np.sqrt(v + 1e-5)
    ref_fm = np.zeros((128, DC * N), dtype=np.float32)
    for c in range(DC):
        ref_fm[:, c * N:(c + 1) * N] = ref[:, c * 128:(c + 1) * 128].T
    err = np.abs(xn - ref_fm)
    print("ln max err:", err.max(), "rel l2:",
          np.sqrt((err ** 2).mean() / (ref_fm ** 2).mean()))


if __name__ == "__main__":
    import sys as _sys
    _arg = _sys.argv[1] if len(_sys.argv) > 1 else ""
    if _arg == "l0attn":
        _test_l0attn()
    elif _arg == "l0ffn":
        _test_l0ffn()
    elif _arg == "l1attn":
        _test_l1attn()
    elif _arg == "l1ffn":
        _test_l1ffn()
    elif _arg == "fused":
        _test_fused()
    else:
        _run_idln_test()




# version 05 (final)
# speedup vs baseline: 1.0613x; 1.0613x over previous
import numpy as np
import concourse.bass as bass
import concourse.mybir as mybir
from concourse.bass_utils import run_bass_kernel_spmd

F32 = mybir.dt.float32
BF16 = mybir.dt.bfloat16
AF = mybir.ActivationFunctionType
ALU = mybir.AluOpType
AX = mybir.AxisListType

N = 2048
D = 1024
DC = 8
NH = 16
DH = 64
HGn = 8
HID = 4096
HCn = 32
NB = 4
NEG = -1.0e30

ENGINES = ("sync", "tensor", "vector", "scalar", "gpsimd")


class Sem:
    def __init__(self, nc, name):
        self.h = nc.alloc_semaphore(name)
        self.c = 0

    def now(self):
        return self.c


class P:

    def __init__(self, nc):
        self.nc = nc
        self.q = {e: [] for e in ENGINES}
        self.sems = {}

    def sem(self, name):
        if name not in self.sems:
            self.sems[name] = Sem(self.nc, name)
        return self.sems[name]

    def alloc(self, name, shape, dtype):
        if not hasattr(self, "bind"):
            self.bind = {}
        if name in self.bind:
            return self.bind[name]
        h = self.nc.alloc_sbuf_tensor(name, shape, dtype)
        return h[tuple(slice(None) for _ in shape)]

    def barrier(self, name=""):
        bar = self.sem("bar")
        if not hasattr(self, "bar_t"):
            self.bar_t = {e: self.nc.alloc_sbuf_tensor(f"bar_{e}", [128, 1], F32)
                          for e in ("vector", "scalar", "gpsimd")}
        for e in ENGINES:
            if e in ("vector", "gpsimd"):
                self.emit(e, lambda eng, o=self.bar_t[e]: eng.memset(o[:], 0.0),
                          incs=[(bar, 1)])
            elif e == "scalar":
                self.emit(e, lambda eng, o=self.bar_t[e]: eng.memzero(o[:]),
                          incs=[(bar, 1)])
            else:
                self.emit(e, lambda eng: eng.nop(), incs=[(bar, 1)])
        tgt = bar.now()
        dws = [self.sems[k] for k in ("dw", "dx", "dq", "do", "dacc", "dcp",
                                      "dbg", "cst")
               if k in self.sems]
        for e in ENGINES:
            self.emit(e, lambda eng: eng.nop(),
                      waits=[(bar, tgt)] + [(s, s.now()) for s in dws])

    def emit(self, eng, fn, waits=(), incs=()):
        wl = [(s.h, int(v)) for (s, v) in waits]
        il = []
        for (s, n) in incs:
            s.c += n
            il.append((s.h, n))
        self.q[eng].append((fn, wl, il))

    def replay(self):
        def runner(lst):
            def run(eng):
                for (fn, wl, il) in lst:
                    for (h, v) in wl:
                        eng.wait_ge(h, v)
                    ins = fn(eng)
                    for (h, n) in il:
                        ins.then_inc(h, n)
            return run

        with self.nc.Block() as blk:
            blk.sync(runner(self.q["sync"]))
            blk.tensor(runner(self.q["tensor"]))
            blk.vector(runner(self.q["vector"]))
            blk.scalar(runner(self.q["scalar"]))
            blk.gpsimd(runner(self.q["gpsimd"]))

    def dma(self, queue, out, in_, waits=(), incs=(), **kw):
        self.emit(queue, lambda e, o=out, i=in_, k=kw: e.dma_start(o, i, **k),
                  waits=waits, incs=incs)

    def mm(self, out, lhsT, rhs, start, stop, waits=(), incs=()):
        self.emit("tensor",
                  lambda e, o=out, l=lhsT, r=rhs, s1=start, s2=stop: e.matmul(o, l, r, start=s1, stop=s2),
                  waits=waits, incs=incs)

    def tr(self, out, in_, ident, waits=(), incs=()):
        self.emit("tensor", lambda e, o=out, i=in_, d=ident: e.transpose(o, i, d),
                  waits=waits, incs=incs)

    def act(self, out, in_, func, bias=0.0, scale=1.0, accum_out=None, waits=(), incs=()):
        self.emit("scalar",
                  lambda e, o=out, i=in_, f=func, b=bias, s=scale, a=accum_out:
                  e.activation(o, i, f, bias=b, scale=s, accum_out=a),
                  waits=waits, incs=incs)

    def vop(self, fn, waits=(), incs=()):
        self.emit("vector", fn, waits=waits, incs=incs)

    def pop(self, fn, waits=(), incs=()):
        self.emit("gpsimd", fn, waits=waits, incs=incs)


def host_constants():
    c = {}
    c["ident"] = np.eye(128, dtype=np.float32)
    pos = np.arange(N, dtype=np.float32)
    ang = pos[:, None] / (10000.0 ** (np.arange(0, DH, 2, dtype=np.float32) / DH))
    rope = np.stack([np.cos(ang), np.sin(ang)], axis=-1).reshape(N, DH).astype(np.float32)
    a_fm = rope.T
    c["rope_a"] = np.concatenate([a_fm, a_fm], axis=0).astype(np.float32)
    c["rope_b"] = (1.0 - c["rope_a"]).astype(np.float32)
    prot = np.zeros((128, 128), dtype=np.float32)
    for i in range(64):
        prot[2 * i + 1, 2 * i] = -1.0
        prot[2 * i, 2 * i + 1] = 1.0
    c["prot"] = prot
    tri = np.where(np.arange(128)[None, :] <= np.arange(128)[:, None], 0.0, NEG)
    c["tri"] = tri.astype(np.float32)
    m = np.arange(576)[None, :] - np.arange(128)[:, None] - 96
    c["mband"] = ((m >= 0) & (m <= 64)).astype(np.float32)
    c["mleft"] = (m < 0).astype(np.float32)
    c["mright"] = (m > 64).astype(np.float32)
    c["ones_1x128"] = np.ones((1, 128), dtype=np.float32)
    c["ones_col"] = np.ones((128, 1), dtype=np.float32)
    sel = np.zeros((8, 8 * 128), dtype=np.float32)
    for e in range(8):
        sel[e, e * 128:(e + 1) * 128] = 1.0
    c["sel8"] = sel
    return c


def relayout_weights(w):
    r = {}
    pp = lambda v: np.ascontiguousarray(v.reshape(-1, 128).T)
    perm = []
    for hg in range(HGn):
        for s in range(3):
            for h2 in range(2):
                o0 = (2 * hg + h2) * 192 + s * 64
                perm.extend(range(o0, o0 + 64))
    r["l0_WqkvT"] = np.ascontiguousarray(w["L0_attn_Wqkv"].T[:, perm])
    r["l0_WprojT"] = np.ascontiguousarray(w["L0_attn_Wproj"].T)
    r["l0_bproj"] = pp(w["L0_attn_bproj"])
    bqkv = w["L0_attn_bqkv"]
    for si, nm in enumerate(("bq", "bk", "bv")):
        arr = np.zeros((128, 8), dtype=np.float32)
        for h in range(NH):
            arr[(h % 2) * 64:(h % 2) * 64 + 64, h // 2] = bqkv[h * 192 + si * 64: h * 192 + si * 64 + 64]
        r[f"l0_{nm}_pp"] = arr
    for k in ("w1", "w2", "h1", "h2", "w3"):
        r[f"l0_{k}T"] = np.ascontiguousarray(w[f"L0_ffn_{k}_w"].T)
        r[f"l0_{k}s"] = pp(w[f"L0_ffn_{k}_s"])
    for k in ("Wq", "Wk", "Wv", "Wo"):
        r[f"l1_{k}"] = np.ascontiguousarray(w[f"L1_attn_{k}"])
    for k in ("bq", "bk", "bv", "bo"):
        r[f"l1_{k}"] = pp(w[f"L1_attn_{k}"])
    rt = np.ascontiguousarray(w["L1_attn_rel_emb"][::-1, :].T)
    r["l1_relT"] = np.concatenate([rt, rt], axis=0)
    r["l1_ln1w"] = pp(w["L1_ffn_ln1_w"]); r["l1_ln1b"] = pp(w["L1_ffn_ln1_b"])
    r["l1_ln2w"] = pp(w["L1_ffn_ln2_w"]); r["l1_ln2b"] = pp(w["L1_ffn_ln2_b"])
    r["l1_wqkv"] = np.ascontiguousarray(w["L1_ffn_w_qkv"])
    r["l1_wproj"] = np.ascontiguousarray(w["L1_ffn_w_proj"])
    r["l1_gatew"] = np.ascontiguousarray(w["L1_ffn_gate_w"])
    r["l1_gateb"] = np.ascontiguousarray(w["L1_ffn_gate_b"].reshape(1, 8))
    r["l1_moe_w1"] = np.ascontiguousarray(w["L1_ffn_w1"])
    r["l1_moe_w2"] = np.ascontiguousarray(w["L1_ffn_w2"])
    return r


def alloc_common(nc, want_consts=True, want_rope=True, want_rel=True, want_io=True,
                 want_x=True, want_ln=True):
    t = {}
    if want_x:
        t["X"] = nc.alloc_sbuf_tensor("X", [128, DC * N], F32)
    t["XN"] = nc.alloc_sbuf_tensor("XN", [128, DC * N], BF16)
    if want_consts:
        t["ident_f"] = nc.alloc_sbuf_tensor("ident_f", [128, 128], F32)
        t["ident_b"] = nc.alloc_sbuf_tensor("ident_b", [128, 128], BF16)
        t["ones_1x128_b"] = nc.alloc_sbuf_tensor("ones_1x128_b", [1, 128], BF16)
        t["ones_1x128_f"] = nc.alloc_sbuf_tensor("ones_1x128_f", [1, 128], F32)
        t["ones_col_f"] = nc.alloc_sbuf_tensor("ones_col_f", [128, 1], F32)
        t["ones_col_b"] = nc.alloc_sbuf_tensor("ones_col_b", [128, 1], BF16)
        t["tri_f"] = nc.alloc_sbuf_tensor("tri_f", [128, 128], F32)
        t["prot_b"] = nc.alloc_sbuf_tensor("prot_b", [128, 128], BF16)
        if want_rope:
            t["rope_a"] = nc.alloc_sbuf_tensor("rope_a", [128, N], BF16)
            t["rope_b"] = nc.alloc_sbuf_tensor("rope_b", [128, N], BF16)
        for i in range(4):
            t[f"s512_{i}"] = nc.alloc_sbuf_tensor(f"s512_{i}", [128, 512], F32)
        t["rtmp0"] = t["s512_2"]
        t["rtmp1"] = t["s512_3"]
        if want_ln:
            for i, k in enumerate(("m", "tA", "tB", "a", "mr")):
                t[f"row_{k}"] = nc.alloc_sbuf_tensor(f"row_{k}", [1, 512], F32)
        if want_io:
            t["xio0"] = nc.alloc_sbuf_tensor("xio0", [128, D], F32)
            t["xio1"] = nc.alloc_sbuf_tensor("xio1", [128, D], F32)
        if want_rel:
            t["mband_b"] = nc.alloc_sbuf_tensor("mband_b", [128, 576], BF16)
            t["mleft_b"] = nc.alloc_sbuf_tensor("mleft_b", [128, 576], BF16)
            t["mright_b"] = nc.alloc_sbuf_tensor("mright_b", [128, 576], BF16)
    return t


def load_consts(p, t, dr):
    cst = p.sem("cst")
    p.dma("sync", t["ident_f"][:], dr["ident"], incs=[(cst, 16)])
    p.dma("gpsimd", t["ident_b"][:], dr["ident"], incs=[(cst, 16)])
    p.dma("gpsimd", t["ones_1x128_b"][:], dr["ones_1x128"], incs=[(cst, 16)])
    p.dma("sync", t["ones_1x128_f"][:], dr["ones_1x128"], incs=[(cst, 16)])
    p.dma("sync", t["ones_col_f"][:], dr["ones_col"], incs=[(cst, 16)])
    p.dma("gpsimd", t["ones_col_b"][:], dr["ones_col"], incs=[(cst, 16)])
    p.dma("sync", t["tri_f"][:], dr["tri"], incs=[(cst, 16)])
    p.dma("gpsimd", t["prot_b"][:], dr["prot"], incs=[(cst, 16)])
    if "rope_a" in t:
        p.dma("gpsimd", t["rope_a"][:], dr["rope_a"], incs=[(cst, 16)])
        p.dma("gpsimd", t["rope_b"][:], dr["rope_b"], incs=[(cst, 16)])
    if "mband_b" in t:
        p.dma("gpsimd", t["mband_b"][:], dr["mband"], incs=[(cst, 16)])
        p.dma("gpsimd", t["mleft_b"][:], dr["mleft"], incs=[(cst, 16)])
        p.dma("gpsimd", t["mright_b"][:], dr["mright"], incs=[(cst, 16)])
    t["cst_all"] = cst.now()


def ph_load_x(p, t, dr, psum):
    nc = p.nc
    cst, dx, tsem, vsem = p.sem("cst"), p.sem("dx"), p.sem("t"), p.sem("v")
    xtm = [t["xio0"], t["xio1"]]
    ps = psum["s"]
    for tt in range(16):
        buf = xtm[tt % 2]
        w = [(tsem, tsem.now())] if tt >= 2 else []
        p.dma("sync", buf[:], dr["x_in"][tt * 128:(tt + 1) * 128, :],
              waits=w, incs=[(dx, 16)])
        for half in range(2):
            pb = ps[:, half * 512:(half + 1) * 512]
            for ci in range(4):
                c = half * 4 + ci
                w = [(dx, dx.now()), (cst, t.get('cst_all', 16))]
                if tt >= 1 or half >= 1:
                    w.append((vsem, vsem.now()))
                p.tr(pb[:, ci * 128:(ci + 1) * 128], buf[:, c * 128:(c + 1) * 128],
                     t["ident_f"][:], waits=w,
                     incs=[(tsem, 1)] if ci == 3 else [])
            for ci in range(4):
                c = half * 4 + ci
                p.vop(lambda e, dst=t["X"][:, c * N + tt * 128: c * N + tt * 128 + 128],
                      src=pb[:, ci * 128:(ci + 1) * 128]: e.tensor_copy(dst, src),
                      waits=[(tsem, tsem.now())],
                      incs=[(vsem, 1)] if ci == 3 else [])


def ph_store_out(p, t, dr, psum, src_key="X"):
    nc = p.nc
    tsem, vsem, do = p.sem("t"), p.sem("v"), p.sem("do")
    xout = [t["xio0"], t["xio1"]]
    ps = psum["s"]
    store_counts = {}
    for tt in range(16):
        buf = xout[tt % 2]
        for half in range(2):
            pb = ps[:, half * 512:(half + 1) * 512]
            for ci in range(4):
                c = half * 4 + ci
                w = [(vsem, vsem.now())]
                p.tr(pb[:, ci * 128:(ci + 1) * 128],
                     t[src_key][:, c * N + tt * 128: c * N + tt * 128 + 128],
                     t["ident_f"][:], waits=w,
                     incs=[(tsem, 1)] if ci == 3 else [])
            for ci in range(4):
                c = half * 4 + ci
                w = [(tsem, tsem.now())]
                if tt >= 2 and half == 0 and ci == 0:
                    w.append((do, store_counts[tt - 2]))
                p.vop(lambda e, dst=buf[:, c * 128:(c + 1) * 128],
                      src=pb[:, ci * 128:(ci + 1) * 128]: e.tensor_copy(dst, src),
                      waits=w, incs=[(vsem, 1)] if ci == 3 else [])
        p.dma("sync", dr["out"][tt * 128:(tt + 1) * 128, :], buf[:],
              waits=[(vsem, vsem.now())], incs=[(do, 16)])
        store_counts[tt] = do.now()
    p.emit("sync", lambda e: e.nop(), waits=[(do, do.now())])


def ph_ln(p, t, psum, src_key, dst_key, wb=None, entry_waits=(), src_bf16=False):
    nc = p.nc
    tsem, vsem, ssem = p.sem("t"), p.sem("v"), p.sem("s")
    sq = [t["s512_0"], t["s512_1"]]
    rows = {k: t[f"row_{k}"] for k in ("m", "tA", "tB", "a", "mr")}
    rows["t1"] = rows["tA"]; rows["v1"] = rows["tB"]
    rows["v2"] = rows["tB"]; rows["sd"] = rows["tA"]
    tmp = [t["s512_2"], t["s512_3"]]
    ps_a, ps_b, ps_c, ps_d = psum["a"], psum["b"], psum["c"], psum["d_ln"]
    src, dst = t[src_key], t[dst_key]
    ones = t["ones_col_b"] if src_bf16 else t["ones_col_f"]
    cst = p.sem("cst")

    prev_apply_done = None
    sumsq_mm_done = {}
    for n in range(NB):
        sl = lambda c, n=n: slice(c * N + n * 512, c * N + n * 512 + 512)
        sq_counts = {}

        def emit_square(c, n=n, sq_counts=sq_counts):
            g = n * DC + c
            w = list(entry_waits) + [(cst, t.get('cst_all', 16))]
            if g - 2 in sumsq_mm_done:
                w.append((tsem, sumsq_mm_done[g - 2]))
            p.act(sq[c % 2][:], src[:, sl(c)], AF.Square, waits=w, incs=[(ssem, 1)])
            sq_counts[c] = ssem.now()

        emit_square(0)
        emit_square(1)
        for c in range(DC):
            w = list(entry_waits) + [(cst, t.get('cst_all', 16))]
            if c == 0 and prev_apply_done is not None:
                w.append((vsem, prev_apply_done))
            p.mm(ps_a[0:1, 0:512], ones[:], src[:, sl(c)], c == 0, c == DC - 1,
                 waits=w, incs=[(tsem, 1)] if c == DC - 1 else [])
        sum_done = tsem.now()
        for c in range(DC):
            w = [(ssem, sq_counts[c])]
            if c == 0 and prev_apply_done is not None:
                w.append((vsem, prev_apply_done))
            p.mm(ps_b[0:1, 0:512], t["ones_col_f"][:], sq[c % 2][:], c == 0, c == DC - 1,
                 waits=w, incs=[(tsem, 1)])
            sumsq_mm_done[n * DC + c] = tsem.now()
            if c + 2 < DC:
                emit_square(c + 2)
        sumsq_done = tsem.now()
        p.vop(lambda e, o=rows["m"], i=ps_a[0:1, 0:512]:
              e.tensor_scalar(o[:], i, 1.0 / D, None, ALU.mult),
              waits=[(tsem, sum_done)], incs=[(vsem, 1)])
        p.vop(lambda e, o=rows["t1"], i=rows["m"]: e.tensor_mul(o[:], i[:], i[:]))
        p.vop(lambda e, o=rows["v1"], i=ps_b[0:1, 0:512]:
              e.tensor_scalar(o[:], i, 1.0 / D, 1e-5, ALU.mult, ALU.add),
              waits=[(tsem, sumsq_done)], incs=[(vsem, 1)])
        p.vop(lambda e, o=rows["v2"], a=rows["v1"], b=rows["t1"]:
              e.tensor_sub(o[:], a[:], b[:]), incs=[(vsem, 1)])
        vv = vsem.now()
        p.act(rows["sd"][:], rows["v2"][:], AF.Sqrt, waits=[(vsem, vv)], incs=[(ssem, 1)])
        sv = ssem.now()
        p.vop(lambda e, o=rows["a"], i=rows["sd"]: e.reciprocal(o[:], i[:]),
              waits=[(ssem, sv)], incs=[(vsem, 1)])
        p.vop(lambda e, o=rows["mr"], a=rows["m"], b=rows["a"]:
              e.tensor_mul(o[:], a[:], b[:]), incs=[(vsem, 1)])
        rows_done = vsem.now()
        w = [(vsem, rows_done)]
        if prev_apply_done is not None:
            w.append((vsem, prev_apply_done))
        p.mm(ps_c[:, 0:512], t["ones_1x128_f"][:], rows["a"][:], True, True,
             waits=w, incs=[(tsem, 1)])
        p.mm(ps_d[:, 0:512], t["ones_1x128_f"][:], rows["mr"][:], True, True,
             incs=[(tsem, 1)])
        bc_done = tsem.now()
        for c in range(DC):
            w = [(tsem, bc_done)]
            p.vop(lambda e, o=tmp[c % 2], i=src[:, sl(c)], b=ps_c[:, 0:512]:
                  e.tensor_mul(o[:], i, b), waits=w)
            if wb is None:
                p.vop(lambda e, o=dst[:, sl(c)], a=tmp[c % 2], b=ps_d[:, 0:512]:
                      e.tensor_sub(o, a[:], b),
                      incs=[(vsem, 1)] if c == DC - 1 else [])
            else:
                w_pp, b_pp = wb
                p.vop(lambda e, o=tmp[c % 2], a=tmp[c % 2], b=ps_d[:, 0:512]:
                      e.tensor_sub(o[:], a[:], b))
                p.vop(lambda e, o=dst[:, sl(c)], i=tmp[c % 2],
                      s1=w_pp[:, c:c + 1], s2=b_pp[:, c:c + 1]:
                      e.tensor_scalar(o, i[:], s1, s2, ALU.mult, ALU.add),
                      waits=list(entry_waits),
                      incs=[(vsem, 1)] if c == DC - 1 else [])
        prev_apply_done = vsem.now()
    return prev_apply_done


def alloc_attn(p, rel=False):
    a = {}
    for k in ("Q", "K", "V", "VT", "P", "PT"):
        a[k] = p.alloc(f"at_{k}", [128, N], BF16)
    a["CTX"] = p.alloc("at_CTX", [128, DC * N], BF16)
    a["negmax"] = p.alloc("at_negmax", [128, 1], F32)
    a["sums"] = p.alloc("at_sums", [128, 4], F32)
    a["rinv"] = p.alloc("at_rinv", [128, 1], F32)
    if rel:
        a["qrel"] = p.alloc("at_qrel", [128, 16 * 65], F32)
        a["qrelb"] = p.alloc("at_qrelb", [128, 16 * 65], BF16)
        a["win"] = p.alloc("at_win", [128, 8 * 384], BF16)
        a["bw"] = p.alloc("at_bw", [128, 384], BF16)
        a["bw2"] = p.alloc("at_bw2", [128, 384], BF16)
        a["sw"] = p.alloc("at_sw", [128, 384], F32)
        a["biasL"] = p.alloc("at_biasL", [128, 1], F32)
        a["biasR"] = p.alloc("at_biasR", [128, 1], F32)
        a["mrow"] = p.alloc("at_mrow", [128, 3], F32)
    return a


def _rel_zone(qt):
    kw = max(0, (qt - 1) * 128)
    zw = min(2048, (qt + 2) * 128) - kw
    q0 = qt * 128
    delta = q0 - 32 - kw
    return kw, zw, delta


REL_BUF = 50100


def ph_attn(p, t, a, dr, psum, cfg):
    nc = p.nc
    name = cfg["name"]
    tsem, vsem, ssem = p.sem("t"), p.sem("v"), p.sem("s")
    dw, dq, gsem = p.sem("dw"), p.sem("dq"), p.sem("g")
    causal, rel, rope = cfg["causal"], cfg["rel"], cfg["rope"]
    XNs = t[cfg["xn_key"]]
    Q, K, V, VT, P_, PT, CTX = (a["Q"], a["K"], a["V"], a["VT"], a["P"], a["PT"], a["CTX"])
    ps_s, ps_a, ps_b, ps_t, ps_d = psum["s"], psum["a"], psum["b"], psum["t"], psum["d"]
    entry = list(cfg.get("entry_waits", ()))
    cst = p.sem("cst")

    wpb = [p.alloc(f"{name}_wpb{i}", [128, DC * 128], BF16) for i in range(2)]

    if rel:
        relT = p.alloc(f"{name}_relT", [128, 65], BF16)
        p.dma("gpsimd", relT[:], dr["l1_relT"], incs=[(dw, 16)])
        relT_ready = dw.now()
        scratch = nc.dram_tensor(f"{name}_scr", [16 * REL_BUF], BF16, kind="Internal").ap()
        scr_read = {}
        win_used = {}

    pp_tiles = [ps_a, ps_b]
    evac_done_for_pp = {}
    prev_scores_exp = None
    prev_mask_v = None
    pt_evac_count = None
    ctx_evac_count = None
    qkv_mm_done = {}

    wq_depth = cfg.get("wq_depth", 2)
    for hg in range(HGn):
        war = []
        if hg >= wq_depth:
            war.append((tsem, qkv_mm_done[hg - wq_depth]))
        lhsT_fn = cfg["wq_dma"](p, hg, hg % 2, war)
        w_ready = dw.now()

        for oc3 in range(cfg["nchunks"]):
            for nb in range(NB):
                ppid = (oc3 * NB + nb) % 2
                pp = pp_tiles[ppid]
                for kc in range(DC):
                    w = []
                    if kc == 0:
                        w = entry + [(dw, w_ready)]
                        if ppid in evac_done_for_pp:
                            w.append((ssem, evac_done_for_pp[ppid]))
                    p.mm(pp[:, 0:512], lhsT_fn(oc3, kc),
                         XNs[:, kc * N + nb * 512: kc * N + nb * 512 + 512],
                         kc == 0, kc == DC - 1,
                         waits=w, incs=[(tsem, 1)] if kc == DC - 1 else [])
                mm_done = tsem.now()
                qkv_mm_done[hg] = mm_done
                for (r0, nr, dk, dr0, bias, scale) in cfg["evac_spec"](hg, oc3):
                    dst = a[dk][dr0:dr0 + nr, nb * 512:nb * 512 + 512]
                    b = bias if bias is not None else 0.0
                    p.act(dst, pp[r0:r0 + nr, 0:512], AF.Identity, bias=b, scale=scale,
                          waits=[(tsem, mm_done)] + cfg.get("bias_waits", []),
                          incs=[(ssem, 1)])
                evac_done_for_pp[ppid] = ssem.now()
        qkv_evac_done = ssem.now()

        if rope:
            rope_done_v = None
            rope_pp_free = {}
            for wkey in ("Q", "K"):
                wt = a[wkey]
                for nb in range(NB):
                    ppid = nb % 2
                    pp = pp_tiles[ppid]
                    sl = slice(nb * 512, nb * 512 + 512)
                    w = [(ssem, qkv_evac_done), (cst, t.get('cst_all', 16))]
                    if rope_done_v is not None:
                        w.append((vsem, rope_done_v))
                    if ppid in rope_pp_free:
                        w.append((vsem, rope_pp_free[ppid]))
                    elif ppid in evac_done_for_pp:
                        w.append((ssem, evac_done_for_pp[ppid]))
                    p.mm(pp[:, 0:512], t["prot_b"][:], wt[:, sl], True, True,
                         waits=w, incs=[(tsem, 1)])
                    rot_done = tsem.now()
                    p.vop(lambda e, o=t["rtmp0"], i=wt[:, sl], b=t["rope_a"][:, sl]:
                          e.tensor_mul(o[:], i, b), waits=[(ssem, qkv_evac_done)])
                    p.vop(lambda e, o=t["rtmp1"], i=pp[:, 0:512], b=t["rope_b"][:, sl]:
                          e.tensor_mul(o[:], i, b), waits=[(tsem, rot_done)],
                          incs=[(vsem, 1)])
                    rope_pp_free[ppid] = vsem.now()
                    p.vop(lambda e, o=wt[:, sl], x=t["rtmp0"], y=t["rtmp1"]:
                          e.tensor_add(o, x[:], y[:]), incs=[(vsem, 1)])
                    rope_done_v = vsem.now()
            evac_done_for_pp = {}
            qk_ready = (vsem, rope_done_v)
        else:
            qk_ready = (ssem, qkv_evac_done)

        vt_evac = None
        for kb in range(16):
            w = [(ssem, qkv_evac_done), (cst, t.get('cst_all', 16))]
            if kb % 8 == 0:
                if vt_evac is not None:
                    w.append((vsem, vt_evac))
                elif pt_evac_count is not None:
                    w.append((vsem, pt_evac_count))
            p.tr(ps_t[:, (kb % 8) * 128:(kb % 8) * 128 + 128],
                 V[:, kb * 128:kb * 128 + 128],
                 t["ident_b"][:], waits=w, incs=[(tsem, 1)] if kb % 8 == 7 else [])
            if kb % 8 == 7:
                p.vop(lambda e, o=VT[:, (kb - 7) * 128:(kb + 1) * 128], i=ps_t[:]:
                      e.tensor_copy(o, i[:]), waits=[(tsem, tsem.now())],
                      incs=[(vsem, 1)])
                vt_evac = vsem.now()
        vt_ready = vt_evac
        pt_evac_count = vt_evac

        for h2 in range(2):
            if rel:
                rel_read = {}
                scr_write = {}
                rviews = {}
                for qt in range(16):
                    kw, zw, delta = _rel_zone(qt)
                    ppid = qt % 2
                    pp = pp_tiles[ppid]
                    w = [(dw, relT_ready), (ssem, qkv_evac_done), qk_ready]
                    if ppid in evac_done_for_pp:
                        w.append((ssem, evac_done_for_pp[ppid]))
                    p.mm(pp[:, 0:65],
                         Q[h2 * 64:h2 * 64 + 64, qt * 128:qt * 128 + 128],
                         relT[h2 * 64:h2 * 64 + 64, :], True, True,
                         waits=w, incs=[(tsem, 1)])
                    mmq = tsem.now()
                    p.act(a["qrel"][:, qt * 65:qt * 65 + 65], pp[:, 0:65], AF.Copy,
                          waits=[(tsem, mmq)], incs=[(ssem, 1)])
                    p.act(a["qrelb"][:, qt * 65:qt * 65 + 65], pp[:, 0:65], AF.Copy,
                          waits=[(tsem, mmq)], incs=[(ssem, 1)])
                    evac_done_for_pp[ppid] = ssem.now()
                    qrel_ready = ssem.now()
                    base = qt * REL_BUF
                    H = 64
                    wv = scratch[base + H + delta: base + H + delta + 128 * 386] \
                        .rearrange("(p m) -> p m", m=386)[:, 0:65]
                    rviews[qt] = scratch[base + H: base + H + 128 * 385] \
                        .rearrange("(p m) -> p m", m=385)[:, 0:zw]
                    w = [(ssem, qrel_ready)]
                    if qt in scr_read:
                        w.append((dq, scr_read[qt]))
                    p.dma("sync", wv, a["qrelb"][:, qt * 65:qt * 65 + 65],
                          waits=w, incs=[(dq, 16)])
                    scr_write[qt] = dq.now()

                next_read = [0]

                def emit_win_read(stop_qt):
                    while next_read[0] <= min(stop_qt, 15):
                        q_ = next_read[0]
                        _, zw_, _ = _rel_zone(q_)
                        w = [(dq, scr_write[q_])]
                        if q_ % 8 in win_used:
                            w.append((gsem, win_used[q_ % 8]))
                        p.dma("sync",
                              a["win"][:, (q_ % 8) * 384: (q_ % 8) * 384 + zw_],
                              rviews[q_], waits=w, incs=[(dq, 16)])
                        scr_read[q_] = dq.now()
                        rel_read[q_] = dq.now()
                        next_read[0] += 1

            for qt in range(16):
                if rel:
                    emit_win_read(qt + 3)
                kmaxb = (qt + 1) if causal else 16
                kmax = kmaxb * 128
                q0 = qt * 128
                ngroups = (kmaxb + 3) // 4
                for g in range(ngroups):
                    ng = min(512, kmax - g * 512)
                    w = []
                    if g == 0:
                        w = [qk_ready]
                        if prev_scores_exp is not None:
                            w.append((ssem, prev_scores_exp))
                        if prev_mask_v is not None:
                            w.append((vsem, prev_mask_v))
                    p.mm(ps_s[:, g * 512:g * 512 + ng],
                         Q[h2 * 64:h2 * 64 + 64, q0:q0 + 128],
                         K[h2 * 64:h2 * 64 + 64, g * 512:g * 512 + ng],
                         True, True, waits=w,
                         incs=[(tsem, 1)] if g == ngroups - 1 else [])
                scores_done = tsem.now()

                if causal:
                    p.vop(lambda e, o=ps_s[:, q0:q0 + 128], m=t["tri_f"]:
                          e.tensor_add(o, o, m[:]),
                          waits=[(tsem, scores_done), (cst, t.get('cst_all', 16))], incs=[(vsem, 1)])
                    prev_mask_v = vsem.now()
                    w = [(vsem, prev_mask_v)]
                    if prev_scores_exp is not None:
                        w.append((ssem, prev_scores_exp))
                    p.vop(lambda e, o=a["negmax"], i=ps_s[:, 0:kmax]:
                          e.reduce_max(o[:], i, axis=AX.X, negate=True),
                          waits=w, incs=[(vsem, 1)])
                    mx = vsem.now()
                    p.act(P_[:, 0:kmax], ps_s[:, 0:kmax], AF.Exp,
                          bias=a["negmax"][:, 0:1], scale=1.0,
                          accum_out=a["sums"][:, 0:1],
                          waits=[(vsem, mx)], incs=[(ssem, 1)])
                    exp_done = ssem.now()
                    p.vop(lambda e, o=a["rinv"], i=a["sums"][:, 0:1]:
                          e.reciprocal(o[:], i), waits=[(ssem, exp_done)],
                          incs=[(vsem, 1)])
                elif not rel:
                    w = [(tsem, scores_done)]
                    if prev_scores_exp is not None:
                        w.append((ssem, prev_scores_exp))
                    p.vop(lambda e, o=a["negmax"], i=ps_s[:]:
                          e.reduce_max(o[:], i[:], axis=AX.X, negate=True),
                          waits=w, incs=[(vsem, 1)])
                    mx = vsem.now()
                    p.act(P_[:], ps_s[:], AF.Exp, bias=a["negmax"][:, 0:1], scale=1.0,
                          accum_out=a["sums"][:, 0:1],
                          waits=[(vsem, mx)], incs=[(ssem, 1)])
                    exp_done = ssem.now()
                    p.vop(lambda e, o=a["rinv"], i=a["sums"][:, 0:1]:
                          e.reciprocal(o[:], i), waits=[(ssem, exp_done)],
                          incs=[(vsem, 1)])
                else:
                    exp_done, wu = _rel_softmax(p, t, a, psum, qt, rel_read[qt],
                                                scores_done, prev_scores_exp)
                    win_used[qt % 8] = wu
                prev_scores_exp = exp_done
                rv = vsem.now()
                p.vop(lambda e, o=P_[:, 0:kmax], s=a["rinv"][:, 0:1]:
                      e.tensor_scalar(o, o, s, None, ALU.mult),
                      waits=[(vsem, rv)], incs=[(vsem, 1)])
                pnorm = vsem.now()

                for kb in range(kmaxb):
                    w = [(vsem, pnorm)]
                    if kb % 8 == 0 and pt_evac_count is not None:
                        w.append((vsem, pt_evac_count))
                    last_in_grp = (kb % 8 == 7) or (kb == kmaxb - 1)
                    p.tr(ps_t[:, (kb % 8) * 128:(kb % 8) * 128 + 128],
                         P_[:, kb * 128:kb * 128 + 128], t["ident_b"][:],
                         waits=w, incs=[(tsem, 1)] if last_in_grp else [])
                    if last_in_grp:
                        g8 = kb // 8
                        wdt = min(8, kmaxb - g8 * 8) * 128
                        p.vop(lambda e, o=PT[:, g8 * 1024:g8 * 1024 + wdt],
                              i=ps_t[:, 0:wdt]: e.tensor_copy(o, i),
                              waits=[(tsem, tsem.now())], incs=[(vsem, 1)])
                        pt_evac_count = vsem.now()
                pt_done = pt_evac_count

                for kb in range(kmaxb):
                    w = []
                    if kb == 0:
                        w = [(vsem, pt_done), (vsem, vt_ready)]
                        if ctx_evac_count is not None and qt % 4 == 0:
                            w.append((ssem, ctx_evac_count))
                    p.mm(ps_d[h2 * 64:h2 * 64 + 64,
                              (qt % 4) * 128:(qt % 4) * 128 + 128],
                         VT[:, kb * 128 + h2 * 64: kb * 128 + h2 * 64 + 64],
                         PT[:, kb * 128:kb * 128 + 128],
                         kb == 0, kb == kmaxb - 1,
                         waits=w, incs=[(tsem, 1)] if kb == kmaxb - 1 else [])
                if qt % 4 == 3:
                    p.act(CTX[h2 * 64:h2 * 64 + 64,
                              hg * N + (qt - 3) * 128: hg * N + (qt - 3) * 128 + 512],
                          ps_d[h2 * 64:h2 * 64 + 64, 0:512], AF.Copy,
                          waits=[(tsem, tsem.now())], incs=[(ssem, 1)])
                    ctx_evac_count = ssem.now()

    ctx_all_ready = ssem.now()
    evac_pp = {}
    pb = cfg.get("proj_bias")
    wp_mm_done = {}
    _pj_state = {}
    for oc in range(DC):
        war = [(tsem, wp_mm_done[oc - 2])] if oc >= 2 else [(ssem, ctx_all_ready)]
        p.dma("gpsimd", wpb[oc % 2].rearrange("p (kc o) -> p kc o", o=128),
              cfg["proj_w"][:, oc * 128:(oc + 1) * 128].rearrange("(kc p) o -> p kc o", p=128),
              waits=war, incs=[(dw, 16)])
        wp_ready = dw.now()
        for nb in range(NB):
            ppid = (oc * NB + nb) % 2
            pp = pp_tiles[ppid]
            for kc in range(DC):
                w = []
                if kc == 0:
                    w = [(ssem, ctx_all_ready), (dw, wp_ready)]
                    if ppid in evac_pp:
                        w.append((vsem, evac_pp[ppid]))
                    elif ppid in evac_done_for_pp:
                        w.append((ssem, evac_done_for_pp[ppid]))
                p.mm(pp[:, 0:512],
                     wpb[oc % 2][:, kc * 128: kc * 128 + 128],
                     CTX[:, kc * N + nb * 512: kc * N + nb * 512 + 512],
                     kc == 0, kc == DC - 1, waits=w,
                     incs=[(tsem, 1)] if kc == DC - 1 else [])
            mm_done = tsem.now()
            wp_mm_done[oc] = mm_done
            xsl = slice(oc * N + nb * 512, oc * N + nb * 512 + 512)
            od = cfg.get("out_dram")
            if od is not None:
                if cfg.get("out_extra") == "x1bf":
                    p.vop(lambda e, o=t["x1bf"][:, xsl], i=pp[:, 0:512],
                          x=t[cfg["x1_src"]][:, xsl]: e.tensor_add(o, i, x),
                          waits=[(tsem, mm_done)], incs=[(vsem, 1)])
                dacc = p.sem("dacc")
                yi = (oc * NB + nb) % 2
                ytile = t[f"s512_{yi}"]
                aw = [(tsem, mm_done)]
                if ("proj_dma", yi) in _pj_state:
                    aw.append((dacc, _pj_state[("proj_dma", yi)]))
                p.vop(lambda e, o=ytile, i=pp[:, 0:512]: e.tensor_copy(o[:], i),
                      waits=aw, incs=[(vsem, 1)])
                cw = cfg.get("copy_wait")
                dmaw = [cw] if cw is not None else []
                p.dma("gpsimd", od[oc * 128:(oc + 1) * 128, nb * 512:nb * 512 + 512],
                      ytile[:], waits=[(vsem, vsem.now())] + dmaw, incs=[(dacc, 16)],
                      accum_op=ALU.add)
                _pj_state[("proj_dma", yi)] = dacc.now()
                evac_pp[ppid] = vsem.now()
                continue
            if pb is not None:
                p.vop(lambda e, o=t["X"][:, xsl], i=pp[:, 0:512],
                      s=pb[:, oc:oc + 1]:
                      e.scalar_tensor_tensor(o, i, s, o, ALU.add, ALU.add),
                      waits=[(tsem, mm_done)] + cfg.get("bias_waits", []),
                      incs=[(vsem, 1)])
            else:
                p.vop(lambda e, o=t["X"][:, xsl], i=pp[:, 0:512]:
                      e.tensor_add(o, o, i), waits=[(tsem, mm_done)],
                      incs=[(vsem, 1)])
            if cfg.get("out_extra") == "x1bf":
                p.vop(lambda e, o=t["x1bf"][:, xsl], i=pp[:, 0:512],
                      x=t[cfg["x1_src"]][:, xsl]: e.tensor_add(o, i, x),
                      incs=[(vsem, 1)])
            evac_pp[ppid] = vsem.now()
    return vsem.now()


def _rel_softmax(p, t, a, psum, qt, win_ready, scores_done, prev_exp=None):
    tsem, vsem, ssem, gsem, dq = (p.sem("t"), p.sem("v"), p.sem("s"),
                                  p.sem("g"), p.sem("dq"))
    cst = p.sem("cst")
    kw, zw, delta = _rel_zone(qt)
    ps_s = psum["s"]
    winap = a["win"][:, (qt % 8) * 384: (qt % 8) * 384 + zw]
    qL = a["qrel"][:, qt * 65 + 0: qt * 65 + 1]
    qR = a["qrel"][:, qt * 65 + 64: qt * 65 + 65]
    moff = 96 - delta
    MB = t["mband_b"][:, moff:moff + zw]
    ML = t["mleft_b"][:, moff:moff + zw]
    MR = t["mright_b"][:, moff:moff + zw]
    qready = ssem.now()
    p.vop(lambda e, o=a["bw"][:, 0:zw], i=winap, m=MB:
          e.tensor_mul(o, i, m),
          waits=[(cst, t.get('cst_all', 16)), (dq, win_ready)], incs=[(vsem, 1)])
    p.vop(lambda e, o=a["bw2"][:, 0:zw], i=ML, s=qL:
          e.tensor_scalar(o, i, s, None, ALU.mult),
          waits=[(ssem, qready), (vsem, vsem.now())], incs=[(vsem, 1)])
    p.vop(lambda e, o=a["bw"][:, 0:zw], x=a["bw"][:, 0:zw], y=a["bw2"][:, 0:zw],
          s=qR, m=MR: e.scalar_tensor_tensor(o, m, s, x, ALU.mult, ALU.add))
    p.vop(lambda e, o=a["bw"][:, 0:zw], x=a["bw"][:, 0:zw], y=a["bw2"][:, 0:zw]:
          e.tensor_add(o, x, y), incs=[(gsem, 1)])
    bw_done = gsem.now()
    w = [(gsem, bw_done), (tsem, scores_done)]
    if prev_exp is not None:
        w.append((ssem, prev_exp))
    p.vop(lambda e, o=a["sw"][:, 0:zw], i=ps_s[:, kw:kw + zw], b=a["bw"][:, 0:zw]:
          e.tensor_add(o, i, b), waits=w, incs=[(vsem, 1)])
    sw_done = vsem.now()
    p.vop(lambda e, o=a["mrow"][:, 1:2], i=a["sw"][:, 0:zw]:
          e.reduce_max(o, i, axis=AX.X), waits=[(vsem, sw_done)], incs=[(vsem, 1)])
    if kw > 0:
        p.vop(lambda e, o=a["mrow"][:, 0:1], i=ps_s[:, 0:kw]:
              e.reduce_max(o, i, axis=AX.X), waits=[(tsem, scores_done)],
              incs=[(vsem, 1)])
        p.vop(lambda e, o=a["mrow"][:, 0:1], i=a["mrow"][:, 0:1], s=qL:
              e.tensor_scalar(o, i, s, None, ALU.add),
              waits=[(vsem, vsem.now()), (ssem, qready)], incs=[(vsem, 1)])
    if kw + zw < 2048:
        p.vop(lambda e, o=a["mrow"][:, 2:3], i=ps_s[:, kw + zw:2048]:
              e.reduce_max(o, i, axis=AX.X), waits=[(tsem, scores_done)],
              incs=[(vsem, 1)])
        p.vop(lambda e, o=a["mrow"][:, 2:3], i=a["mrow"][:, 2:3], s=qR:
              e.tensor_scalar(o, i, s, None, ALU.add),
              waits=[(vsem, vsem.now()), (ssem, qready)], incs=[(vsem, 1)])
    lo = 0 if kw > 0 else 1
    hi = 3 if kw + zw < 2048 else 2
    w = [(vsem, vsem.now())]
    if prev_exp is not None:
        w.append((ssem, prev_exp))
    p.vop(lambda e, o=a["negmax"], i=a["mrow"][:, lo:hi]:
          e.reduce_max(o[:], i, axis=AX.X, negate=True),
          waits=w, incs=[(vsem, 1)])
    nm = vsem.now()
    p.vop(lambda e, o=a["biasL"], i=a["negmax"], s=qL:
          e.tensor_scalar(o[:], i[:], s, None, ALU.add),
          waits=[(vsem, nm)], incs=[(vsem, 1)])
    p.vop(lambda e, o=a["biasR"], i=a["negmax"], s=qR:
          e.tensor_scalar(o[:], i[:], s, None, ALU.add), incs=[(vsem, 1)])
    bias_done = vsem.now()
    nsum = 0
    if kw > 0:
        p.act(a["P"][:, 0:kw], ps_s[:, 0:kw], AF.Exp, bias=a["biasL"][:, 0:1],
              scale=1.0, accum_out=a["sums"][:, 0:1],
              waits=[(vsem, bias_done)], incs=[(ssem, 1)])
        nsum += 1
    p.act(a["P"][:, kw:kw + zw], a["sw"][:, 0:zw], AF.Exp, bias=a["negmax"][:, 0:1],
          scale=1.0, accum_out=a["sums"][:, nsum:nsum + 1],
          waits=[(vsem, bias_done)], incs=[(ssem, 1)])
    nsum += 1
    if kw + zw < 2048:
        p.act(a["P"][:, kw + zw:2048], ps_s[:, kw + zw:2048], AF.Exp,
              bias=a["biasR"][:, 0:1], scale=1.0,
              accum_out=a["sums"][:, nsum:nsum + 1],
              waits=[(vsem, bias_done)], incs=[(ssem, 1)])
        nsum += 1
    exp_done = ssem.now()
    p.vop(lambda e, o=a["rinv"], i=a["sums"][:, 0:nsum]:
          e.reduce_sum(o[:], i, axis=AX.X), waits=[(ssem, exp_done)],
          incs=[(vsem, 1)])
    p.vop(lambda e, o=a["rinv"], i=a["rinv"]: e.reciprocal(o[:], i[:]),
          waits=[(vsem, vsem.now())], incs=[(vsem, 1)])
    return exp_done, bw_done


def ph_l0_ffn(p, t, dr, psum, xdram, entry_waits=()):
    nc = p.nc
    tsem, vsem, ssem = p.sem("t"), p.sem("v"), p.sem("s")
    dw, dacc = p.sem("dw"), p.sem("dacc")
    ps_a, ps_b = psum["a"], psum["b"]
    XNs = t["XN"]
    entry = list(entry_waits)

    a0 = p.alloc("ffn_a0", [128, HCn * 1024], BF16)
    a1 = p.alloc("ffn_a1", [128, HCn * 1024], BF16)
    w12 = [p.alloc(f"ffn_w12_{i}", [128, DC * 128], BF16) for i in range(4)]
    wh = [p.alloc(f"ffn_wh{i}", [128, HCn * 128], BF16) for i in range(2)]
    sil = p.alloc("ffn_sil", [128, 512], BF16)
    acc = [p.alloc(f"ffn_acc{i}", [128, 512], F32) for i in range(2)]
    scales = {}
    for k, w_ in (("w1", 32), ("w2", 32), ("h1", 32), ("h2", 32), ("w3", 8)):
        scales[k] = p.alloc(f"ffn_s_{k}", [128, w_], F32)
        p.dma("sync", scales[k], dr[f"l0_{k}s"], incs=[(dw, 16)])
    sc_ready = dw.now()
    acc_dma = {}
    wh_last_reader = 0

    for pas in range(2):
        tok0 = pas * 1024
        w12_mm_done = {}
        psa_free = None
        psb_free = None
        for oc in range(HCn):
            bi = oc % 2
            if oc >= 2:
                war = [(tsem, w12_mm_done[oc - 2])]
            elif pas > 0:
                war = [(tsem, w12_last)]
            else:
                war = []
            p.dma("gpsimd", w12[bi].rearrange("p (kc o) -> p kc o", o=128),
                  dr["l0_w1T"][:, oc * 128:(oc + 1) * 128].rearrange("(kc p) o -> p kc o", p=128),
                  waits=war, incs=[(dw, 16)])
            p.dma("gpsimd", w12[2 + bi].rearrange("p (kc o) -> p kc o", o=128),
                  dr["l0_w2T"][:, oc * 128:(oc + 1) * 128].rearrange("(kc p) o -> p kc o", p=128),
                  waits=list(war), incs=[(dw, 16)])
            w_ready = dw.now()
            for ns in range(2):
                for kc in range(DC):
                    w = []
                    if kc == 0:
                        w = entry + [(dw, w_ready)]
                        if psa_free is not None:
                            w.append((ssem, psa_free))
                    p.mm(ps_a[:, 0:512], w12[bi][:, kc * 128:kc * 128 + 128],
                         XNs[:, kc * N + tok0 + ns * 512: kc * N + tok0 + ns * 512 + 512],
                         kc == 0, kc == DC - 1, waits=w,
                         incs=[(tsem, 1)] if kc == DC - 1 else [])
                mm1 = tsem.now()
                for kc in range(DC):
                    w = []
                    if kc == 0 and psb_free is not None:
                        w = [(vsem, psb_free)]
                    p.mm(ps_b[:, 0:512], w12[2 + bi][:, kc * 128:kc * 128 + 128],
                         XNs[:, kc * N + tok0 + ns * 512: kc * N + tok0 + ns * 512 + 512],
                         kc == 0, kc == DC - 1, waits=w,
                         incs=[(tsem, 1)] if kc == DC - 1 else [])
                mm2 = tsem.now()
                w12_mm_done[oc] = mm2
                p.act(sil, ps_a[:, 0:512], AF.Silu, scale=scales["w1"][:, oc:oc + 1],
                      waits=[(tsem, mm1), (dw, sc_ready)], incs=[(ssem, 1)])
                psa_free = ssem.now()
                p.vop(lambda e, o=a0[:, oc * 1024 + ns * 512: oc * 1024 + ns * 512 + 512],
                      i=ps_b[:, 0:512], s=scales["w2"][:, oc:oc + 1], m=sil:
                      e.scalar_tensor_tensor(o, i, s, m[:], ALU.mult, ALU.mult),
                      waits=[(tsem, mm2), (ssem, psa_free), (dw, sc_ready)],
                      incs=[(vsem, 1)])
                psb_free = vsem.now()
        w12_last = max(w12_mm_done.values())
        a0_done = vsem.now()

        def h_layer(wkey, src, dst, src_ready, extra_wait, prev_reader, nchunks_out):
            mm_done = {}
            free = {}
            for oc in range(nchunks_out):
                bi = oc % 2
                war = [(tsem, mm_done[oc - 2])] if oc >= 2 else [(tsem, prev_reader)]
                p.dma("gpsimd", wh[bi].rearrange("p (kc o) -> p kc o", o=128),
                      dr[f"l0_{wkey}T"][:, oc * 128:(oc + 1) * 128].rearrange("(kc p) o -> p kc o", p=128),
                      waits=war, incs=[(dw, 16)])
                w_ready = dw.now()
                for ns in range(2):
                    ppid = (oc * 2 + ns) % 2
                    pp = [ps_a, ps_b][ppid]
                    for kc in range(HCn):
                        w = []
                        if kc == 0:
                            w = [src_ready, (dw, w_ready)]
                            if ppid in free:
                                w.append((ssem, free[ppid]))
                            if extra_wait is not None:
                                w.append(extra_wait)
                        p.mm(pp[:, 0:512], wh[bi][:, kc * 128:kc * 128 + 128],
                             src[:, kc * 1024 + ns * 512: kc * 1024 + ns * 512 + 512],
                             kc == 0, kc == HCn - 1, waits=w,
                             incs=[(tsem, 1)] if kc == HCn - 1 else [])
                    mmd = tsem.now()
                    mm_done[oc] = mmd
                    p.act(dst[:, oc * 1024 + ns * 512: oc * 1024 + ns * 512 + 512],
                          pp[:, 0:512], AF.Gelu, scale=scales[wkey][:, oc:oc + 1],
                          waits=[(tsem, mmd), (dw, sc_ready)], incs=[(ssem, 1)])
                    free[ppid] = ssem.now()
            return ssem.now(), max(mm_done.values())

        a1_done, h1_last = h_layer("h1", a0, a1, (vsem, a0_done),
                                   (ssem, psa_free), wh_last_reader, HCn)
        a2_done, h2_last = h_layer("h2", a1, a0, (ssem, a1_done),
                                   (tsem, h1_last), h1_last, HCn)
        wh_last_reader = h2_last

        mm_done = {}
        free_v = {}
        for oc in range(DC):
            bi = oc % 2
            war = [(tsem, mm_done[oc - 2])] if oc >= 2 else [(tsem, wh_last_reader)]
            p.dma("gpsimd", wh[bi].rearrange("p (kc o) -> p kc o", o=128),
                  dr["l0_w3T"][:, oc * 128:(oc + 1) * 128].rearrange("(kc p) o -> p kc o", p=128),
                  waits=war, incs=[(dw, 16)])
            w_ready = dw.now()
            for ns in range(2):
                ppid = (oc * 2 + ns) % 2
                pp = [ps_a, ps_b][ppid]
                for kc in range(HCn):
                    w = []
                    if kc == 0:
                        w = [(ssem, a2_done), (dw, w_ready)]
                        if ppid in free_v:
                            w.append((vsem, free_v[ppid]))
                        else:
                            w.append((ssem, a2_done))
                    p.mm(pp[:, 0:512], wh[bi][:, kc * 128:kc * 128 + 128],
                         a0[:, kc * 1024 + ns * 512: kc * 1024 + ns * 512 + 512],
                         kc == 0, kc == HCn - 1, waits=w,
                         incs=[(tsem, 1)] if kc == HCn - 1 else [])
                mmd = tsem.now()
                mm_done[oc] = mmd
                ai = (oc * 2 + ns) % 2
                aw = [(tsem, mmd), (dw, sc_ready)]
                if ai in acc_dma:
                    aw.append((dacc, acc_dma[ai]))
                p.vop(lambda e, o=acc[ai], i=pp[:, 0:512], s=scales["w3"][:, oc:oc + 1]:
                      e.tensor_scalar(o[:], i, s, None, ALU.mult),
                      waits=aw, incs=[(vsem, 1)])
                free_v[ppid] = vsem.now()
                vd = vsem.now()
                p.dma("gpsimd", xdram[oc * 128:(oc + 1) * 128,
                                      tok0 + ns * 512: tok0 + ns * 512 + 512],
                      acc[ai][:], waits=[(vsem, vd)], incs=[(dacc, 16)],
                      accum_op=ALU.add)
                acc_dma[ai] = dacc.now()
        wh_last_reader = max(mm_done.values())
    p.emit("sync", lambda e: e.nop(), waits=[(dacc, p.sem("dacc").now())])


def _alloc_psum(nc):
    ps = {
        "s": nc.alloc_psum_tensor("ps_s", [128, 2048], F32),
        "a": nc.alloc_psum_tensor("ps_a", [128, 512], F32),
        "b": nc.alloc_psum_tensor("ps_b", [128, 512], F32),
        "t": nc.alloc_psum_tensor("ps_t", [128, 1024], BF16),
        "d": nc.alloc_psum_tensor("ps_d", [128, 512], F32),
    }
    ps["c"] = ps["s"][:, 0:512]
    ps["d_ln"] = ps["s"][:, 512:1024]
    return ps


def _build_l0attn_test(weights):
    nc = bass.Bass("TRN2", target_bir_lowering=False)
    dr = {}
    dr["x_in"] = nc.dram_tensor("x_in", [N, D], F32, kind="ExternalInput").ap()
    dr["out"] = nc.dram_tensor("out", [N, D], F32, kind="ExternalOutput").ap()
    for k, v in host_constants().items():
        dr[k] = nc.dram_tensor("c_" + k, list(v.shape), F32, kind="ExternalInput").ap()
    for k in ("l0_WqkvT", "l0_bq_pp", "l0_bk_pp", "l0_bv_pp", "l0_WprojT", "l0_bproj"):
        dr[k] = nc.dram_tensor(k, list(weights[k].shape), F32, kind="ExternalInput").ap()
    p = P(nc)
    t = alloc_common(nc, want_rel=False)
    a = alloc_attn(p, rel=False)
    psum = _alloc_psum(nc)
    load_consts(p, t, dr)
    ph_load_x(p, t, dr, psum)
    x_ready = [(p.sem("v"), p.sem("v").now())]
    ln_done = ph_ln(p, t, psum, "X", "XN", entry_waits=x_ready)
    cfg = cfg_l0_attn(p, nc, t, dr, entry_waits=[(p.sem("v"), ln_done)])
    ph_attn(p, t, a, dr, psum, cfg)
    ph_store_out(p, t, dr, psum)
    p.replay()
    return nc


def _build_identity_test(with_ln=False):
    nc = bass.Bass("TRN2", target_bir_lowering=False)
    dr = {}
    dr["x_in"] = nc.dram_tensor("x_in", [N, D], F32, kind="ExternalInput").ap()
    dr["out"] = nc.dram_tensor("out", [N, D], F32, kind="ExternalOutput").ap()
    if with_ln:
        dr["xn_out"] = nc.dram_tensor("xn_out", [128, DC * N], BF16, kind="ExternalOutput").ap()
    for k, v in host_constants().items():
        dr[k] = nc.dram_tensor("c_" + k, list(v.shape), F32, kind="ExternalInput").ap()
    p = P(nc)
    t = alloc_common(nc)
    psum = _alloc_psum(nc)
    load_consts(p, t, dr)
    ph_load_x(p, t, dr, psum)
    if with_ln:
        x_ready = [(p.sem("v"), p.sem("v").now())]
        done = ph_ln(p, t, psum, "X", "XN", entry_waits=x_ready)
        do = p.sem("do")
        p.dma("sync", dr["xn_out"], t["XN"][:], waits=[(p.sem("v"), done)],
              incs=[(do, 16)])
    ph_store_out(p, t, dr, psum)
    p.replay()
    return nc


def cfg_l0_attn(p, nc, t, dr, entry_waits):
    dw = p.sem("dw")
    wq = [p.alloc(f"l0wq{i}", [128, 8 * 384], BF16) for i in range(2)]
    bq = p.alloc("l0bq", [128, 8], F32)
    bk = p.alloc("l0bk", [128, 8], F32)
    bv = p.alloc("l0bv", [128, 8], F32)
    bqs = p.alloc("l0bqs", [128, 8], F32)
    bp = p.alloc("l0bp", [128, 8], F32)
    for nm, h in (("l0_bq_pp", bq), ("l0_bk_pp", bk), ("l0_bv_pp", bv), ("l0_bproj", bp)):
        p.dma("sync", h[:], dr[nm], incs=[(dw, 16)])
    bias_ready = dw.now()
    vsem = p.sem("v")
    p.vop(lambda e, o=bqs, i=bq: e.tensor_scalar(o[:], i[:], 0.125, None, ALU.mult),
          waits=[(dw, bias_ready)], incs=[(vsem, 1)])
    bias_v = vsem.now()

    def wq_dma(p_, hg, bufi, war):
        p_.dma("gpsimd", wq[bufi].rearrange("p (kc o) -> p kc o", o=384),
               dr["l0_WqkvT"][:, hg * 384:(hg + 1) * 384].rearrange("(kc p) o -> p kc o", p=128),
               waits=war, incs=[(dw, 16)])
        return lambda oc3, kc, b=wq[bufi]: b[:, kc * 384 + oc3 * 128: kc * 384 + oc3 * 128 + 128]

    def evac_spec(hg, oc3):
        dk = ("Q", "K", "V")[oc3]
        bias = (bqs, bk, bv)[oc3][:, hg:hg + 1]
        sc = 0.125 if oc3 == 0 else 1.0
        return [(0, 128, dk, 0, bias, sc)]

    return dict(name="l0at", xn_key="XN", causal=True, rope=True, rel=False,
                nchunks=3, wq_dma=wq_dma, evac_spec=evac_spec,
                proj_w=dr["l0_WprojT"], proj_bias=bp,
                bias_waits=[(dw, bias_ready), (vsem, bias_v)],
                entry_waits=entry_waits)


def cfg_l1_attn(p, nc, t, dr, entry_waits):
    dw = p.sem("dw")
    wq = [p.alloc("l1wq0", [128, 3 * DC * 128], BF16)]
    bq = p.alloc("l1bq", [128, 8], F32)
    bk = p.alloc("l1bk", [128, 8], F32)
    bv = p.alloc("l1bv", [128, 8], F32)
    bqs = p.alloc("l1bqs", [128, 8], F32)
    bo = p.alloc("l1bo", [128, 8], F32)
    for nm, h in (("l1_bq", bq), ("l1_bk", bk), ("l1_bv", bv), ("l1_bo", bo)):
        p.dma("sync", h[:], dr[nm], incs=[(dw, 16)])
    bias_ready = dw.now()
    vsem = p.sem("v")
    p.vop(lambda e, o=bqs, i=bq: e.tensor_scalar(o[:], i[:], 0.125, None, ALU.mult),
          waits=[(dw, bias_ready)], incs=[(vsem, 1)])
    bias_v = vsem.now()

    def wq_dma(p_, hg, bufi, war):
        for wi, wname in enumerate(("l1_Wq", "l1_Wk", "l1_Wv")):
            p_.dma("gpsimd",
                   wq[0].rearrange("p (w kc o) -> p w kc o", w=3, o=128)[:, wi],
                   dr[wname][:, hg * 128:(hg + 1) * 128].rearrange("(kc p) o -> p kc o", p=128),
                   waits=list(war) if wi == 0 else [], incs=[(dw, 16)])
        return lambda oc3, kc, b=wq[0]: b[:, oc3 * 1024 + kc * 128: oc3 * 1024 + kc * 128 + 128]

    def evac_spec(hg, oc3):
        dk = ("Q", "K", "V")[oc3]
        bias = (bqs, bk, bv)[oc3][:, hg:hg + 1]
        sc = 0.125 if oc3 == 0 else 1.0
        return [(0, 128, dk, 0, bias, sc)]

    return dict(name="l1at", xn_key="XN", causal=False, rope=False, rel=True,
                nchunks=3, wq_dma=wq_dma, evac_spec=evac_spec, wq_depth=1,
                proj_w=dr["l1_Wo"], proj_bias=bo,
                bias_waits=[(dw, bias_ready), (vsem, bias_v)],
                entry_waits=entry_waits)


def cfg_msa(p, nc, t, dr, entry_waits):
    dw = p.sem("dw")
    wq = [p.alloc("msawq0", [128, 3 * DC * 128], BF16)]

    def wq_dma(p_, hg, bufi, war):
        for wi in range(3):
            p_.dma("gpsimd",
                   wq[0].rearrange("p (w kc o) -> p w kc o", w=3, o=128)[:, wi],
                   dr["l1_wqkv"][:, wi * 1024 + hg * 128: wi * 1024 + (hg + 1) * 128]
                   .rearrange("(kc p) o -> p kc o", p=128),
                   waits=list(war) if wi == 0 else [], incs=[(dw, 16)])
        return lambda oc3, kc, b=wq[0]: b[:, oc3 * 1024 + kc * 128: oc3 * 1024 + kc * 128 + 128]

    def evac_spec(hg, oc3):
        dk = ("Q", "K", "V")[oc3]
        sc = 0.125 if oc3 == 0 else 1.0
        return [(0, 128, dk, 0, None, sc)]

    return dict(name="msa", xn_key="XNa", causal=False, rope=False, rel=False,
                nchunks=3, wq_dma=wq_dma, evac_spec=evac_spec, wq_depth=1,
                proj_w=dr["l1_wproj"], proj_bias=None,
                out_extra="x1bf", x1_src="XN",
                bias_waits=[], entry_waits=entry_waits)


def ph_moe(p, t, dr, psum, arena_a, xdram_out, entry_waits=()):
    nc = p.nc
    tsem, vsem, ssem = p.sem("t"), p.sem("v"), p.sem("s")
    dw = p.sem("dw")
    ps_a, ps_b = psum["a"], psum["b"]
    X2 = t["XN2"]
    entry = list(entry_waits)
    cst = p.sem("cst")

    sel8 = p.alloc("moe_sel8", [8, 8 * 128], BF16)
    p.dma("gpsimd", sel8, dr["sel8"], incs=[(p.sem("dw"), 16)])
    gw = p.alloc("moe_gw", [128, DC * 8], BF16)
    gb = p.alloc("moe_gb", [1, 8], F32)
    gbb = p.alloc("moe_gbb", [1, 8], BF16)
    gbc = p.alloc("moe_gbc", [128, 8], F32)
    glog = p.alloc("moe_glog", [128, 8], F32)
    gmax = p.alloc("moe_gmax", [128, 1], F32)
    oh_tm = p.alloc("moe_ohtm", [128, 16 * 8], BF16)
    ohfm = p.alloc("moe_ohfm", [8, N], BF16)
    maskbc = p.alloc("moe_mask", [128, N], BF16)
    w1b = [p.alloc(f"moe_w1{i}", [128, DC * 128], BF16) for i in range(2)]
    w2b = [p.alloc("moe_w2a", [128, HCn * 128], BF16),
           p.alloc("moe_w2b", [128, HCn * 128], BF16)]
    ytmp = [t["s512_0"], t["s512_1"]]
    p.dma("gpsimd", gw.rearrange("p (kc e) -> p kc e", e=8),
          dr["l1_gatew"].rearrange("(kc p) e -> p kc e", p=128), incs=[(dw, 16)])
    p.dma("sync", gb, dr["l1_gateb"], incs=[(dw, 16)])
    gw_ready = dw.now()
    p.vop(lambda e, o=gbb, i=gb: e.tensor_copy(o[:], i[:]),
          waits=[(dw, gw_ready)], incs=[(vsem, 1)])
    p.mm(psum["d"][:, 0:8], t["ones_1x128_b"][:], gbb[:], True, True,
         waits=[(vsem, vsem.now()), (cst, t.get('cst_all', 16))], incs=[(tsem, 1)])
    p.vop(lambda e, o=gbc, i=psum["d"][:, 0:8]: e.tensor_copy(o[:], i),
          waits=[(tsem, tsem.now())], incs=[(vsem, 1)])
    gbc_done = vsem.now()

    glog_done = {}
    for tt in range(16):
        for kc in range(DC):
            w = []
            if kc == 0:
                w = entry + [(dw, gw_ready), (vsem, glog_done.get(tt - 1, 0))]
                w.append((vsem, gbc_done))
            p.mm(ps_a[:, 0:8], X2[:, kc * N + tt * 128: kc * N + tt * 128 + 128],
                 gw[:, kc * 8: kc * 8 + 8], kc == 0, kc == DC - 1,
                 waits=w, incs=[(tsem, 1)] if kc == DC - 1 else [])
        mmd = tsem.now()
        p.vop(lambda e, o=glog, i=ps_a[:, 0:8], b=gbc:
              e.tensor_add(o[:], i, b[:]), waits=[(tsem, mmd)], incs=[(vsem, 1)])
        ga = vsem.now()
        p.vop(lambda e, o=gmax, i=glog: e.reduce_max(o[:], i[:], axis=AX.X),
              waits=[(vsem, ga)], incs=[(vsem, 1)])
        gm = vsem.now()
        p.vop(lambda e, o=oh_tm[:, tt * 8:tt * 8 + 8], i=glog, s=gmax[:, 0:1]:
              e.tensor_scalar(o, i[:], s, None, ALU.is_equal),
              waits=[(vsem, gm)], incs=[(vsem, 1)])
        glog_done[tt] = vsem.now()
    pt_free = None
    for tt in range(16):
        w = [(vsem, glog_done[tt]), (cst, t.get('cst_all', 16))]
        if pt_free is not None:
            w.append((vsem, pt_free))
        p.tr(psum["t"][0:8, 0:128], oh_tm[:, tt * 8:tt * 8 + 8], t["ident_b"][:],
             waits=w, incs=[(tsem, 1)])
        p.vop(lambda e, o=ohfm[0:8, tt * 128:(tt + 1) * 128], i=psum["t"][0:8, 0:128]:
              e.tensor_copy(o, i), waits=[(tsem, tsem.now())], incs=[(vsem, 1)])
        pt_free = vsem.now()
    oh_done = vsem.now()
    if "dbg_oh" in dr:
        dbg = p.sem("dbg")
        p.dma("sync", dr["dbg_oh"], oh_tm[:], waits=[(vsem, oh_done)], incs=[(dbg, 16)])
        p.dma("sync", dr["dbg_fm"], ohfm[:], waits=[(vsem, oh_done)], incs=[(dbg, 16)])
        if "dbg_log" in dr:
            p.dma("sync", dr["dbg_log"], glog[:], waits=[(vsem, oh_done)], incs=[(dbg, 16)])
        if "dbg_max" in dr:
            p.dma("sync", dr["dbg_max"], gmax[:], waits=[(vsem, oh_done)], incs=[(dbg, 16)])
        if "dbg_gbc" in dr:
            p.dma("sync", dr["dbg_gbc"], gbc[:], waits=[(vsem, oh_done)], incs=[(dbg, 16)])

    a = arena_a
    mask_done = 0
    w1_mm_done = {}
    w2_mm_done = {}
    yt_dma = {}
    for e_i in range(8):
        for nb4 in range(4):
            p.mm(psum["d"][:, 0:512], sel8[:, e_i * 128:(e_i + 1) * 128],
                 ohfm[0:8, nb4 * 512:(nb4 + 1) * 512], True, True,
                 waits=[(vsem, vsem.now()), (cst, t.get('cst_all', 16)), (dw, gw_ready)],
                 incs=[(tsem, 1)])
            p.vop(lambda e, o=maskbc[:, nb4 * 512:(nb4 + 1) * 512],
                  i=psum["d"][:, 0:512]: e.tensor_copy(o, i),
                  waits=[(tsem, tsem.now())], incs=[(vsem, 1)])
        mask_ready = vsem.now()

        for pas in range(2):
            tok0 = pas * 1024
            free_s = {}
            for oc in range(HCn):
                bi = oc % 2
                war = [(tsem, w1_mm_done[(e_i, pas, oc - 2)])] if oc >= 2 else                     [(tsem, w1_mm_done.get((e_i, pas - 1, HCn - 1),
                                           w1_mm_done.get((e_i - 1, 1, HCn - 1), 0)))]
                p.dma("gpsimd", w1b[bi].rearrange("p (kc o) -> p kc o", o=128),
                      dr["l1_moe_w1"][e_i][:, oc * 128:(oc + 1) * 128]
                      .rearrange("(kc p) o -> p kc o", p=128),
                      waits=war, incs=[(dw, 16)])
                w_ready = dw.now()
                for ns in range(2):
                    ppid = (oc * 2 + ns) % 2
                    pp = [ps_a, ps_b][ppid]
                    for kc in range(DC):
                        w = []
                        if kc == 0:
                            w = entry + [(dw, w_ready)]
                            if ppid in free_s:
                                w.append((ssem, free_s[ppid]))
                            else:
                                w.append((ssem, ssem.now()))
                        p.mm(pp[:, 0:512], w1b[bi][:, kc * 128:kc * 128 + 128],
                             X2[:, kc * N + tok0 + ns * 512: kc * N + tok0 + ns * 512 + 512],
                             kc == 0, kc == DC - 1, waits=w,
                             incs=[(tsem, 1)] if kc == DC - 1 else [])
                    mmd = tsem.now()
                    w1_mm_done[(e_i, pas, oc)] = mmd
                    p.act(a[:, oc * 1024 + ns * 512: oc * 1024 + ns * 512 + 512],
                          pp[:, 0:512], AF.Gelu_apprx_tanh,
                          waits=[(tsem, mmd)], incs=[(ssem, 1)])
                    free_s[ppid] = ssem.now()
            a_done = ssem.now()
            free_v = {}
            for oc in range(DC):
                bi = oc % 2
                war = [(tsem, w2_mm_done[(e_i, pas, oc - 2)])] if oc >= 2 else                     [(tsem, w2_mm_done.get((e_i, pas - 1, DC - 2 + oc),
                                           w2_mm_done.get((e_i - 1, 1, DC - 2 + oc), 0)))]
                p.dma("gpsimd", w2b[bi].rearrange("p (kc o) -> p kc o", o=128),
                      dr["l1_moe_w2"][e_i][:, oc * 128:(oc + 1) * 128]
                      .rearrange("(kc p) o -> p kc o", p=128),
                      waits=war, incs=[(dw, 16)])
                w_ready = dw.now()
                for ns in range(2):
                    ppid = (oc * 2 + ns) % 2
                    pp = [ps_a, ps_b][ppid]
                    for kc in range(HCn):
                        w = []
                        if kc == 0:
                            w = [(ssem, a_done), (dw, w_ready)]
                            if ppid in free_v:
                                w.append((vsem, free_v[ppid]))
                        p.mm(pp[:, 0:512], w2b[bi][:, kc * 128:kc * 128 + 128],
                             a[:, kc * 1024 + ns * 512: kc * 1024 + ns * 512 + 512],
                             kc == 0, kc == HCn - 1, waits=w,
                             incs=[(tsem, 1)] if kc == HCn - 1 else [])
                    mmd = tsem.now()
                    w2_mm_done[(e_i, pas, oc)] = mmd
                    yi = (oc * 2 + ns) % 2
                    msl = slice(tok0 + ns * 512, tok0 + ns * 512 + 512)
                    mw = [(tsem, mmd), (vsem, mask_ready)]
                    if yi in yt_dma:
                        mw.append((p.sem("dacc"), yt_dma[yi]))
                    p.vop(lambda e, o=ytmp[yi], i=pp[:, 0:512], m=maskbc[:, msl]:
                          e.tensor_mul(o[:], i, m), waits=mw, incs=[(vsem, 1)])
                    free_v[ppid] = vsem.now()
                    dacc = p.sem("dacc")
                    p.dma("gpsimd", xdram_out[oc * 128:(oc + 1) * 128, msl],
                          ytmp[yi][:], waits=[(vsem, vsem.now())],
                          incs=[(dacc, 16)], accum_op=ALU.add)
                    yt_dma[yi] = dacc.now()
    return vsem.now()


L1A_KEYS = ("l1_Wq", "l1_Wk", "l1_Wv", "l1_Wo", "l1_bq", "l1_bk", "l1_bv",
            "l1_bo", "l1_relT")


def _build_l1attn_test(weights):
    nc = bass.Bass("TRN2", target_bir_lowering=False)
    dr = {}
    dr["x_in"] = nc.dram_tensor("x_in", [N, D], F32, kind="ExternalInput").ap()
    dr["out"] = nc.dram_tensor("out", [N, D], F32, kind="ExternalOutput").ap()
    for k, v in host_constants().items():
        dr[k] = nc.dram_tensor("c_" + k, list(v.shape), F32, kind="ExternalInput").ap()
    for k in L1A_KEYS:
        dr[k] = nc.dram_tensor(k, list(weights[k].shape), F32, kind="ExternalInput").ap()
    p = P(nc)
    t = alloc_common(nc, want_rope=False, want_rel=True)
    a = alloc_attn(p, rel=True)
    psum = _alloc_psum(nc)
    load_consts(p, t, dr)
    ph_load_x(p, t, dr, psum)
    x_ready = [(p.sem("v"), p.sem("v").now())]
    ln_done = ph_ln(p, t, psum, "X", "XN", entry_waits=x_ready)
    cfg = cfg_l1_attn(p, nc, t, dr, entry_waits=[(p.sem("v"), ln_done)])
    ph_attn(p, t, a, dr, psum, cfg)
    ph_store_out(p, t, dr, psum)
    p.replay()
    return nc


def _np_l1attn(x, w):
    T, Dm, H, hd, MD = N, D, 16, 64, 32
    xn = _np_ln(x)
    proj = lambda W, b: (xn @ w[W] + w[b]).reshape(T, H, hd).transpose(1, 0, 2)
    q = proj("L1_attn_Wq", "L1_attn_bq")
    k = proj("L1_attn_Wk", "L1_attn_bk")
    v = proj("L1_attn_Wv", "L1_attn_bv")
    sc = hd ** -0.5
    pos = np.arange(T)
    relm = np.clip(pos[:, None] - pos[None, :], -MD, MD) + MD
    rpe = w["L1_attn_rel_emb"][relm]
    cc = np.einsum("hqd,hkd->hqk", q, k) * sc
    cp = np.einsum("hqd,qkd->hqk", q, rpe) * sc
    att = _np_softmax(cc + cp, -1)
    ctx = np.einsum("hqk,hkd->hqd", att, v)
    ctx = ctx.transpose(1, 0, 2).reshape(T, Dm)
    return x + ctx @ w["L1_attn_Wo"] + w["L1_attn_bo"]


def _test_l1attn():
    w = dict(np.load("/tmp/ref_inputs_0.npz"))
    rel = relayout_weights(w)
    x = np.ascontiguousarray(w["L1_attn_x"][0] if "L1_attn_x" in w else w["L0_attn_x"][0])
    nc = _build_l1attn_test(rel)
    im = {"x_in": x}
    im.update({"c_" + k: v for k, v in host_constants().items()})
    for k in L1A_KEYS:
        im[k] = rel[k]
    res = run_bass_kernel_spmd(nc, [im], core_ids=[0])
    out = res.results[0]["out"]
    wd = {k: v.astype(np.float64) for k, v in w.items()}
    ref = _np_l1attn(x.astype(np.float64), wd).astype(np.float32)
    err = np.abs(out - ref)
    print("l1attn max err:", err.max(), "rel l2:",
          np.sqrt((err ** 2).mean() / (ref ** 2).mean()))


L0A_KEYS = ("l0_WqkvT", "l0_bq_pp", "l0_bk_pp", "l0_bv_pp", "l0_WprojT", "l0_bproj")
L0F_KEYS = ("l0_w1T", "l0_w2T", "l0_h1T", "l0_h2T", "l0_w3T",
            "l0_w1s", "l0_w2s", "l0_h1s", "l0_h2s", "l0_w3s")
MSA_KEYS = ("l1_wqkv", "l1_wproj", "l1_ln1w", "l1_ln1b", "l1_ln2w", "l1_ln2b")
MOE_KEYS = ("l1_moe_w1", "l1_moe_w2", "l1_gatew", "l1_gateb")


def _declare(nc, dr, weights, keys):
    for k in keys:
        dr[k] = nc.dram_tensor(k, list(weights[k].shape), F32, kind="ExternalInput").ap()


def build_launch_A(weights):
    nc = bass.Bass("TRN2", target_bir_lowering=False)
    dr = {}
    dr["x_in"] = nc.dram_tensor("x_in", [N, D], F32, kind="ExternalInput").ap()
    dr["xdram"] = nc.dram_tensor("xdram", [D, N], F32, kind="ExternalOutput").ap()
    dr["xnd"] = nc.dram_tensor("xnd", [128, DC * N], BF16, kind="ExternalOutput").ap()
    for k, v in host_constants().items():
        dr[k] = nc.dram_tensor("c_" + k, list(v.shape), F32, kind="ExternalInput").ap()
    _declare(nc, dr, weights, L0A_KEYS)
    p = P(nc)
    t = alloc_common(nc, want_rel=False)
    a = alloc_attn(p, rel=False)
    psum = _alloc_psum(nc)
    load_consts(p, t, dr)
    ph_load_x(p, t, dr, psum)
    x_ready = [(p.sem("v"), p.sem("v").now())]
    ln_done = ph_ln(p, t, psum, "X", "XN", entry_waits=x_ready)
    cfg = cfg_l0_attn(p, nc, t, dr, entry_waits=[(p.sem("v"), ln_done)])
    attn_done = ph_attn(p, t, a, dr, psum, cfg)
    do = p.sem("do")
    for c in range(DC):
        p.dma("sync", dr["xdram"][c * 128:(c + 1) * 128, :], t["X"][:, c * N:(c + 1) * N],
              waits=[(p.sem("v"), attn_done)], incs=[(do, 16)])
    p.dma("sync", dr["xnd"], t["XN"][:], waits=[(p.sem("v"), ln_done)], incs=[(do, 16)])
    p.emit("sync", lambda e: e.nop(), waits=[(do, do.now())])
    p.replay()
    return nc


def build_launch_B(weights):
    nc = bass.Bass("TRN2", target_bir_lowering=False)
    dr = {}
    dr["xdram_in"] = nc.dram_tensor("xdram_in", [D, N], F32, kind="ExternalInput").ap()
    dr["xnd"] = nc.dram_tensor("xnd", [128, DC * N], BF16, kind="ExternalInput").ap()
    xdram = nc.dram_tensor("xdram", [D, N], F32, kind="ExternalOutput").ap()
    _declare(nc, dr, weights, L0F_KEYS)
    p = P(nc)
    t = {"XN": nc.alloc_sbuf_tensor("XN", [128, DC * N], BF16)}
    psum = _alloc_psum(nc)
    dx, dcp = p.sem("dx"), p.sem("dcp")
    p.dma("sync", t["XN"][:], dr["xnd"], incs=[(dx, 16)])
    p.dma("sync", xdram, dr["xdram_in"], incs=[(dcp, 16)])
    ph_l0_ffn(p, t, dr, psum, xdram, entry_waits=[(dx, 16), (dcp, 16)])
    p.replay()
    return nc


def build_launch_C(weights):
    nc = bass.Bass("TRN2", target_bir_lowering=False)
    dr = {}
    dr["xdram_in"] = nc.dram_tensor("xdram_in", [D, N], F32, kind="ExternalInput").ap()
    dr["xdram"] = nc.dram_tensor("xdram", [D, N], F32, kind="ExternalOutput").ap()
    dr["xnd1"] = nc.dram_tensor("xnd1", [128, DC * N], BF16, kind="ExternalOutput").ap()
    for k, v in host_constants().items():
        dr[k] = nc.dram_tensor("c_" + k, list(v.shape), F32, kind="ExternalInput").ap()
    _declare(nc, dr, weights, L1A_KEYS)
    p = P(nc)
    t = alloc_common(nc, want_rope=False, want_rel=True)
    a = alloc_attn(p, rel=True)
    psum = _alloc_psum(nc)
    load_consts(p, t, dr)
    dx = p.sem("dx")
    for c in range(DC):
        p.dma("sync", t["X"][:, c * N:(c + 1) * N], dr["xdram_in"][c * 128:(c + 1) * 128, :],
              incs=[(dx, 16)])
    x_ready = [(dx, dx.now())]
    ln_done = ph_ln(p, t, psum, "X", "XN", entry_waits=x_ready)
    vsem = p.sem("v")
    for c in range(DC):
        p.vop(lambda e, o=t["X"][:, c * N:(c + 1) * N], i=t["XN"][:, c * N:(c + 1) * N]:
              e.tensor_add(o, o, i), waits=[(vsem, ln_done)], incs=[(vsem, 1)])
    cfg = cfg_l1_attn(p, nc, t, dr, entry_waits=[(vsem, ln_done)])
    attn_done = ph_attn(p, t, a, dr, psum, cfg)
    do = p.sem("do")
    for c in range(DC):
        p.dma("sync", dr["xdram"][c * 128:(c + 1) * 128, :], t["X"][:, c * N:(c + 1) * N],
              waits=[(vsem, attn_done)], incs=[(do, 16)])
    p.dma("sync", dr["xnd1"], t["XN"][:], waits=[(vsem, ln_done)], incs=[(do, 16)])
    p.emit("sync", lambda e: e.nop(), waits=[(do, do.now())])
    p.replay()
    return nc


def build_launch_D(weights):
    nc = bass.Bass("TRN2", target_bir_lowering=False)
    dr = {}
    dr["xdram_in"] = nc.dram_tensor("xdram_in", [D, N], F32, kind="ExternalInput").ap()
    dr["xnd1"] = nc.dram_tensor("xnd1", [128, DC * N], BF16, kind="ExternalInput").ap()
    dr["xdram"] = nc.dram_tensor("xdram", [D, N], F32, kind="ExternalOutput").ap()
    dr["xn2d"] = nc.dram_tensor("xn2d", [128, DC * N], BF16, kind="ExternalOutput").ap()
    for k, v in host_constants().items():
        dr[k] = nc.dram_tensor("c_" + k, list(v.shape), F32, kind="ExternalInput").ap()
    _declare(nc, dr, weights, MSA_KEYS)
    p = P(nc)
    t = alloc_common(nc, want_rope=False, want_rel=False, want_io=False, want_x=False)
    t["XNa"] = nc.alloc_sbuf_tensor("XNa", [128, DC * N], BF16)
    t["x1bf"] = t["XNa"]
    a = alloc_attn(p, rel=False)
    psum = _alloc_psum(nc)
    load_consts(p, t, dr)
    dx, dw, dcp = p.sem("dx"), p.sem("dw"), p.sem("dcp")
    p.dma("sync", dr["xdram"], dr["xdram_in"], incs=[(dcp, 16)])
    p.dma("sync", t["XN"][:], dr["xnd1"], incs=[(dx, 16)])
    x_ready = dx.now()
    lnw = {}
    for k in ("ln1w", "ln1b", "ln2w", "ln2b"):
        lnw[k] = nc.alloc_sbuf_tensor("w_" + k, [128, 8], F32)
        p.dma("sync", lnw[k][:], dr[f"l1_{k}"], incs=[(dw, 16)])
    lnw_ready = dw.now()
    ln1_done = ph_ln(p, t, psum, "XN", "XNa", wb=(lnw["ln1w"], lnw["ln1b"]),
                     entry_waits=[(dx, x_ready), (dw, lnw_ready)], src_bf16=True)
    cfg = cfg_msa(p, nc, t, dr, entry_waits=[(p.sem("v"), ln1_done)])
    cfg["out_dram"] = dr["xdram"]
    cfg["copy_wait"] = (dcp, 16)
    msa_done = ph_attn(p, t, a, dr, psum, cfg)
    ln2_done = ph_ln(p, t, psum, "x1bf", "XN", wb=(lnw["ln2w"], lnw["ln2b"]),
                     entry_waits=[(p.sem("v"), msa_done), (dw, lnw_ready),
                                  (p.sem("dacc"), p.sem("dacc").now())],
                     src_bf16=True)
    do = p.sem("do")
    p.dma("sync", dr["xn2d"], t["XN"][:], waits=[(p.sem("v"), ln2_done)], incs=[(do, 16)])
    p.emit("sync", lambda e: e.nop(),
           waits=[(do, do.now()), (p.sem("dacc"), p.sem("dacc").now())])
    p.replay()
    return nc


def build_launch_E(weights):
    nc = bass.Bass("TRN2", target_bir_lowering=False)
    dr = {}
    dr["xdram_in"] = nc.dram_tensor("xdram_in", [D, N], F32, kind="ExternalInput").ap()
    dr["xn2d"] = nc.dram_tensor("xn2d", [128, DC * N], BF16, kind="ExternalInput").ap()
    dr["out"] = nc.dram_tensor("out", [N, D], F32, kind="ExternalOutput").ap()
    xdram = nc.dram_tensor("xdram_s", [D, N], F32, kind="Internal").ap()
    for k, v in host_constants().items():
        dr[k] = nc.dram_tensor("c_" + k, list(v.shape), F32, kind="ExternalInput").ap()
    _declare(nc, dr, weights, MOE_KEYS)
    p = P(nc)
    t = alloc_common(nc, want_rope=False, want_rel=False, want_ln=False)
    t["XN2"] = t["XN"]
    arena = nc.alloc_sbuf_tensor("arena", [128, HCn * 1024], BF16)
    psum = _alloc_psum(nc)
    load_consts(p, t, dr)
    dx, dcp = p.sem("dx"), p.sem("dcp")
    p.dma("sync", t["XN"][:], dr["xn2d"], incs=[(dx, 16)])
    p.dma("sync", xdram, dr["xdram_in"], incs=[(dcp, 16)])
    ph_moe(p, t, dr, psum, arena[:], xdram,
           entry_waits=[(dx, 16), (dcp, 16)])
    dacc = p.sem("dacc")
    acc_all = dacc.now()
    dx2 = p.sem("dx")
    for c in range(DC):
        p.dma("sync", t["X"][:, c * N:(c + 1) * N], xdram[c * 128:(c + 1) * 128, :],
              waits=[(dacc, acc_all)], incs=[(dx2, 16)])
    vsem = p.sem("v")
    p.emit("vector", lambda e: e.nop(), waits=[(dx2, dx2.now())], incs=[(vsem, 1)])
    ph_store_out(p, t, dr, psum)
    p.replay()
    return nc


ALL_KEYS = L0A_KEYS + L0F_KEYS + tuple(L1A_KEYS) + MSA_KEYS + MOE_KEYS


def build_fused(weights, debug=False, upto=99):
    nc = bass.Bass("TRN2", target_bir_lowering=False)
    dr = {}
    dr["x_in"] = nc.dram_tensor("x_in", [N, D], F32, kind="ExternalInput").ap()
    dr["out"] = nc.dram_tensor("out", [N, D], F32, kind="ExternalOutput").ap()
    xdram = nc.dram_tensor("xdram_s", [D, N], F32, kind="Internal").ap()
    for k, v in host_constants().items():
        dr[k] = nc.dram_tensor("c_" + k, list(v.shape), F32, kind="ExternalInput").ap()
    _declare(nc, dr, weights, ALL_KEYS)
    if debug:
        for nm in ("dbg_x0", "dbg_x1", "dbg_x2", "dbg_x3"):
            dr[nm] = nc.dram_tensor(nm, [D, N], F32, kind="ExternalOutput").ap()
        dr["dbg_xn"] = nc.dram_tensor("dbg_xn", [128, DC * N], BF16, kind="ExternalOutput").ap()

    p = P(nc)
    arena = nc.alloc_sbuf_tensor("arena", [128, 78336], BF16)
    arap = arena[:]

    def carve(off_kb, shape, dtype):
        s = int(off_kb * 512)
        w = shape[1] * (2 if dtype == F32 else 1)
        ap = arap[:shape[0], s:s + w]
        if dtype == F32:
            ap = ap.bitcast(F32)
        return ap

    p.bind = {}
    B = p.bind
    B["at_Q"] = carve(64, [128, N], BF16)
    B["at_K"] = carve(68, [128, N], BF16)
    B["at_V"] = carve(72, [128, N], BF16)
    B["at_VT"] = carve(76, [128, N], BF16)
    B["at_P"] = carve(80, [128, N], BF16)
    B["at_PT"] = carve(84, [128, N], BF16)
    B["at_CTX"] = carve(88, [128, DC * N], BF16)
    B["l0wq0"] = carve(120, [128, 3072], BF16)
    B["l0wq1"] = carve(126, [128, 3072], BF16)
    B["l1wq0"] = carve(120, [128, 3072], BF16)
    B["msawq0"] = carve(120, [128, 3072], BF16)
    B["at_qrel"] = carve(132, [128, 1040], F32)
    B["at_qrelb"] = carve(150.5, [128, 1040], BF16)
    B["at_win"] = carve(136.5, [128, 3072], BF16)
    B["at_sw"] = carve(142.5, [128, 384], F32)
    B["at_bw"] = carve(144, [128, 384], BF16)
    B["at_bw2"] = carve(144.75, [128, 384], BF16)
    for nm in ("l0at", "l1at", "msa"):
        B[f"{nm}_wpb0"] = carve(146, [128, DC * 128], BF16)
        B[f"{nm}_wpb1"] = carve(148, [128, DC * 128], BF16)
    B["l1at_relT"] = carve(129.5, [128, 65], BF16)
    B["ffn_a1"] = carve(0, [128, HCn * 1024], BF16)
    B["ffn_a0"] = carve(64, [128, HCn * 1024], BF16)
    for i in range(4):
        B[f"ffn_w12_{i}"] = carve(128 + 2 * i, [128, DC * 128], BF16)
    B["ffn_wh0"] = carve(136, [128, HCn * 128], BF16)
    B["ffn_wh1"] = carve(144, [128, HCn * 128], BF16)
    B["ffn_sil"] = carve(152, [128, 512], BF16)
    B["moe_w10"] = carve(128, [128, DC * 128], BF16)
    B["moe_w11"] = carve(130, [128, DC * 128], BF16)
    B["moe_w2a"] = carve(0, [128, HCn * 128], BF16)
    B["moe_w2b"] = carve(8, [128, HCn * 128], BF16)
    B["moe_mask"] = carve(140, [128, N], BF16)
    B["moe_gw"] = carve(144, [128, DC * 8], BF16)
    B["moe_sel8"] = carve(146, [8, 8 * 128], BF16)
    B["moe_ohfm"] = carve(148, [8, N], BF16)

    t = alloc_common(nc, want_rope=False, want_rel=False, want_io=False, want_x=False)
    t["X"] = carve(0, [128, DC * N], F32)
    t["XNa"] = carve(0, [128, DC * N], BF16)
    t["x1bf"] = t["XNa"]
    t["XN2"] = t["XN"]
    t["xio0"] = carve(128, [128, D], F32)
    t["xio1"] = carve(132, [128, D], F32)
    a = alloc_attn(p, rel=True)
    psum = _alloc_psum(nc)
    ffn_acc0 = t["s512_0"]
    ffn_acc1 = t["s512_1"]
    B["ffn_acc0"] = ffn_acc0[:]
    B["ffn_acc1"] = ffn_acc1[:]

    load_consts(p, t, dr)
    t["rope_a"] = carve(132, [128, N], BF16)
    t["rope_b"] = carve(136, [128, N], BF16)
    t["mband_b"] = carve(126, [128, 576], BF16)
    t["mleft_b"] = carve(127.25, [128, 576], BF16)
    t["mright_b"] = carve(130, [128, 576], BF16)
    cst = p.sem("cst")
    vsem = p.sem("v")

    ph_load_x(p, t, dr, psum)
    p.barrier()
    p.dma("gpsimd", t["rope_a"], dr["rope_a"], incs=[(cst, 16)])
    p.dma("gpsimd", t["rope_b"], dr["rope_b"], incs=[(cst, 16)])
    t["cst_all"] = cst.now()
    ln_done = ph_ln(p, t, psum, "X", "XN", entry_waits=[])
    cfg = cfg_l0_attn(p, nc, t, dr, entry_waits=[(vsem, ln_done)])
    ph_attn(p, t, a, dr, psum, cfg)
    p.barrier()
    do = p.sem("do")
    for c in range(DC):
        p.dma("sync", xdram[c * 128:(c + 1) * 128, :], t["X"][:, c * N:(c + 1) * N],
              incs=[(do, 16)])
    p.emit("sync", lambda e: e.nop(), waits=[(do, do.now())])
    p.barrier()
    if debug:
        dbg = p.sem("dbg")
        p.dma("sync", dr["dbg_x3"], xdram, incs=[(dbg, 16)])
        p.dma("sync", dr["dbg_xn"], t["XN"][:], incs=[(dbg, 16)])
        p.emit("sync", lambda e: e.nop(), waits=[(dbg, p.sem("dbg").now())])
        p.barrier()
    ph_l0_ffn(p, t, dr, psum, xdram, entry_waits=[])
    p.barrier()
    if debug:
        dbg = p.sem("dbg")
        p.dma("sync", dr["dbg_x0"], xdram, incs=[(dbg, 16)])
        p.emit("sync", lambda e: e.nop(), waits=[(dbg, p.sem("dbg").now())])
        p.barrier()
    if upto <= 3:
        p.replay()
        return nc
    dx = p.sem("dx")
    for c in range(DC):
        p.dma("sync", t["X"][:, c * N:(c + 1) * N], xdram[c * 128:(c + 1) * 128, :],
              incs=[(dx, 16)])
    p.emit("sync", lambda e: e.nop(), waits=[(dx, dx.now())])
    p.barrier()
    p.dma("gpsimd", t["mband_b"], dr["mband"], incs=[(cst, 16)])
    p.dma("gpsimd", t["mleft_b"], dr["mleft"], incs=[(cst, 16)])
    p.dma("gpsimd", t["mright_b"], dr["mright"], incs=[(cst, 16)])
    t["cst_all"] = cst.now()
    ln1_done = ph_ln(p, t, psum, "X", "XN", entry_waits=[])
    for c in range(DC):
        p.vop(lambda e, o=t["X"][:, c * N:(c + 1) * N], i=t["XN"][:, c * N:(c + 1) * N]:
              e.tensor_add(o, o, i), waits=[(vsem, ln1_done)], incs=[(vsem, 1)])
    xadd_done = vsem.now()
    cfg = cfg_l1_attn(p, nc, t, dr, entry_waits=[(vsem, xadd_done)])
    ph_attn(p, t, a, dr, psum, cfg)
    p.barrier()
    if upto <= 4:
        p.replay()
        return nc
    for c in range(DC):
        p.dma("sync", xdram[c * 128:(c + 1) * 128, :], t["X"][:, c * N:(c + 1) * N],
              incs=[(do, 16)])
    p.emit("sync", lambda e: e.nop(), waits=[(do, do.now())])
    p.barrier()
    if debug:
        dbg = p.sem("dbg")
        p.dma("sync", dr["dbg_x1"], xdram, incs=[(dbg, 16)])
        p.emit("sync", lambda e: e.nop(), waits=[(dbg, p.sem("dbg").now())])
        p.barrier()
    dw = p.sem("dw")
    lnw = {}
    for k in ("ln1w", "ln1b", "ln2w", "ln2b"):
        lnw[k] = nc.alloc_sbuf_tensor("w_" + k, [128, 8], F32)
        p.dma("sync", lnw[k][:], dr[f"l1_{k}"], incs=[(dw, 16)])
    lnw_ready = dw.now()
    ln1i_done = ph_ln(p, t, psum, "XN", "XNa", wb=(lnw["ln1w"], lnw["ln1b"]),
                      entry_waits=[(dw, lnw_ready)], src_bf16=True)
    cfg = cfg_msa(p, nc, t, dr, entry_waits=[(vsem, ln1i_done)])
    cfg["out_dram"] = xdram
    msa_done = ph_attn(p, t, a, dr, psum, cfg)
    p.barrier()
    if debug:
        dbg = p.sem("dbg")
        p.dma("sync", dr["dbg_x2"], xdram, incs=[(dbg, 16)])
        p.emit("sync", lambda e: e.nop(), waits=[(dbg, p.sem("dbg").now())])
        p.barrier()
    if upto <= 5:
        p.replay()
        return nc
    ln2_done = ph_ln(p, t, psum, "x1bf", "XN", wb=(lnw["ln2w"], lnw["ln2b"]),
                     entry_waits=[(dw, lnw_ready)], src_bf16=True)
    p.barrier()
    ph_moe(p, t, dr, psum, B["ffn_a0"], xdram, entry_waits=[])
    p.barrier()
    if upto <= 6:
        p.replay()
        return nc
    for c in range(DC):
        p.dma("sync", t["X"][:, c * N:(c + 1) * N], xdram[c * 128:(c + 1) * 128, :],
              incs=[(dx, 16)])
    p.emit("sync", lambda e: e.nop(), waits=[(dx, dx.now())])
    p.barrier()
    p.emit("vector", lambda e: e.nop(), incs=[(vsem, 1)])
    ph_store_out(p, t, dr, psum)
    p.replay()
    return nc


def _shard_inputs(inputs):
    x = np.asarray(inputs["L0_attn_x"])
    shards = [np.ascontiguousarray(x[i]) for i in range(8)]
    return shards


_CACHED = {}


def kernel(**inputs):
    rel = relayout_weights({k: np.asarray(v) for k, v in inputs.items()})
    consts = {"c_" + k: v for k, v in host_constants().items()}
    shards = _shard_inputs(inputs)
    nc = build_fused(rel)
    ims = []
    for i in range(8):
        im = {**consts, "x_in": shards[i]}
        for k in ALL_KEYS:
            im[k] = rel[k]
        ims.append(im)
    res = run_bass_kernel_spmd(nc, ims, core_ids=list(range(8)))
    return np.stack([res.results[i]["out"] for i in range(8)], axis=0)


def _np_softmax(x, axis=-1):
    m = x.max(axis=axis, keepdims=True)
    e = np.exp(x - m)
    return e / e.sum(axis=axis, keepdims=True)


def _np_ln(x, eps=1e-5):
    m = x.mean(-1, keepdims=True)
    v = x.var(-1, keepdims=True)
    return (x - m) / np.sqrt(v + eps)


def _np_l0attn(x, w):
    import math
    T, Dm, H, DHl = N, D, 16, 64
    xn = _np_ln(x)
    qkv = xn @ w["L0_attn_Wqkv"].T + w["L0_attn_bqkv"]
    qkv = qkv.reshape(T, H, 3 * DHl).transpose(1, 0, 2)
    q, k, vv = qkv[:, :, :64], qkv[:, :, 64:128], qkv[:, :, 128:]
    pos = np.arange(T, dtype=np.float32)
    ang = pos[:, None] / 10000.0 ** (np.arange(0, DHl, 2, dtype=np.float32) / DHl)
    rope = np.stack([np.cos(ang), np.sin(ang)], -1).reshape(T, DHl).astype(np.float32)
    rot = lambda z: np.stack([-z[..., 1::2], z[..., 0::2]], -1).reshape(z.shape)
    q = q * rope + rot(q) * (1 - rope)
    k = k * rope + rot(k) * (1 - rope)
    att = np.einsum("htd,hsd->hts", q, k) / math.sqrt(DHl)
    causal = np.tril(np.ones((T, T), dtype=bool))
    att = np.where(causal, att, -np.inf)
    att = _np_softmax(att, -1)
    y = np.einsum("hts,hsd->htd", att, vv)
    y = y.transpose(1, 0, 2).reshape(T, Dm)
    return x + (y @ w["L0_attn_Wproj"].T + w["L0_attn_bproj"])


def _build_spill_test():
    nc = bass.Bass("TRN2", target_bir_lowering=False)
    dr = {}
    dr["x_in"] = nc.dram_tensor("x_in", [N, D], F32, kind="ExternalInput").ap()
    dr["xdram"] = nc.dram_tensor("xdram", [D, N], F32, kind="ExternalOutput").ap()
    dr["xnd"] = nc.dram_tensor("xnd", [128, DC * N], BF16, kind="ExternalOutput").ap()
    for k, v in host_constants().items():
        dr[k] = nc.dram_tensor("c_" + k, list(v.shape), F32, kind="ExternalInput").ap()
    p = P(nc)
    t = alloc_common(nc, want_rope=False, want_rel=False)
    psum = _alloc_psum(nc)
    load_consts(p, t, dr)
    ph_load_x(p, t, dr, psum)
    x_ready = [(p.sem("v"), p.sem("v").now())]
    ln_done = ph_ln(p, t, psum, "X", "XN", entry_waits=x_ready)
    do = p.sem("do")
    for c in range(DC):
        p.dma("sync", dr["xdram"][c * 128:(c + 1) * 128, :], t["X"][:, c * N:(c + 1) * N],
              waits=x_ready, incs=[(do, 16)])
    p.dma("sync", dr["xnd"], t["XN"][:], waits=[(p.sem("v"), ln_done)], incs=[(do, 16)])
    p.emit("sync", lambda e: e.nop(), waits=[(do, do.now())])
    p.replay()
    return nc


def _build_l0ffn_test(weights):
    nc = bass.Bass("TRN2", target_bir_lowering=False)
    dr = {}
    dr["xdram_in"] = nc.dram_tensor("xdram_in", [D, N], F32, kind="ExternalInput").ap()
    dr["xnd"] = nc.dram_tensor("xnd", [128, DC * N], BF16, kind="ExternalInput").ap()
    xdram = nc.dram_tensor("xdram", [D, N], F32, kind="ExternalOutput").ap()
    for k in ("l0_w1T", "l0_w2T", "l0_h1T", "l0_h2T", "l0_w3T"):
        dr[k] = nc.dram_tensor(k, list(weights[k].shape), F32, kind="ExternalInput").ap()
    for k in ("l0_w1s", "l0_w2s", "l0_h1s", "l0_h2s", "l0_w3s"):
        dr[k] = nc.dram_tensor(k, list(weights[k].shape), F32, kind="ExternalInput").ap()
    p = P(nc)
    t = {"XN": nc.alloc_sbuf_tensor("XN", [128, DC * N], BF16)}
    psum = _alloc_psum(nc)
    dx, dcp = p.sem("dx"), p.sem("dcp")
    p.dma("sync", t["XN"][:], dr["xnd"], incs=[(dx, 16)])
    p.dma("sync", xdram, dr["xdram_in"], incs=[(dcp, 16)])
    ph_l0_ffn(p, t, dr, psum, xdram,
              entry_waits=[(dx, 16), (dcp, 16)])
    p.replay()
    return nc


def _test_l0ffn():
    import scipy.special as sp
    w = dict(np.load("/tmp/ref_inputs_0.npz"))
    rel = relayout_weights(w)
    x = np.ascontiguousarray(w["L0_attn_x"][0])
    ncA = _build_spill_test()
    im = {"x_in": x}
    im.update({"c_" + k: v for k, v in host_constants().items()})
    rA = run_bass_kernel_spmd(ncA, [im], core_ids=[0]).results[0]
    ncB = _build_l0ffn_test(rel)
    imB = {"xdram_in": rA["xdram"], "xnd": rA["xnd"]}
    for k in ("l0_w1T", "l0_w2T", "l0_h1T", "l0_h2T", "l0_w3T",
              "l0_w1s", "l0_w2s", "l0_h1s", "l0_h2s", "l0_w3s"):
        imB[k] = rel[k]
    rB = run_bass_kernel_spmd(ncB, [imB], core_ids=[0]).results[0]
    out_fm = rB["xdram"]
    xd = x.astype(np.float64)
    xn = _np_ln(xd)
    tl = lambda z, wk, sk: (z @ w[wk].astype(np.float64).T) * w[sk].astype(np.float64)
    h = (tl(xn, "L0_ffn_w1_w", "L0_ffn_w1_s") / (1 + np.exp(-tl(xn, "L0_ffn_w1_w", "L0_ffn_w1_s")))) \
        * tl(xn, "L0_ffn_w2_w", "L0_ffn_w2_s")
    ge = lambda z: 0.5 * z * (1 + sp.erf(z / np.sqrt(2)))
    h = ge(tl(h, "L0_ffn_h1_w", "L0_ffn_h1_s"))
    h = ge(tl(h, "L0_ffn_h2_w", "L0_ffn_h2_s"))
    ref = (xd + tl(h, "L0_ffn_w3_w", "L0_ffn_w3_s")).astype(np.float32)
    ref_fm = np.ascontiguousarray(ref.T)
    err = np.abs(out_fm - ref_fm)
    print("l0ffn max err:", err.max(), "rel l2:",
          np.sqrt((err ** 2).mean() / (ref_fm ** 2).mean()))


def _np_msa_moe(x, xn1, w):
    T, Dm, H, DK, E = N, D, 16, 64, 8
    lnw = lambda z, wk, bk: _np_ln(z) * w[wk] + w[bk]
    xa = lnw(xn1, "L1_ffn_ln1_w", "L1_ffn_ln1_b")
    qkv = (xa @ w["L1_ffn_w_qkv"]).reshape(T, 3, H, DK)
    q = qkv[:, 0].transpose(1, 0, 2)
    k = qkv[:, 1].transpose(1, 0, 2)
    v = qkv[:, 2].transpose(1, 0, 2)
    att = _np_softmax(np.einsum("hnd,hmd->hnm", q, k) / np.sqrt(DK), -1)
    y = np.einsum("hnm,hmd->hnd", att, v).transpose(1, 0, 2).reshape(T, Dm)
    msa = y @ w["L1_ffn_w_proj"]
    x1 = xn1 + msa
    x2 = lnw(x1, "L1_ffn_ln2_w", "L1_ffn_ln2_b")
    logits = x2 @ w["L1_ffn_gate_w"] + w["L1_ffn_gate_b"]
    sel = logits.argmax(-1)
    h = np.einsum("td,edh->teh", x2, w["L1_ffn_w1"])
    h = 0.5 * h * (1 + np.tanh(0.7978845608028654 * (h + 0.044715 * h ** 3)))
    y2 = np.einsum("teh,ehd->ted", h, w["L1_ffn_w2"])
    moe = y2[np.arange(T), sel]
    return msa, moe


def _test_l1ffn():
    import ml_dtypes
    w = dict(np.load("/tmp/ref_inputs_0.npz"))
    rel = relayout_weights(w)
    consts = {"c_" + k: v for k, v in host_constants().items()}
    x = np.ascontiguousarray(w["L0_attn_x"][0]).astype(np.float32)
    xn1 = _np_ln(x.astype(np.float64)).astype(np.float32)
    xn1_bf = xn1.astype(ml_dtypes.bfloat16)
    xnd1 = np.zeros((128, DC * N), dtype=ml_dtypes.bfloat16)
    for c in range(DC):
        xnd1[:, c * N:(c + 1) * N] = xn1_bf[:, c * 128:(c + 1) * 128].T
    xfm = np.ascontiguousarray(x.T)
    ncD = build_launch_D(rel)
    imD = {**consts, "xdram_in": xfm, "xnd1": xnd1}
    for k in MSA_KEYS:
        imD[k] = rel[k]
    rD = run_bass_kernel_spmd(ncD, [imD], core_ids=[0]).results[0]
    ncE = build_launch_E(rel)
    imE = {**consts, "xdram_in": rD["xdram"], "xn2d": rD["xn2d"]}
    for k in MOE_KEYS:
        imE[k] = rel[k]
    rE = run_bass_kernel_spmd(ncE, [imE], core_ids=[0]).results[0]
    out = rE["out"]
    wd = {k: v.astype(np.float64) for k, v in w.items()}
    msa, moe = _np_msa_moe(x.astype(np.float64),
                           xn1_bf.astype(np.float64), wd)
    refD = (x.astype(np.float64) + msa).astype(np.float32).T
    errD = np.abs(rD["xdram"] - refD)
    print("launchD rel l2:", np.sqrt((errD ** 2).mean() / (refD ** 2).mean()))
    ref = (x.astype(np.float64) + msa + moe).astype(np.float32)
    err = np.abs(out - ref)
    print("l1ffn max err:", err.max(), "rel l2:",
          np.sqrt((err ** 2).mean() / (ref ** 2).mean()))


def _np_full_ref(x, w):
    import scipy.special as sp
    xd = x.astype(np.float64)
    att0 = _np_l0attn(xd, w) - xd
    xn0 = _np_ln(xd)
    tl = lambda z, wk, sk: (z @ w[wk].T) * w[sk]
    g1 = tl(xn0, "L0_ffn_w1_w", "L0_ffn_w1_s")
    h = (g1 / (1 + np.exp(-g1))) * tl(xn0, "L0_ffn_w2_w", "L0_ffn_w2_s")
    ge = lambda z: 0.5 * z * (1 + sp.erf(z / np.sqrt(2)))
    h = ge(tl(h, "L0_ffn_h1_w", "L0_ffn_h1_s"))
    h = ge(tl(h, "L0_ffn_h2_w", "L0_ffn_h2_s"))
    ffn0 = tl(h, "L0_ffn_w3_w", "L0_ffn_w3_s")
    x1 = xd + att0 + ffn0
    att1 = _np_l1attn(x1, w) - x1
    xn1 = _np_ln(x1)
    msa, moe = _np_msa_moe(x1, xn1, w)
    return (x1 + att1 + xn1 + msa + moe).astype(np.float32)


def _test_fused():
    w = dict(np.load("/tmp/ref_inputs_0.npz"))
    rel = relayout_weights(w)
    consts = {"c_" + k: v for k, v in host_constants().items()}
    x = np.ascontiguousarray(w["L0_attn_x"][0]).astype(np.float32)
    nc = build_fused(rel)
    im = {**consts, "x_in": x}
    for k in ALL_KEYS:
        im[k] = rel[k]
    res = run_bass_kernel_spmd(nc, [im], core_ids=[0])
    out = res.results[0]["out"]
    wd = {k: v.astype(np.float64) for k, v in w.items()}
    ref = _np_full_ref(x, wd)
    err = np.abs(out - ref)
    print("fused max err:", err.max(), "rel l2:",
          np.sqrt((err ** 2).mean() / (ref ** 2).mean()))


def _test_l0attn():
    w = dict(np.load("/tmp/ref_inputs_0.npz"))
    rel = relayout_weights(w)
    x = np.ascontiguousarray(w["L0_attn_x"][0])
    nc = _build_l0attn_test(rel)
    im = {"x_in": x}
    im.update({"c_" + k: v for k, v in host_constants().items()})
    for k in ("l0_WqkvT", "l0_bq_pp", "l0_bk_pp", "l0_bv_pp", "l0_WprojT", "l0_bproj"):
        im[k] = rel[k]
    res = run_bass_kernel_spmd(nc, [im], core_ids=[0])
    out = res.results[0]["out"]
    ref = _np_l0attn(x.astype(np.float64), {k: v.astype(np.float64) for k, v in w.items()}).astype(np.float32)
    err = np.abs(out - ref)
    print("l0attn max err:", err.max(), "rel l2:",
          np.sqrt((err ** 2).mean() / (ref ** 2).mean()))


def _run_idln_test():
    rng = np.random.default_rng(0)
    x = rng.standard_normal((N, D)).astype(np.float32)
    nc = _build_identity_test(with_ln=True)
    im = {"x_in": x}
    im.update({"c_" + k: v for k, v in host_constants().items()})
    res = run_bass_kernel_spmd(nc, [im], core_ids=[0])
    out = res.results[0]["out"]
    print("roundtrip err:", np.abs(out - x).max())
    xn = res.results[0]["xn_out"].astype(np.float32)
    m = x.mean(axis=1, keepdims=True)
    v = x.var(axis=1, keepdims=True)
    ref = (x - m) / np.sqrt(v + 1e-5)
    ref_fm = np.zeros((128, DC * N), dtype=np.float32)
    for c in range(DC):
        ref_fm[:, c * N:(c + 1) * N] = ref[:, c * 128:(c + 1) * 128].T
    err = np.abs(xn - ref_fm)
    print("ln max err:", err.max(), "rel l2:",
          np.sqrt((err ** 2).mean() / (ref_fm ** 2).mean()))


if __name__ == "__main__":
    import sys as _sys
    _arg = _sys.argv[1] if len(_sys.argv) > 1 else ""
    if _arg == "l0attn":
        _test_l0attn()
    elif _arg == "l0ffn":
        _test_l0ffn()
    elif _arg == "l1attn":
        _test_l1attn()
    elif _arg == "l1ffn":
        _test_l1ffn()
    elif _arg == "fused":
        _test_fused()
    else:
        _run_idln_test()


